# version 01
# baseline (speedup 1.0000x reference)
import numpy as np
import ml_dtypes
import concourse.bass as bass
import concourse.mybir as mybir
from concourse.bass_utils import run_bass_kernel_spmd
from contextlib import ExitStack

F32 = mybir.dt.float32
BF16 = mybir.dt.bfloat16
AF = mybir.ActivationFunctionType
ALU = mybir.AluOpType
AX = mybir.AxisListType
BF = ml_dtypes.bfloat16

N = 2048
D = 1024
C = 8
NT = 16
NSEQ = 2
EPS = 1e-5

ENGINES = ("tensor", "vector", "scalar", "gpsimd", "sync")
COMPUTE_SEM_OWNER = {"T": "tensor", "V": "vector", "S": "scalar", "G": "gpsimd"}
DMA_SEMS = ("LW", "LS", "ST")
ALL_SEMS = tuple(COMPUTE_SEM_OWNER) + DMA_SEMS


class Plan:
    def __init__(self):
        self.steps = {e: [] for e in ENGINES}
        self.cnt = {s: 0 for s in ALL_SEMS}

    def emit(self, eng, fn, waits=(), inc=None):
        waits = tuple(w for w in waits if w is not None)
        ev = None
        amt = 0
        if inc is not None:
            amt = 16 if inc in DMA_SEMS else 1
            self.cnt[inc] += amt
            ev = (inc, self.cnt[inc])
        self.steps[eng].append((waits, fn, inc, amt))
        return ev

    def now(self, *sems):
        return tuple((s, self.cnt[s]) for s in sems)

    def realize(self, nc, sems):
        hw = {e: {s: 0 for s in ALL_SEMS} for e in ENGINES}
        with nc.Block() as block:
            for ename in ENGINES:
                def body(eng, ename=ename):
                    marks = hw[ename]
                    for waits, fn, inc, amt in self.steps[ename]:
                        for (s, v) in waits:
                            if COMPUTE_SEM_OWNER.get(s) == ename:
                                continue
                            if marks[s] < v:
                                eng.wait_ge(sems[s], v)
                                marks[s] = v
                        ret = fn(eng)
                        if inc is not None:
                            inst = ret[-1] if isinstance(ret, (list, tuple)) else ret
                            inst.then_inc(sems[inc], amt)
                getattr(block, ename)(body)



def np_ln(x):
    m = x.mean(-1, keepdims=True)
    v = x.var(-1, keepdims=True)
    return (x - m) / np.sqrt(v + EPS)

def np_softmax(x, axis=-1):
    m = x.max(axis=axis, keepdims=True)
    e = np.exp(x - m)
    return e / e.sum(axis=axis, keepdims=True)

def np_l0_attn(x, bvq, bvk, bvv):
    bq, bk, bv = np.sign(bvq), np.sign(bvk), np.sign(bvv)
    Q = np.sign(x * bq); K = np.sign(x * bk); V = x * bv
    s = (Q @ K.T) * (D ** -0.5)
    n = x.shape[0]
    causal = np.tril(np.ones((n, n), bool))
    a = 1.0 / (1.0 + np.exp(-4.0 * s))
    a = np.where(causal, a, 0.0)
    return (a @ V).astype(np.float32)

def np_l0_ffn(x, router_w, w1, w2):
    logits = x @ router_w.T
    top2 = np.argsort(-logits, axis=1, kind="stable")[:, :2]
    vals = np.take_along_axis(logits, top2, axis=1)
    w = np_softmax(vals, axis=-1)
    G = np.zeros_like(logits)
    np.put_along_axis(G, top2, w, axis=1)
    out = np.zeros_like(x)
    for e in range(8):
        h = np.maximum(x @ w1[e], 0.0) ** 2
        out += (h @ w2[e]) * G[:, e:e + 1]
    return out.astype(np.float32)

def np_l1_attn(x, Wq, bq, Wk, bk, Wv, bv, Wo, bo):
    CH = 256
    n = x.shape[0]; nch = n // CH
    sc = 1.0 / np.sqrt(D)
    Q = (x @ Wq + bq).reshape(nch, CH, D)
    K = (x @ Wk + bk).reshape(nch, CH, D)
    V = (x @ Wv + bv).reshape(nch, CH, D)
    out = np.zeros((nch, CH, D), np.float32)
    for i in range(nch):
        Gs = np.zeros((CH, CH), np.float32)
        for m in range(nch):
            A = np_softmax(Q[i] @ K[m].T * sc, axis=-1)
            Gs += A * (2.0 if m == i else 1.0)
        out[i] = Gs @ V[i]
    return (out.reshape(n, D) @ Wo + bo).astype(np.float32)

def np_l1_ffn(x, w_in, w_out, w1, w2, wg, we, gb, eb):
    def swiglu(h):
        a, b = np.split(h, 2, axis=-1)
        return (a / (1.0 + np.exp(-a))) * b
    shared = swiglu(x @ w_in.T) @ w_out.T
    gl = x @ wg.T + gb
    gprob = np_softmax(gl, axis=-1)
    gidx = gl.argmax(axis=1)
    el = x @ we.T + eb
    egid = np.arange(8) // 4
    masked = np.where(gidx[:, None] == egid[None, :], el, -np.inf)
    eprob = np_softmax(masked, axis=-1)
    pexp = eprob * np.take_along_axis(gprob, gidx[:, None], axis=1)
    top2 = np.argsort(-pexp, axis=1, kind="stable")[:, :2]
    vals = np.take_along_axis(pexp, top2, axis=1)
    Wse = np.zeros_like(el)
    np.put_along_axis(Wse, top2, vals, axis=1)
    h = swiglu(x @ w1.T)
    routed = np.zeros_like(shared)
    for e in range(8):
        routed += (h * Wse[:, e:e + 1]) @ w2[e].T
    return (shared + routed).astype(np.float32)

def np_module(x, W, blocks):
    if "l0a" in blocks:
        x = x + np_l0_attn(np_ln(x), W["L0_attn_bv_q"], W["L0_attn_bv_k"], W["L0_attn_bv_v"])
    if "l0f" in blocks:
        x = x + np_l0_ffn(np_ln(x), W["L0_ffn_router_w"], W["L0_ffn_w1"], W["L0_ffn_w2"])
    if "l1a" in blocks:
        x = x + np_l1_attn(np_ln(x), W["L1_attn_Wq"], W["L1_attn_bq"], W["L1_attn_Wk"], W["L1_attn_bk"],
                           W["L1_attn_Wv"], W["L1_attn_bv"], W["L1_attn_Wo"], W["L1_attn_bo"])
    if "l1f" in blocks:
        x = x + np_l1_ffn(np_ln(x), W["L1_ffn_w_in"], W["L1_ffn_w_out"], W["L1_ffn_w1"], W["L1_ffn_w2"],
                          W["L1_ffn_wg"], W["L1_ffn_we"], W["L1_ffn_gb"], W["L1_ffn_eb"])
    return x



def host_constants():
    cst = {}
    cst["id_bf"] = np.eye(128, dtype=BF)
    cst["id_f32"] = np.eye(128, dtype=np.float32)
    cst["ones128"] = np.ones((128, 128), dtype=BF)
    sel = np.zeros((8, 8 * 128), dtype=BF)
    for e in range(8):
        sel[e, e * 128:(e + 1) * 128] = 1
    cst["sel"] = sel
    masks = np.zeros((128, 4 * 512), dtype=BF)
    i = np.arange(128)[:, None]
    j = np.arange(512)[None, :]
    for oi, off in enumerate((0, 128, 256, 384)):
        masks[:, oi * 512:(oi + 1) * 512] = (i + off <= j).astype(BF)
    cst["masks"] = masks
    return cst

def feat_major_vec(v):
    return np.ascontiguousarray(np.asarray(v, np.float32).reshape(C, 128).T)


class Builder:
    def __init__(self, blocks=("l0a", "l0f", "l1a", "l1f"), debug=False):
        self.blocks = blocks
        self.debug = debug
        self.nc = bass.Bass("TRN2", target_bir_lowering=False)
        self.P = Plan()
        self.stack = ExitStack()
        self.dram = {}
        self.sb = {}
        self.ps = []
        self.bank_free = [None] * 8
        self._build()

    def dram_in(self, name, shape, dtype=F32):
        self.dram[name] = self.nc.dram_tensor(name, list(shape), dtype, kind="ExternalInput").ap()
        return self.dram[name]

    def dram_out(self, name, shape, dtype=F32):
        self.dram[name] = self.nc.dram_tensor(name, list(shape), dtype, kind="ExternalOutput").ap()
        return self.dram[name]

    def sbuf(self, name, shape, dtype):
        self.sb[name] = self.stack.enter_context(self.nc.sbuf_tensor(name, list(shape), dtype))
        return self.sb[name]

    def bf(self, lo, w):
        return self.sb["BFA"][:, lo:lo + w]

    def use_bank(self, k, waits=()):
        if self.bank_free[k] is not None:
            waits = tuple(waits) + (self.bank_free[k],)
        return tuple(waits)

    def free_bank(self, k, ev):
        self.bank_free[k] = ev

    def _build(self):
        nc, P = self.nc, self.P
        self.dram_in("xin", [NSEQ * N, D])
        self.dram_out("yout", [NSEQ * N, D])
        self.dram_in("id_bf", [128, 128], BF16)
        self.dram_in("id_f32", [128, 128])
        self.dram_in("ones128", [128, 128], BF16)
        self.dram_in("sel", [8, 8 * 128], BF16)
        self.dram_in("masks", [128, 4 * 512], BF16)
        self.dram_in("bvq", [128, C]); self.dram_in("bvk", [128, C]); self.dram_in("bvv", [128, C])
        self.dram_in("rw_t", [D, 8])
        self.dram_in("l0w1", [8, D, 2048])
        self.dram_in("l0w2", [8, 2048, D])
        for w in ("wq", "wk", "wv", "wo"):
            self.dram_in(w, [D, D])
        for b in ("bq", "bk", "bv", "bo"):
            self.dram_in(b, [128, C])
        self.dram_in("w_in_t", [D, 4096])
        self.dram_in("w_out_t", [2048, D])
        self.dram_in("w1_t", [D, 4096])
        self.dram_in("w2_t", [8, 2048, D])
        self.dram_in("wg_t", [D, 2])
        self.dram_in("we_t", [D, 8])
        self.dram_in("gb_b", [128, 2])
        self.dram_in("eb_b", [128, 8])

        XT = self.sbuf("XT", [128, C * N], F32)
        for nm in ("M", "R", "S1", "S2"):
            self.sbuf(nm, [128, N], F32)
        self.sbuf("IDB", [128, 128], BF16)
        self.sbuf("IDF", [128, 128], F32)
        self.sbuf("ONE", [128, 128], BF16)
        self.sbuf("SELS", [8, 8 * 128], BF16)
        self.sbuf("MSK", [128, 4 * 512], BF16)
        self.sbuf("SGN", [128, 4 * C], F32)
        self.sbuf("BQKV", [128, 4 * C], F32)
        self.sbuf("RWT", [128, C * 8], BF16)
        self.sbuf("RWLO", [128, C * 8], BF16)
        self.sbuf("WGLO", [128, C * 2], BF16)
        self.sbuf("WELO", [128, C * 8], BF16)
        self.sbuf("WGT", [128, C * 2], BF16)
        self.sbuf("WET", [128, C * 8], BF16)
        self.sbuf("GBB", [128, 2], F32)
        self.sbuf("EBB", [128, 8], F32)
        self.sbuf("GROUT", [128, 128], F32)
        self.sbuf("G0B", [128, 128], F32)
        self.sbuf("GTR", [8, 2048], BF16)
        self.sbuf("RA", [128, 128], F32)
        self.sbuf("RB", [128, 128], F32)
        self.sbuf("RC", [128, 128], F32)
        self.sbuf("RV1", [128, 16], F32)
        self.sbuf("RV2", [128, 16], F32)
        self.sbuf("RV3", [128, 16], F32)
        self.BFW = 49152
        self.sbuf("BFA", [128, self.BFW], BF16)

        for i in range(8):
            self.ps.append(self.stack.enter_context(nc.psum_tensor(f"ps{i}", [128, 512], F32)))

        self.sems = {s: self.stack.enter_context(nc.semaphore(s)) for s in ALL_SEMS}

        def load(dst, src, q="sync"):
            return P.emit(q, lambda e, d=dst, s=src: e.dma_start(out=d, in_=s),
                          inc="LS" if q == "sync" else "LW")
        load(self.sb["IDB"][:], self.dram["id_bf"])
        load(self.sb["IDF"][:], self.dram["id_f32"])
        load(self.sb["ONE"][:], self.dram["ones128"])
        load(self.sb["SELS"][:], self.dram["sel"])
        load(self.sb["MSK"][:], self.dram["masks"])
        load(self.sb["BQKV"][:, 0:C], self.dram["bq"])
        load(self.sb["BQKV"][:, C:2 * C], self.dram["bk"])
        load(self.sb["BQKV"][:, 2 * C:3 * C], self.dram["bv"])
        load(self.sb["BQKV"][:, 3 * C:4 * C], self.dram["bo"])
        load(self.sb["GBB"][:], self.dram["gb_b"])
        load(self.sb["EBB"][:], self.dram["eb_b"])
        load(self.sb["RA"][:, 0:C], self.dram["bvq"])
        load(self.sb["RA"][:, C:2 * C], self.dram["bvk"])
        load(self.sb["RA"][:, 2 * C:3 * C], self.dram["bvv"])
        load(self.sb["RWT"][:].rearrange("p (c k) -> p c k", k=8),
             self.dram["rw_t"].rearrange("(c p) k -> p c k", p=128), q="gpsimd")
        load(self.sb["WGT"][:].rearrange("p (c k) -> p c k", k=2),
             self.dram["wg_t"].rearrange("(c p) k -> p c k", p=128), q="gpsimd")
        load(self.sb["WET"][:].rearrange("p (c k) -> p c k", k=8),
             self.dram["we_t"].rearrange("(c p) k -> p c k", p=128), q="gpsimd")
        CST = self.sbuf("CST", [128, 8], F32)
        for i, v in enumerate((EPS, -0.5, 0.125)):
            P.emit("vector", lambda e, v=v, i=i: e.memset(CST[:, i:i + 1], v), inc="V")
            nc.const_aps.register(F32, v, CST[:, i:i + 1])
        load(self.sb["GROUT"][:, 0:64].rearrange("p (c k) -> p c k", k=8),
             self.dram["rw_t"].rearrange("(c p) k -> p c k", p=128))
        load(self.sb["G0B"][:, 0:16].rearrange("p (c k) -> p c k", k=2),
             self.dram["wg_t"].rearrange("(c p) k -> p c k", p=128))
        load(self.sb["RB"][:, 0:64].rearrange("p (c k) -> p c k", k=8),
             self.dram["we_t"].rearrange("(c p) k -> p c k", p=128))
        ev_wf = P.now("LS", "LW")
        P.emit("vector", lambda e: e.tensor_tensor(
            self.sb["RWLO"][:], self.sb["GROUT"][:, 0:64], self.sb["RWT"][:], ALU.subtract),
            waits=ev_wf)
        P.emit("vector", lambda e: e.tensor_tensor(
            self.sb["WGLO"][:], self.sb["G0B"][:, 0:16], self.sb["WGT"][:], ALU.subtract))
        ev_wlo = P.emit("vector", lambda e: e.tensor_tensor(
            self.sb["WELO"][:], self.sb["RB"][:, 0:64], self.sb["WET"][:], ALU.subtract), inc="V")
        ev_const = P.now("LS", "LW") + (ev_wlo,)
        ev_sgn = P.emit("scalar", lambda e: e.activation(
            self.sb["SGN"][:, 0:3 * C], self.sb["RA"][:, 0:3 * C], AF.Sign),
            waits=ev_const, inc="S")
        self.ev_setup = (P.emit("vector", lambda e: e.tensor_tensor(
            self.sb["SGN"][:, 3 * C:4 * C], self.sb["SGN"][:, 0:C],
            self.sb["SGN"][:, C:2 * C], ALU.mult), waits=(ev_sgn,), inc="V"),) + ev_const

        for seq in range(NSEQ):
            self.build_seq(seq)

        final = P.now("T", "V", "S", "G", "LW", "LS", "ST")
        def _clear_all(e):
            ret = None
            for s in ALL_SEMS:
                ret = e.sem_clear(self.sems[s])
            return ret
        P.emit("sync", lambda e: _clear_all(e), waits=final)
        P.realize(nc, self.sems)
        self.stack.close()

    def build_seq(self, seq):
        self.load_input(seq)
        if "l0a" in self.blocks:
            self.ln_stats()
            self.block_l0_attn(seq)
        if "l0f" in self.blocks:
            self.ln_stats()
            self.block_l0_ffn(seq)
        if "l1a" in self.blocks:
            self.ln_stats()
            self.block_l1_attn(seq)
        if "l1f" in self.blocks:
            self.ln_stats()
            self.block_l1_ffn(seq)
        self.store_output(seq)

    def load_input(self, seq):
        P = self.P
        XT, S1, S2, IDF = self.sb["XT"], self.sb["S1"], self.sb["S2"], self.sb["IDF"]
        xin = self.dram["xin"]
        stage = [S1, S2]
        ev_done = [None, None]
        w0 = P.now("T", "V", "S", "G") + self.ev_setup + (("ST", P.cnt["ST"]),)
        for t in range(NT):
            buf = stage[t % 2]
            w = w0 if t < 2 else (ev_done[t % 2],)
            ev_ld = P.emit("sync", lambda e, b=buf, t=t: e.dma_start(
                out=b[:, 0:1024], in_=xin[seq * N + t * 128: seq * N + (t + 1) * 128, :]),
                waits=w, inc="LS")
            for c in range(C):
                bank = (t * C + c) % 2
                tw = self.use_bank(bank, (ev_ld,))
                ev_tp = P.emit("tensor", lambda e, b=buf, c=c, k=bank: e.transpose(
                    self.ps[k][:, 0:128], b[:, c * 128:(c + 1) * 128], IDF[:]),
                    waits=tw, inc="T")
                ev_ev = P.emit("vector", lambda e, c=c, t=t, k=bank: e.tensor_copy(
                    XT[:, c * N + t * 128: c * N + (t + 1) * 128], self.ps[k][:, 0:128]),
                    waits=(ev_tp,), inc="V")
                self.free_bank(bank, ev_ev)
            ev_done[t % 2] = ev_ev

    def store_output(self, seq):
        P = self.P
        XT, S1, S2, IDF = self.sb["XT"], self.sb["S1"], self.sb["S2"], self.sb["IDF"]
        yout = self.dram["yout"]
        stage = [S1, S2]
        ev_store = [None, None]
        w0 = P.now("T", "V", "S", "G")
        for t in range(NT):
            buf = stage[t % 2]
            for c in range(C):
                bank = (t * C + c) % 2
                w = self.use_bank(bank, w0 if (t == 0 and c == 0) else ())
                if c == 0 and ev_store[t % 2] is not None:
                    w = w + (ev_store[t % 2],)
                ev_tp = P.emit("tensor", lambda e, c=c, t=t, k=bank: e.transpose(
                    self.ps[k][:, 0:128], XT[:, c * N + t * 128: c * N + (t + 1) * 128], IDF[:]),
                    waits=w, inc="T")
                ev_ev = P.emit("vector", lambda e, c=c, k=bank, b=buf: e.tensor_copy(
                    b[:, c * 128:(c + 1) * 128], self.ps[k][:, 0:128]),
                    waits=(ev_tp,), inc="V")
                self.free_bank(bank, ev_ev)
            ev_store[t % 2] = P.emit("sync", lambda e, b=buf, t=t: e.dma_start(
                out=yout[seq * N + t * 128: seq * N + (t + 1) * 128, :], in_=b[:, 0:1024]),
                waits=(ev_ev,), inc="ST")

    def ln_stats(self):
        P = self.P
        XT, M, R, S1, ONE = (self.sb[k] for k in ("XT", "M", "R", "S1", "ONE"))
        off = self.BFW - 6 * N
        XH = [self.bf(off + i * N, N) for i in range(2)]
        XL = [self.bf(off + (2 + i) * N, N) for i in range(2)]
        SQ = [self.bf(off + (4 + i) * N, N) for i in range(2)]
        w0 = P.now("T", "S", "G")
        ev_sq = [None, None]
        last_mm = None
        for c in range(C):
            b = c % 2
            w = w0 if c < 2 else ()
            if ev_sq[b] is not None:
                w = w + (ev_sq[b],)
            if last_mm is not None and c >= 2:
                w = w + (last_mm,)
            P.emit("vector", lambda e, c=c, b=b: e.tensor_copy(
                XH[b], XT[:, c * N:(c + 1) * N]), waits=w)
            P.emit("vector", lambda e: e.drain())
            ev_c = P.emit("vector", lambda e, c=c, b=b: e.tensor_tensor(
                XL[b], XT[:, c * N:(c + 1) * N], XH[b], ALU.subtract), inc="V")
            ev_q = P.emit("scalar", lambda e, b=b: e.activation(SQ[b], XH[b], AF.Square),
                          waits=(ev_c,), inc="S")
            ev_sq[b] = ev_q
            for g in range(4):
                bw = self.use_bank(g, (ev_c,)) if c == 0 else (ev_c,)
                P.emit("tensor", lambda e, b=b, g=g, c=c: e.matmul(
                    self.ps[g][:], ONE[:], XH[b][:, g * 512:(g + 1) * 512],
                    start=(c == 0), stop=False), waits=bw)
                P.emit("tensor", lambda e, b=b, g=g, c=c: e.matmul(
                    self.ps[g][:], ONE[:], XL[b][:, g * 512:(g + 1) * 512],
                    start=False, stop=(c == C - 1)))
                bw2 = self.use_bank(4 + g, (ev_q,)) if c == 0 else (ev_q,)
                evm = P.emit("tensor", lambda e, b=b, g=g, c=c: e.matmul(
                    self.ps[4 + g][:], ONE[:], SQ[b][:, g * 512:(g + 1) * 512],
                    start=(c == 0), stop=(c == C - 1)), waits=bw2,
                    inc="T" if g == 3 else None)
                if g == 3:
                    last_mm = ("T", P.cnt["T"])
        ev_mm = last_mm
        ev_m = []
        for g in range(4):
            ev_m.append(P.emit("vector", lambda e, g=g: e.tensor_scalar(
                M[:, g * 512:(g + 1) * 512], self.ps[g][:], 1.0 / D, None, ALU.mult),
                waits=(ev_mm,), inc="V"))
        P.emit("vector", lambda e: e.drain())
        ev_m2 = P.emit("vector", lambda e: e.tensor_tensor(S1[:], M[:], M[:], ALU.mult), inc="V")
        P.emit("vector", lambda e: e.drain())
        ev_var = []
        for g in range(4):
            ev_var.append(P.emit("vector", lambda e, g=g: e.scalar_tensor_tensor(
                S1[:, g * 512:(g + 1) * 512], self.ps[4 + g][:], 1.0 / D,
                S1[:, g * 512:(g + 1) * 512], ALU.mult, ALU.subtract), inc="V"))
        for g in range(4):
            self.free_bank(g, ev_m[g])
            self.free_bank(4 + g, ev_var[g])
        ev_ln = P.emit("scalar", lambda e: e.activation(S1[:], S1[:], AF.Ln, bias=EPS),
                       waits=tuple(ev_var), inc="S")
        self.ev_lnr = P.emit("scalar", lambda e: e.activation(R[:], S1[:], AF.Exp, scale=-0.5),
                             waits=(ev_ln,), inc="S")
        self.ev_lnm = ev_m[-1]

    def block_l0_attn(self, seq):
        P = self.P
        XT, M, R, S1, S2 = (self.sb[k] for k in ("XT", "M", "R", "S1", "S2"))
        IDB, MSK, SGN = self.sb["IDB"], self.sb["MSK"], self.sb["SGN"]
        oZS, oV, oZQ, oAST, oSG = 0, 16384, 32768, 36864, 45056
        oVSC = oSG + 1024

        ev_prev_zs = None
        ev_prev_tp = None
        ev_vt = None
        for c in range(C):
            w = (self.ev_lnr, self.ev_lnm)
            if ev_prev_zs is not None:
                w = w + (ev_prev_zs,)
            P.emit("vector", lambda e, c=c: e.tensor_tensor(
                S1[:], XT[:, c * N:(c + 1) * N], M[:], ALU.subtract), waits=w)
            ev_rs = P.emit("vector", lambda e, c=c: e.tensor_scalar(
                S2[:], R[:], SGN[:, 2 * C + c:2 * C + c + 1], None, ALU.mult), inc="V")
            P.emit("vector", lambda e: e.drain())
            wv = (ev_prev_tp,) if ev_prev_tp is not None else ()
            ev_vsc = P.emit("vector", lambda e: e.tensor_tensor(
                self.bf(oVSC, 2048), S1[:], S2[:], ALU.mult), waits=wv, inc="V")
            ev_prev_zs = P.emit("scalar", lambda e, c=c: e.activation(
                self.bf(oZS + c * N, N), S1[:], AF.Sign,
                scale=SGN[:, 3 * C + c:3 * C + c + 1]), waits=(ev_rs,), inc="S")
            for t in range(NT):
                bank = (c * NT + t) % 2
                tw = self.use_bank(bank, (ev_vsc,))
                ev_tp = P.emit("tensor", lambda e, t=t, k=bank: e.transpose(
                    self.ps[k][:, 0:64].bitcast(BF16),
                    self.bf(oVSC + t * 128, 128), IDB[:]), waits=tw, inc="T")
                ev_vt = P.emit("vector", lambda e, t=t, c=c, k=bank: e.tensor_copy(
                    self.bf(oV + t * 1024 + c * 128, 128),
                    self.ps[k][:, 0:64].bitcast(BF16)), waits=(ev_tp,), inc="V")
                self.free_bank(bank, ev_vt)
            ev_prev_tp = ("T", P.cnt["T"])
        ev_zs_all = ("S", P.cnt["S"])
        ev_vt_all = ("V", P.cnt["V"])

        for qb in range(4):
            q0 = qb * 512
            nkt = qb * 4 + 4
            wT = P.now("T")
            ev_zq = None
            ev_sub_prev = None
            for c in range(C):
                w = wT + (self.ev_lnr, self.ev_lnm)
                if ev_zq is not None:
                    w = w + (ev_zq,)
                ev_sub = P.emit("vector", lambda e, c=c, q0=q0: e.tensor_tensor(
                    S1[:, 0:512], XT[:, c * N + q0: c * N + q0 + 512],
                    M[:, q0:q0 + 512], ALU.subtract), waits=w, inc="V")
                ev_zq = P.emit("scalar", lambda e, c=c: e.activation(
                    self.bf(oZQ + c * 512, 512), S1[:, 0:512], AF.Sign),
                    waits=(ev_sub,), inc="S")
            ev_mask = [None, None]
            for kt in range(nkt):
                k0 = kt * 128
                bank = kt % 2
                w = self.use_bank(bank, (ev_zs_all, ev_zq))
                ev_mm = None
                for c in range(C):
                    ev_mm = P.emit("tensor", lambda e, c=c, kt=kt, k=bank: e.matmul(
                        self.ps[k][:],
                        self.bf(oZS + c * N + kt * 128, 128),
                        self.bf(oZQ + c * 512, 512),
                        start=(c == 0), stop=(c == C - 1)),
                        waits=(w if c == 0 else ()), inc="T" if c == C - 1 else None)
                off = k0 - q0
                if off < 0:
                    ev_a = P.emit("scalar", lambda e, kt=kt, k=bank: e.activation(
                        self.bf(oAST + kt * 512, 512), self.ps[k][:], AF.Sigmoid,
                        scale=0.125), waits=(ev_mm,), inc="S")
                else:
                    oi = off // 128
                    sb_ = kt % 2
                    ws = (ev_mm,)
                    if ev_mask[sb_] is not None:
                        ws = ws + (ev_mask[sb_],)
                    ev_sg = P.emit("scalar", lambda e, k=bank, sb_=sb_: e.activation(
                        self.bf(oSG + sb_ * 512, 512), self.ps[k][:], AF.Sigmoid, scale=0.125),
                        waits=ws, inc="S")
                    ev_a = P.emit("vector", lambda e, kt=kt, oi=oi, sb_=sb_: e.tensor_tensor(
                        self.bf(oAST + kt * 512, 512), self.bf(oSG + sb_ * 512, 512),
                        MSK[:, oi * 512:(oi + 1) * 512], ALU.mult),
                        waits=(ev_sg,), inc="V")
                    ev_mask[sb_] = ev_a
                self.free_bank(bank, ev_a)
            ev_attn_all = P.now("S", "V")
            for eg in range(2):
                for e4 in range(4):
                    ec = eg * 4 + e4
                    bank = 2 + e4
                    w = self.use_bank(bank, ev_attn_all + (ev_vt_all,))
                    ev_mm = None
                    for kt in range(nkt):
                        ev_mm = P.emit("tensor", lambda e, kt=kt, ec=ec, k=bank, nkt=nkt: e.matmul(
                            self.ps[k][:],
                            self.bf(oV + kt * 1024 + ec * 128, 128),
                            self.bf(oAST + kt * 512, 512),
                            start=(kt == 0), stop=(kt == nkt - 1)),
                            waits=(w if kt == 0 else ()), inc="T" if kt == nkt - 1 else None)
                    ev_add = P.emit("vector", lambda e, ec=ec, k=bank, q0=q0: e.tensor_tensor(
                        XT[:, ec * N + q0: ec * N + q0 + 512],
                        XT[:, ec * N + q0: ec * N + q0 + 512], self.ps[k][:], ALU.add),
                        waits=(ev_mm,), inc="V")
                    self.free_bank(bank, ev_add)

    def xn_snapshot(self, oXN, routers=()):
        P = self.P
        XT, M, R, S1, S2 = (self.sb[k] for k in ("XT", "M", "R", "S1", "S2"))
        IDF = self.sb["IDF"]
        S2b = S2[:].bitcast(BF16)
        ev = None
        ev_rt = None
        rbw = {}
        for (whi, wlo, k, banks4, dst) in routers:
            for b in banks4:
                rbw[b] = self.use_bank(b, ())
        for c in range(C):
            w = (self.ev_lnr, self.ev_lnm)
            if routers and c >= 1:
                w = w + (ev_rt,)
            P.emit("vector", lambda e, c=c: e.tensor_tensor(
                S1[:], XT[:, c * N:(c + 1) * N], M[:], ALU.subtract), waits=w)
            P.emit("vector", lambda e: e.drain())
            P.emit("vector", lambda e, c=c: e.tensor_tensor(S1[:], S1[:], R[:], ALU.mult))
            P.emit("vector", lambda e: e.drain())
            ev = P.emit("vector", lambda e, c=c, oXN=oXN: e.tensor_copy(
                self.bf(oXN + c * N, N), S1[:]), inc="V")
            if routers:
                P.emit("vector", lambda e: e.drain())
                ev_lo = P.emit("vector", lambda e, c=c, oXN=oXN: e.tensor_tensor(
                    S2b[:, 0:N], S1[:], self.bf(oXN + c * N, N), ALU.subtract), inc="V")
                for (whi, wlo, k, banks4, dst) in routers:
                    for nb in range(4):
                        bank = banks4[nb]
                        terms = ((whi, True), (whi, False), (wlo, True), (wlo, False))
                        for ti, (wp, is_hi) in enumerate(terms):
                            first = (c == 0 and ti == 0)
                            last = (c == C - 1 and ti == len(terms) - 1)
                            wmm = (rbw[bank] if first else ()) + ((ev,) if is_hi else (ev_lo,))
                            def mk(wp=wp, is_hi=is_hi, c=c, nb=nb, bank=bank, k=k, first=first, last=last, oXN=oXN):
                                def f(e):
                                    rhs = (self.bf(oXN + c * N + nb * 512, 512) if is_hi
                                           else S2b[:, nb * 512:(nb + 1) * 512])
                                    return e.matmul(self.ps[bank][0:k, :], wp[:, c * k:(c + 1) * k],
                                                    rhs, start=first, stop=last)
                                return f
                            ev_rt = P.emit("tensor", mk(),
                                           waits=wmm, inc="T" if (nb == 3 and ti == 3) else None)
        for (whi, wlo, k, banks4, dst) in routers:
            ev_ev = None
            for nb in range(4):
                ev_ev = P.emit("vector", lambda e, nb=nb, k=k, bank=banks4[nb]: e.tensor_copy(
                    S1[0:k, nb * 512:(nb + 1) * 512], self.ps[bank][0:k, :]),
                    waits=(ev_rt,), inc="V")
                self.free_bank(banks4[nb], ev_ev)
            for t in range(NT):
                tb = (t % 2)
                tw = self.use_bank(tb, (ev_ev,))
                ev_tp = P.emit("tensor", lambda e, t=t, k=k, tb=tb: e.transpose(
                    self.ps[tb][:, 0:k], S1[0:k, t * 128:(t + 1) * 128], IDF[0:k, 0:k]),
                    waits=tw, inc="T")
                ev_rt = P.emit("vector", lambda e, t=t, k=k, tb=tb, dst=dst: e.tensor_copy(
                    dst[:, t * k:(t + 1) * k], self.ps[tb][:, 0:k]),
                    waits=(ev_tp,), inc="V")
                self.free_bank(tb, ev_rt)
        return ev, ev_rt

    def top2_gates(self, ev_logits):
        P = self.P
        NEG = -1e30
        GROUT, G0B = self.sb["GROUT"], self.sb["G0B"]
        RA, RB, RC = self.sb["RA"], self.sb["RB"], self.sb["RC"]
        RV1, RV2, RV3 = self.sb["RV1"], self.sb["RV2"], self.sb["RV3"]
        def g3(ap):
            return ap[:].rearrange("p (t k) -> p t k", k=8)
        def b3(ap):
            return ap[:].rearrange("p (t o) -> p t o", o=1).broadcast_to([128, 16, 8])
        P.emit("vector", lambda e: e.tensor_reduce(RV1[:], g3(GROUT), axis=AX.X, op=ALU.max),
               waits=(ev_logits,))
        P.emit("vector", lambda e: e.drain())
        P.emit("vector", lambda e: e.tensor_tensor(g3(RA), g3(GROUT), b3(RV1), ALU.is_ge))
        P.emit("vector", lambda e: e.drain())
        P.emit("vector", lambda e: e.scalar_tensor_tensor(g3(RB), g3(RA), NEG, g3(GROUT),
                                                          ALU.mult, ALU.add))
        P.emit("vector", lambda e: e.drain())
        P.emit("vector", lambda e: e.tensor_reduce(RV2[:], g3(RB), axis=AX.X, op=ALU.max))
        P.emit("vector", lambda e: e.drain())
        P.emit("vector", lambda e: e.tensor_tensor(g3(RC), g3(RB), b3(RV2), ALU.is_ge))
        ev_sm = P.emit("vector", lambda e: e.tensor_tensor(RV3[:], RV1[:], RV2[:], ALU.subtract),
                       inc="V")
        ev_w1 = P.emit("scalar", lambda e: e.activation(RV2[:], RV3[:], AF.Sigmoid),
                       waits=(ev_sm,), inc="S")
        P.emit("vector", lambda e: e.tensor_tensor(g3(RB), g3(RA), b3(RV2), ALU.mult),
               waits=(ev_w1,))
        P.emit("vector", lambda e: e.drain())
        P.emit("vector", lambda e: e.tensor_tensor(g3(RA), g3(RC), b3(RV2), ALU.mult))
        P.emit("vector", lambda e: e.drain())
        P.emit("vector", lambda e: e.tensor_tensor(RC[:], RC[:], RA[:], ALU.subtract))
        P.emit("vector", lambda e: e.drain())
        return P.emit("vector", lambda e: e.tensor_tensor(G0B[:], RB[:], RC[:], ALU.add), inc="V")

    def gates_transpose(self, ev_gates):
        P = self.P
        G0B, GTR, IDF = self.sb["G0B"], self.sb["GTR"], self.sb["IDF"]
        ev = None
        for t in range(NT):
            bank = 6 + t % 2
            w = self.use_bank(bank, (ev_gates,))
            ev_tp = P.emit("tensor", lambda e, t=t, k=bank: e.transpose(
                self.ps[k][0:8, 0:128], G0B[:, t * 8:(t + 1) * 8], IDF[:]),
                waits=w, inc="T")
            ev = P.emit("vector", lambda e, t=t, k=bank: e.tensor_copy(
                GTR[0:8, t * 128:(t + 1) * 128], self.ps[k][0:8, 0:128]),
                waits=(ev_tp,), inc="V")
            self.free_bank(bank, ev)
        return ev

    def block_l0_ffn(self, seq):
        P = self.P
        XT, M, R, S1, S2 = (self.sb[k] for k in ("XT", "M", "R", "S1", "S2"))
        RWT, GROUT, GTR, SELS = self.sb["RWT"], self.sb["GROUT"], self.sb["GTR"], self.sb["SELS"]
        oXN, oW1, oHG, oW2 = 0, 16384, 32768, 40960
        S1b = S1[:].bitcast(BF16)
        S2b = S2[:].bitcast(BF16)
        l0w1, l0w2 = self.dram["l0w1"], self.dram["l0w2"]

        ev_xn, ev_rt = self.xn_snapshot(oXN, routers=(
            (RWT[:], self.sb["RWLO"][:], 8, (4, 5, 6, 7), GROUT[:]),))
        ev_lg = ev_rt
        ev_gates = self.top2_gates(ev_lg)
        ev_gtr = self.gates_transpose(ev_gates)

        ev_w1_free = None
        ev_gb_free = None
        ev_w2buf = [None, None]
        ev_hg_rd = None
        for ex in range(8):
            wv = (ev_w1_free,) if ev_w1_free is not None else P.now("T", "V", "S")
            ev_w1l = None
            for c in range(C):
                ev_w1l = P.emit("gpsimd", lambda e, ex=ex, c=c: e.dma_start(
                    out=self.bf(oW1 + c * 2048, 2048),
                    in_=l0w1[ex, c * 128:(c + 1) * 128, :]), waits=wv if c == 0 else (),
                    inc="LW")
            wg = (ev_gtr,) if ex == 0 else ()
            if ev_gb_free is not None:
                wg = wg + (ev_gb_free,)
            ev_gb = None
            for g in range(4):
                bank = 6 + g % 2
                bwg = self.use_bank(bank, wg if g < 2 else ())
                ev_bc = P.emit("tensor", lambda e, ex=ex, g=g, k=bank: e.matmul(
                    self.ps[k][:], SELS[:, ex * 128:(ex + 1) * 128],
                    GTR[:, g * 512:(g + 1) * 512], start=True, stop=True),
                    waits=bwg, inc="T")
                ev_gb = P.emit("vector", lambda e, g=g, k=bank: e.tensor_copy(
                    S2b[:, g * 512:(g + 1) * 512], self.ps[k][:]),
                    waits=(ev_bc,), inc="V")
                self.free_bank(bank, ev_gb)

            for nb in range(4):
                n0 = nb * 512
                ev_hg = None
                for hc in range(16):
                    bank = 6 + hc % 2
                    w = self.use_bank(bank, ((("LW", P.cnt["LW"]),) if hc == 0 and nb == 0 else ()))
                    ev_h = None
                    for c in range(C):
                        ev_h = P.emit("tensor", lambda e, c=c, hc=hc, k=bank, n0=n0: e.matmul(
                            self.ps[k][:],
                            self.bf(oW1 + c * 2048 + hc * 128, 128),
                            self.bf(oXN + c * N + n0, 512),
                            start=(c == 0), stop=(c == C - 1)),
                            waits=(w if c == 0 else ()), inc="T" if c == C - 1 else None)
                    sl = hc % 2
                    ev_rl = P.emit("scalar", lambda e, k=bank, sl=sl: e.activation(
                        S1b[:, sl * 512:(sl + 1) * 512], self.ps[k][:], AF.Relu),
                        waits=(ev_h,), inc="S")
                    self.free_bank(bank, ev_rl)
                    ev_sq = P.emit("vector", lambda e, sl=sl: e.tensor_tensor(
                        S1b[:, (2 + sl) * 512:(3 + sl) * 512],
                        S1b[:, sl * 512:(sl + 1) * 512],
                        S1b[:, sl * 512:(sl + 1) * 512], ALU.mult),
                        waits=(ev_rl,), inc="V")
                    whg = (ev_sq, ev_gb)
                    if ev_hg_rd is not None:
                        whg = whg + (ev_hg_rd,)
                    ev_hg = P.emit("gpsimd", lambda e, hc=hc, sl=sl, n0=n0: e.tensor_tensor(
                        self.bf(oHG + hc * 512, 512),
                        S1b[:, (2 + sl) * 512:(3 + sl) * 512],
                        S2b[:, n0:n0 + 512],
                        ALU.mult), waits=whg, inc="G")
                ev_y = None
                for g in range(4):
                    buf = g % 2
                    wl = (ev_w2buf[buf],) if ev_w2buf[buf] is not None else P.now("V", "S")
                    ev_l = P.emit("gpsimd", lambda e, ex=ex, g=g, buf=buf: e.dma_start(
                        out=self.bf(oW2 + buf * 4096, 4096).rearrange("p (s d) -> p s d", d=1024),
                        in_=l0w2[ex, g * 512:(g + 1) * 512, :].rearrange("(s p) d -> p s d", p=128)),
                        waits=wl, inc="LW")
                    for si in range(4):
                        hc = g * 4 + si
                        for dc in range(C):
                            wy = ()
                            if si == 0 and dc == 0:
                                wy = (ev_l, ("G", P.cnt["G"]))
                                if g == 0:
                                    wy = self.use_bank(dc, wy)
                            elif si == 0 and g == 0:
                                wy = self.use_bank(dc, ())
                            ev_y = P.emit("tensor", lambda e, buf=buf, si=si, dc=dc, hc=hc: e.matmul(
                                self.ps[dc][:],
                                self.bf(oW2 + buf * 4096 + si * 1024 + dc * 128, 128),
                                self.bf(oHG + hc * 512, 512),
                                start=(hc == 0), stop=(hc == 15)),
                                waits=wy, inc="T" if (si == 3 and dc == C - 1) else None)
                    ev_w2buf[buf] = ev_y
                ev_hg_rd = ev_y
                for dc in range(C):
                    ev_add = P.emit("vector", lambda e, dc=dc, n0=n0: e.tensor_tensor(
                        XT[:, dc * N + n0: dc * N + n0 + 512],
                        XT[:, dc * N + n0: dc * N + n0 + 512], self.ps[dc][:], ALU.add),
                        waits=(ev_y,), inc="V")
                    self.free_bank(dc, ev_add)
            ev_w1_free = ("T", P.cnt["T"])
            ev_gb_free = ("G", P.cnt["G"])

    def xn_half(self, oXN, h):
        P = self.P
        XT, M, R, S2 = (self.sb[k] for k in ("XT", "M", "R", "S2"))
        lo = h * 1024
        ev = None
        for c in range(C):
            w = (self.ev_lnr, self.ev_lnm) + (P.now("T", "G") if c == 0 else ())
            P.emit("vector", lambda e, c=c, lo=lo: e.tensor_tensor(
                S2[:, 0:1024], XT[:, c * N + lo: c * N + lo + 1024],
                M[:, lo:lo + 1024], ALU.subtract), waits=w)
            P.emit("vector", lambda e: e.drain())
            ev = P.emit("vector", lambda e, c=c, oXN=oXN, lo=lo: e.tensor_tensor(
                self.bf(oXN + c * 1024, 1024), S2[:, 0:1024], R[:, lo:lo + 1024],
                ALU.mult), inc="V")
        return ev

    def load_wslot(self, oW, wname):
        P = self.P
        wt = self.dram[wname]
        wv = P.now("T", "V", "S", "G")
        ev = None
        for c in range(C):
            ev = P.emit("gpsimd", lambda e, c=c, oW=oW, wt=wt: e.dma_start(
                out=self.bf(oW + c * 1024, 1024), in_=wt[c * 128:(c + 1) * 128, :]),
                waits=wv if c == 0 else (), inc="LW")
        return ev

    def proj_half(self, oW, oXN, dst, h, bias_col, ev_w, ev_xn, to_token_major=False):
        P = self.P
        BQKV, IDB = self.sb["BQKV"], self.sb["IDB"]
        S1b = self.sb["S1"][:].bitcast(BF16)
        ev_out = None
        for eo in range(C):
            for blk in range(2):
                bank = (eo * 2 + blk) % 4
                w = self.use_bank(bank, (ev_w, ev_xn))
                ev_mm = None
                for c in range(C):
                    ev_mm = P.emit("tensor", lambda e, c=c, eo=eo, blk=blk, k=bank, oW=oW, oXN=oXN: e.matmul(
                        self.ps[k][:],
                        self.bf(oW + c * 1024 + eo * 128, 128),
                        self.bf(oXN + c * 1024 + blk * 512, 512),
                        start=(c == 0), stop=(c == C - 1)),
                        waits=(w if c == 0 else ()), inc="T" if c == C - 1 else None)
                if not to_token_major:
                    ev_out = P.emit("scalar", lambda e, eo=eo, blk=blk, k=bank, dst=dst, bias_col=bias_col: e.activation(
                        self.bf(dst + eo * 1024 + blk * 512, 512), self.ps[k][:], AF.Identity,
                        bias=BQKV[:, bias_col * C + eo: bias_col * C + eo + 1]),
                        waits=(ev_mm,), inc="S")
                    self.free_bank(bank, ev_out)
                else:
                    ev_st = P.emit("scalar", lambda e, k=bank, bias_col=bias_col, eo=eo: e.activation(
                        S1b[:, 0:512], self.ps[k][:], AF.Identity,
                        bias=BQKV[:, bias_col * C + eo: bias_col * C + eo + 1]),
                        waits=(ev_mm,), inc="S")
                    self.free_bank(bank, ev_st)
                    for tt in range(4):
                        tb = 4 + (tt % 2)
                        tw = self.use_bank(tb, (ev_st,))
                        ev_tp = P.emit("tensor", lambda e, tt=tt, k=tb: e.transpose(
                            self.ps[k][:, 0:64].bitcast(BF16),
                            S1b[:, tt * 128:(tt + 1) * 128], IDB[:]), waits=tw, inc="T")
                        lt = blk * 4 + tt
                        ev_out = P.emit("vector", lambda e, lt=lt, eo=eo, k=tb, dst=dst: e.tensor_copy(
                            self.bf(dst + lt * 1024 + eo * 128, 128),
                            self.ps[k][:, 0:64].bitcast(BF16)), waits=(ev_tp,), inc="V")
                        self.free_bank(tb, ev_out)
        return ev_out

    def block_l1_attn(self, seq):
        P = self.P
        XT, M, R, S1, S2 = (self.sb[k] for k in ("XT", "M", "R", "S1", "S2"))
        BQKV, IDB, RV1, RV2, RV3 = (self.sb[k] for k in ("BQKV", "IDB", "RV1", "RV2", "RV3"))
        oK, oQ, oV, oW, oXN3 = 0, 16384, 24576, 32768, 40960
        S1b = S1[:].bitcast(BF16)
        SCALE = 1.0 / 32.0

        for kh in range(2):
            ev_xn = self.xn_half(oXN3, kh)
            ev_w = self.load_wslot(oW, "wk")
            for eo in range(C):
                for blk in range(2):
                    bank = (eo * 2 + blk) % 4
                    w = self.use_bank(bank, (ev_w, ev_xn))
                    ev_mm = None
                    for c in range(C):
                        ev_mm = P.emit("tensor", lambda e, c=c, eo=eo, blk=blk, k=bank: e.matmul(
                            self.ps[k][:],
                            self.bf(oW + c * 1024 + eo * 128, 128),
                            self.bf(oXN3 + c * 1024 + blk * 512, 512),
                            start=(c == 0), stop=(c == C - 1)),
                            waits=(w if c == 0 else ()), inc="T" if c == C - 1 else None)
                    ev_k = P.emit("scalar", lambda e, eo=eo, blk=blk, k=bank, kh=kh: e.activation(
                        self.bf(oK + eo * 2048 + kh * 1024 + blk * 512, 512),
                        self.ps[k][:], AF.Identity,
                        bias=BQKV[:, C + eo: C + eo + 1]), waits=(ev_mm,), inc="S")
                    self.free_bank(bank, ev_k)
        ev_k_all = P.now("S")

        for h in range(2):
            ev_xn = self.xn_half(oXN3, h)
            ev_w = self.load_wslot(oW, "wq")
            self.proj_half(oW, oXN3, oQ, h, 0, ev_w, ev_xn)
            ev_w = self.load_wslot(oW, "wv")
            self.proj_half(oW, oXN3, oV, h, 2, ev_w, ev_xn, to_token_major=True)
            ev_w = self.load_wslot(oW, "wo")
            ev_qv = P.now("S", "V")
            for nl in range(4):
                ng = h * 4 + nl
                q0 = nl * 256
                for mg in range(2):
                    for qt in range(2):
                        for mi in range(4):
                            m = mg * 4 + mi
                            bank = qt * 2 + mi // 2
                            boff = (mi % 2) * 256
                            w = self.use_bank(bank, ev_k_all + ev_qv) if mi % 2 == 0 else ()
                            ev_mm = None
                            for c in range(C):
                                ev_mm = P.emit("tensor", lambda e, c=c, qt=qt, m=m, k=bank, boff=boff, q0=q0: e.matmul(
                                    self.ps[k][:, boff:boff + 256],
                                    self.bf(oQ + c * 1024 + q0 + qt * 128, 128),
                                    self.bf(oK + c * 2048 + m * 256, 256),
                                    start=(c == 0), stop=(c == C - 1)),
                                    waits=(w if c == 0 else ()), inc="T" if c == C - 1 else None)
                    ev_sc = ("T", P.cnt["T"])
                    for qt in range(2):
                        pv0, pv1 = self.ps[qt * 2], self.ps[qt * 2 + 1]
                        P.emit("vector", lambda e, pv0=pv0: e.tensor_reduce(
                            RV1[:, 0:2], pv0[:].rearrange("p (g k) -> p g k", k=256),
                            axis=AX.X, op=ALU.max), waits=(ev_sc,))
                        P.emit("vector", lambda e, pv1=pv1: e.tensor_reduce(
                            RV1[:, 2:4], pv1[:].rearrange("p (g k) -> p g k", k=256),
                            axis=AX.X, op=ALU.max))
                        P.emit("vector", lambda e: e.drain())
                        def mk_sub(pv, rcols, dcol):
                            def f(e, pv=pv, rcols=rcols, dcol=dcol):
                                return e.tensor_tensor(
                                    S1[:, dcol:dcol + 512].rearrange("p (g k) -> p g k", k=256),
                                    pv[:].rearrange("p (g k) -> p g k", k=256),
                                    RV1[:, rcols:rcols + 2].rearrange("p (g o) -> p g o", o=1)
                                        .broadcast_to([128, 2, 256]), ALU.subtract)
                            return f
                        P.emit("vector", mk_sub(pv0, 0, 0))
                        ev_d = P.emit("vector", mk_sub(pv1, 2, 512), inc="V")
                        ev_e = P.emit("scalar", lambda e: e.activation(
                            S1[:, 0:1024], S1[:, 0:1024], AF.Exp, scale=SCALE),
                            waits=(ev_d,), inc="S")
                        P.emit("vector", lambda e: e.tensor_reduce(
                            RV2[:, 0:4], S1[:, 0:1024].rearrange("p (g k) -> p g k", k=256),
                            axis=AX.X, op=ALU.add), waits=(ev_e,))
                        P.emit("vector", lambda e: e.drain())
                        P.emit("vector", lambda e: e.reciprocal(RV3[:, 0:4], RV2[:, 0:4]))
                        P.emit("vector", lambda e: e.drain())
                        P.emit("vector", lambda e: e.tensor_tensor(
                            S1[:, 0:1024].rearrange("p (g k) -> p g k", k=256),
                            S1[:, 0:1024].rearrange("p (g k) -> p g k", k=256),
                            RV3[:, 0:4].rearrange("p (g o) -> p g o", o=1)
                                .broadcast_to([128, 4, 256]), ALU.mult))
                        P.emit("vector", lambda e: e.drain())
                        P.emit("vector", lambda e: e.tensor_tensor(
                            S1[:, 1024:1280], S1[:, 0:256], S1[:, 256:512], ALU.add))
                        P.emit("vector", lambda e: e.tensor_tensor(
                            S1[:, 1280:1536], S1[:, 512:768], S1[:, 768:1024], ALU.add))
                        P.emit("vector", lambda e: e.drain())
                        P.emit("vector", lambda e: e.tensor_tensor(
                            S1[:, 1024:1280], S1[:, 1024:1280], S1[:, 1280:1536], ALU.add))
                        P.emit("vector", lambda e: e.drain())
                        if mg * 4 <= ng < mg * 4 + 4:
                            mi = ng - mg * 4
                            P.emit("vector", lambda e, mi=mi: e.tensor_tensor(
                                S1[:, 1024:1280], S1[:, 1024:1280],
                                S1[:, mi * 256:(mi + 1) * 256], ALU.add))
                            P.emit("vector", lambda e: e.drain())
                        if mg == 0:
                            ev_g = P.emit("vector", lambda e, qt=qt: e.tensor_copy(
                                S2[:, qt * 256:(qt + 1) * 256], S1[:, 1024:1280]), inc="V")
                        else:
                            ev_g = P.emit("vector", lambda e, qt=qt: e.tensor_tensor(
                                S2[:, qt * 256:(qt + 1) * 256],
                                S2[:, qt * 256:(qt + 1) * 256], S1[:, 1024:1280], ALU.add),
                                inc="V")
                        self.free_bank(qt * 2, ev_d)
                        self.free_bank(qt * 2 + 1, ev_d)
                ev_g_all = ("V", P.cnt["V"])
                P.emit("vector", lambda e: e.tensor_copy(S1b[:, 0:512], S2[:, 0:512]),
                       waits=(ev_g_all,))
                ev_gb = P.emit("vector", lambda e: e.drain(), inc="V")
                for qt in range(2):
                    for kb in range(2):
                        tb = 4 + (qt * 2 + kb) % 2
                        tw = self.use_bank(tb, (ev_gb,))
                        ev_tp = P.emit("tensor", lambda e, qt=qt, kb=kb, k=tb: e.transpose(
                            self.ps[k][:, 0:64].bitcast(BF16),
                            S1b[:, qt * 256 + kb * 128: qt * 256 + (kb + 1) * 128], IDB[:]),
                            waits=tw, inc="T")
                        ev_gt = P.emit("vector", lambda e, qt=qt, kb=kb, k=tb: e.tensor_copy(
                            S1b[:, 512 + kb * 256 + qt * 128: 512 + kb * 256 + (qt + 1) * 128],
                            self.ps[k][:, 0:64].bitcast(BF16)), waits=(ev_tp,), inc="V")
                        self.free_bank(tb, ev_gt)
                ev_gt_all = ("V", P.cnt["V"])
                for ec in range(C):
                    bank = 4 + ec // 2
                    boff = (ec % 2) * 256
                    w = self.use_bank(bank, (ev_gt_all,)) if ec % 2 == 0 else ()
                    ev_mm = None
                    for kt in range(2):
                        lt = (ng - h * 4) * 2 + kt
                        ev_mm = P.emit("tensor", lambda e, lt=lt, ec=ec, kt=kt, k=bank, boff=boff: e.matmul(
                            self.ps[k][:, boff:boff + 256],
                            self.bf(oV + lt * 1024 + ec * 128, 128),
                            S1b[:, 512 + kt * 256: 512 + (kt + 1) * 256],
                            start=(kt == 0), stop=(kt == 1)),
                            waits=(w if kt == 0 else ()), inc="T" if kt == 1 else None)
                    if ec % 2 == 1:
                        ev_av = P.emit("vector", lambda e, ec=ec, k=bank: e.tensor_copy(
                            S1b[:, 1536 + (ec - 1) * 256: 1536 + (ec + 1) * 256],
                            self.ps[k][:]), waits=(ev_mm,), inc="V")
                        self.free_bank(bank, ev_av)
                ev_av_all = ("V", P.cnt["V"])
                gq = h * 1024 + nl * 256
                for eo in range(C):
                    bank = eo // 2
                    boff = (eo % 2) * 256
                    w = self.use_bank(bank, (ev_av_all, ("LW", P.cnt["LW"]))) if eo % 2 == 0 else ()
                    ev_mm = None
                    for ec in range(C):
                        ev_mm = P.emit("tensor", lambda e, ec=ec, eo=eo, k=bank, boff=boff: e.matmul(
                            self.ps[k][:, boff:boff + 256],
                            self.bf(oW + ec * 1024 + eo * 128, 128),
                            S1b[:, 1536 + ec * 256: 1536 + (ec + 1) * 256],
                            start=(ec == 0), stop=(ec == C - 1)),
                            waits=(w if ec == 0 else ()), inc="T" if ec == C - 1 else None)
                    if eo % 2 == 1:
                        for eo2 in (eo - 1, eo):
                            bo2 = (eo2 % 2) * 256
                            ev_o = P.emit("vector", lambda e, eo2=eo2, k=bank, bo2=bo2, gq=gq: e.scalar_tensor_tensor(
                                XT[:, eo2 * N + gq: eo2 * N + gq + 256],
                                self.ps[k][:, bo2:bo2 + 256],
                                BQKV[:, 3 * C + eo2: 3 * C + eo2 + 1],
                                XT[:, eo2 * N + gq: eo2 * N + gq + 256],
                                ALU.add, ALU.add), waits=(ev_mm,), inc="V")
                        self.free_bank(bank, ev_o)

    def block_l1_ffn(self, seq):
        P = self.P
        XT, M, R, S1, S2 = (self.sb[k] for k in ("XT", "M", "R", "S1", "S2"))
        GROUT, G0B, GTR, SELS = self.sb["GROUT"], self.sb["G0B"], self.sb["GTR"], self.sb["SELS"]
        RA, RB, RC = self.sb["RA"], self.sb["RB"], self.sb["RC"]
        RV1, RV2, RV3 = self.sb["RV1"], self.sb["RV2"], self.sb["RV3"]
        WGT, WET, GBB, EBB = self.sb["WGT"], self.sb["WET"], self.sb["GBB"], self.sb["EBB"]
        S1b = S1[:].bitcast(BF16)
        S2b = S2[:].bitcast(BF16)
        oHR, oXN4 = 0, 32768
        NEG = -1e30
        w1t, w2t, w_int, w_outt = (self.dram[k] for k in ("w1_t", "w2_t", "w_in_t", "w_out_t"))

        ev_xn, ev_rt = self.xn_snapshot(oXN4, routers=(
            (WGT[:], self.sb["WGLO"][:], 2, (0, 1, 2, 3), RB[:, 0:32]),
            (WET[:], self.sb["WELO"][:], 8, (4, 5, 6, 7), GROUT[:])))
        ev_gl = ev_el = ev_rt
        def g2(ap):
            return ap.rearrange("p (t k) -> p t k", k=2)
        def g8(ap):
            return ap.rearrange("p (t k) -> p t k", k=8)
        def b16(ap, k):
            return ap.rearrange("p (t o) -> p t o", o=1).broadcast_to([128, 16, k])
        ev1 = P.emit("vector", lambda e: e.tensor_tensor(
            g2(RA[:, 0:32]), g2(RB[:, 0:32]),
            GBB[:].rearrange("p (o k) -> p o k", o=1).broadcast_to([128, 16, 2]), ALU.add),
            waits=(ev_gl,), inc="V")
        ev2 = P.emit("vector", lambda e: e.tensor_tensor(
            g8(GROUT[:]), g8(GROUT[:]),
            EBB[:].rearrange("p (o k) -> p o k", o=1).broadcast_to([128, 16, 8]), ALU.add),
            waits=(ev_el,), inc="V")
        P.emit("vector", lambda e: e.drain())
        GL = RA[:, 0:32]
        GEQ = RB[:, 0:32]
        P.emit("vector", lambda e: e.drain())
        P.emit("vector", lambda e: e.tensor_reduce(RV1[:], g2(GL), axis=AX.X, op=ALU.max))
        P.emit("vector", lambda e: e.drain())
        P.emit("vector", lambda e: e.tensor_tensor(g2(GEQ), g2(GL), b16(RV1[:], 2), ALU.is_ge))
        P.emit("vector", lambda e: e.tensor_reduce(RV2[:], g2(GL), axis=AX.X, op=ALU.add))
        P.emit("vector", lambda e: e.drain())
        P.emit("vector", lambda e: e.tensor_tensor(RV3[:], RV1[:], RV2[:], ALU.subtract))
        P.emit("vector", lambda e: e.drain())
        ev_sm = P.emit("vector", lambda e: e.tensor_tensor(RV3[:], RV3[:], RV1[:], ALU.add),
                       inc="V")
        ev_gp = P.emit("scalar", lambda e: e.activation(RV3[:], RV3[:], AF.Sigmoid),
                       waits=(ev_sm,), inc="S")
        geq_b = GEQ.rearrange("p (t g o) -> p t g o", g=2, o=1).broadcast_to([128, 16, 2, 4])
        P.emit("vector", lambda e: e.tensor_scalar(
            RC[:].rearrange("p (t g k) -> p t g k", g=2, k=4), geq_b, 1e30, -1e30,
            ALU.mult, ALU.add))
        P.emit("vector", lambda e: e.drain())
        P.emit("vector", lambda e: e.tensor_tensor(RC[:], GROUT[:], RC[:], ALU.add))
        P.emit("vector", lambda e: e.drain())
        P.emit("vector", lambda e: e.tensor_reduce(RV1[:], g8(RC[:]), axis=AX.X, op=ALU.max))
        P.emit("vector", lambda e: e.drain())
        ev_d = P.emit("vector", lambda e: e.tensor_tensor(
            g8(G0B[:]), g8(RC[:]), b16(RV1[:], 8), ALU.subtract), inc="V")
        ev_e = P.emit("scalar", lambda e: e.activation(G0B[:], G0B[:], AF.Exp),
                      waits=(ev_d,), inc="S")
        P.emit("vector", lambda e: e.tensor_reduce(RV1[:], g8(G0B[:]), axis=AX.X, op=ALU.add),
               waits=(ev_e,))
        P.emit("vector", lambda e: e.drain())
        P.emit("vector", lambda e: e.reciprocal(RV2[:], RV1[:]))
        P.emit("vector", lambda e: e.drain())
        P.emit("vector", lambda e: e.tensor_tensor(g8(G0B[:]), g8(G0B[:]), b16(RV2[:], 8), ALU.mult),
               waits=(ev_gp,))
        P.emit("vector", lambda e: e.drain())
        P.emit("vector", lambda e: e.tensor_tensor(g8(G0B[:]), g8(G0B[:]), b16(RV3[:], 8), ALU.mult))
        P.emit("vector", lambda e: e.drain())
        P.emit("vector", lambda e: e.tensor_reduce(RV1[:], g8(G0B[:]), axis=AX.X, op=ALU.max))
        P.emit("vector", lambda e: e.drain())
        P.emit("vector", lambda e: e.tensor_tensor(g8(RB[:]), g8(G0B[:]), b16(RV1[:], 8), ALU.is_ge))
        P.emit("vector", lambda e: e.drain())
        P.emit("vector", lambda e: e.scalar_tensor_tensor(g8(RC[:]), g8(RB[:]), NEG, g8(G0B[:]),
                                                          ALU.mult, ALU.add))
        P.emit("vector", lambda e: e.drain())
        P.emit("vector", lambda e: e.tensor_reduce(RV2[:], g8(RC[:]), axis=AX.X, op=ALU.max))
        P.emit("vector", lambda e: e.drain())
        P.emit("vector", lambda e: e.tensor_tensor(g8(RA[:]), g8(RC[:]), b16(RV2[:], 8), ALU.is_ge))
        P.emit("vector", lambda e: e.drain())
        P.emit("vector", lambda e: e.tensor_tensor(g8(RB[:]), g8(RB[:]), b16(RV1[:], 8), ALU.mult))
        P.emit("vector", lambda e: e.drain())
        P.emit("vector", lambda e: e.tensor_tensor(g8(RA[:]), g8(RA[:]), b16(RV2[:], 8), ALU.mult))
        P.emit("vector", lambda e: e.drain())
        ev_gates = P.emit("vector", lambda e: e.tensor_tensor(G0B[:], RB[:], RA[:], ALU.add),
                          inc="V")
        ev_gtr = self.gates_transpose(ev_gates)

        ev_stripe = [None, None]
        ev_hr = None
        for hc in range(16):
            buf = hc % 2
            wl = (ev_stripe[buf],) if ev_stripe[buf] is not None else (ev_xn, ("V", P.cnt["V"]))
            ev_la = P.emit("gpsimd", lambda e, hc=hc, buf=buf: e.dma_start(
                out=S2b[:, buf * 2048: buf * 2048 + 1024].rearrange("p (c k) -> p c k", k=128),
                in_=w1t[:, hc * 128:(hc + 1) * 128].rearrange("(c p) k -> p c k", p=128)),
                waits=wl, inc="LW")
            ev_lb = P.emit("gpsimd", lambda e, hc=hc, buf=buf: e.dma_start(
                out=S2b[:, buf * 2048 + 1024: buf * 2048 + 2048].rearrange("p (c k) -> p c k", k=128),
                in_=w1t[:, 2048 + hc * 128: 2048 + (hc + 1) * 128].rearrange("(c p) k -> p c k", p=128)),
                inc="LW")
            for nb in range(4):
                n0 = nb * 512
                bka, bkb = 6, 7
                wa = self.use_bank(bka, (ev_lb,) if nb == 0 else ())
                wb = self.use_bank(bkb, ())
                ev_ma = ev_mb = None
                for c in range(C):
                    ev_ma = P.emit("tensor", lambda e, c=c, buf=buf, n0=n0: e.matmul(
                        self.ps[6][:], S2b[:, buf * 2048 + c * 128: buf * 2048 + (c + 1) * 128],
                        self.bf(oXN4 + c * N + n0, 512),
                        start=(c == 0), stop=(c == C - 1)),
                        waits=(wa if c == 0 else ()), inc="T" if c == C - 1 else None)
                for c in range(C):
                    ev_mb = P.emit("tensor", lambda e, c=c, buf=buf, n0=n0: e.matmul(
                        self.ps[7][:], S2b[:, buf * 2048 + 1024 + c * 128: buf * 2048 + 1024 + (c + 1) * 128],
                        self.bf(oXN4 + c * N + n0, 512),
                        start=(c == 0), stop=(c == C - 1)),
                        waits=(wb if c == 0 else ()), inc="T" if c == C - 1 else None)
                sl = nb % 2
                ev_sa = P.emit("scalar", lambda e, sl=sl: e.activation(
                    S1b[:, sl * 512:(sl + 1) * 512], self.ps[6][:], AF.Silu),
                    waits=(ev_ma,), inc="S")
                self.free_bank(6, ev_sa)
                ev_hr = P.emit("vector", lambda e, hc=hc, n0=n0, sl=sl: e.tensor_tensor(
                    self.bf(oHR + hc * N + n0, 512),
                    S1b[:, sl * 512:(sl + 1) * 512], self.ps[7][:], ALU.mult),
                    waits=(ev_sa, ev_mb), inc="V")
                self.free_bank(7, ev_hr)
            ev_stripe[buf] = ev_hr and ("T", P.cnt["T"])
        ev_hr_all = ("V", P.cnt["V"])
        ev_gb_free = None
        ev_stripe = [None, None]
        ev_hr_mm_done = P.now("T")
        for ex in range(8):
            wg = (ev_gtr,) + ev_hr_mm_done if ex == 0 else ()
            if ev_gb_free is not None:
                wg = wg + (ev_gb_free,)
            ev_gb = None
            for g in range(4):
                bank = 6 + g % 2
                bwg = self.use_bank(bank, wg if g < 2 else ())
                ev_bc = P.emit("tensor", lambda e, ex=ex, g=g, k=bank: e.matmul(
                    self.ps[k][:], SELS[:, ex * 128:(ex + 1) * 128],
                    GTR[:, g * 512:(g + 1) * 512], start=True, stop=True),
                    waits=bwg, inc="T")
                ev_gb = P.emit("vector", lambda e, g=g, k=bank: e.tensor_copy(
                    S2b[:, 2048 + g * 512: 2048 + (g + 1) * 512], self.ps[k][:]),
                    waits=(ev_bc,), inc="V")
                self.free_bank(bank, ev_gb)
            for nb in range(4):
                n0 = nb * 512
                ev_y = None
                for hc in range(16):
                    buf = hc % 2
                    wl = (ev_stripe[buf],) if ev_stripe[buf] is not None else ev_hr_mm_done
                    ev_l = P.emit("gpsimd", lambda e, ex=ex, hc=hc, buf=buf: e.dma_start(
                        out=S2b[:, buf * 1024:(buf + 1) * 1024],
                        in_=w2t[ex, hc * 128:(hc + 1) * 128, :]), waits=wl, inc="LW")
                    sl = hc % 2
                    whg = (ev_gb, ("V", P.cnt["V"]))
                    if ev_y is not None:
                        whg = whg + (ev_y,)
                    ev_hg = P.emit("gpsimd", lambda e, hc=hc, n0=n0, sl=sl: e.tensor_tensor(
                        S1b[:, sl * 512:(sl + 1) * 512],
                        self.bf(oHR + hc * N + n0, 512),
                        S2b[:, 2048 + n0: 2048 + n0 + 512], ALU.mult),
                        waits=whg, inc="G")
                    for dc in range(C):
                        wy = ()
                        if dc == 0:
                            wy = (ev_l, ev_hg)
                            if hc == 0:
                                wy = self.use_bank(dc, wy) + self.use_bank((dc + 1) % 8, ())
                                for dcb in range(2, 8):
                                    wy = wy + self.use_bank(dcb, ())
                                self.bank_free = [None] * 8
                        ev_y = P.emit("tensor", lambda e, buf=buf, dc=dc, hc=hc: e.matmul(
                            self.ps[dc][:],
                            S2b[:, buf * 1024 + dc * 128: buf * 1024 + (dc + 1) * 128],
                            S1b[:, (hc % 2) * 512: (hc % 2 + 1) * 512],
                            start=(hc == 0), stop=(hc == 15)),
                            waits=wy, inc="T" if (dc == C - 1) else None)
                    ev_stripe[buf] = ev_y
                for dc in range(C):
                    ev_add = P.emit("vector", lambda e, dc=dc, n0=n0: e.tensor_tensor(
                        XT[:, dc * N + n0: dc * N + n0 + 512],
                        XT[:, dc * N + n0: dc * N + n0 + 512], self.ps[dc][:], ALU.add),
                        waits=(ev_y,), inc="V")
                    self.free_bank(dc, ev_add)
            ev_gb_free = ("G", P.cnt["G"])

        oHST, oWAB, oWO = 0, 8192, 12288
        ev_hr_free = ("T", P.cnt["T"])
        ev_ab = [None, None]
        ev_wo = [None, None]
        for nb in range(4):
            n0 = nb * 512
            ev_hst = None
            for hc in range(16):
                buf = hc % 2
                wl = (ev_ab[buf],) if ev_ab[buf] is not None else ()
                if nb == 0 and hc < 2:
                    wl = wl + (ev_hr_free,)
                ev_la = P.emit("gpsimd", lambda e, hc=hc, buf=buf: e.dma_start(
                    out=self.bf(oWAB + buf * 2048, 1024).rearrange("p (c k) -> p c k", k=128),
                    in_=w_int[:, hc * 128:(hc + 1) * 128].rearrange("(c p) k -> p c k", p=128)),
                    waits=wl, inc="LW")
                ev_lb = P.emit("gpsimd", lambda e, hc=hc, buf=buf: e.dma_start(
                    out=self.bf(oWAB + buf * 2048 + 1024, 1024).rearrange("p (c k) -> p c k", k=128),
                    in_=w_int[:, 2048 + hc * 128: 2048 + (hc + 1) * 128].rearrange("(c p) k -> p c k", p=128)),
                    inc="LW")
                wa = self.use_bank(6, (ev_lb,))
                wb = self.use_bank(7, ())
                ev_ma = ev_mb = None
                for c in range(C):
                    ev_ma = P.emit("tensor", lambda e, c=c, buf=buf, n0=n0: e.matmul(
                        self.ps[6][:], self.bf(oWAB + buf * 2048 + c * 128, 128),
                        self.bf(oXN4 + c * N + n0, 512),
                        start=(c == 0), stop=(c == C - 1)),
                        waits=(wa if c == 0 else ()), inc="T" if c == C - 1 else None)
                for c in range(C):
                    ev_mb = P.emit("tensor", lambda e, c=c, buf=buf, n0=n0: e.matmul(
                        self.ps[7][:], self.bf(oWAB + buf * 2048 + 1024 + c * 128, 128),
                        self.bf(oXN4 + c * N + n0, 512),
                        start=(c == 0), stop=(c == C - 1)),
                        waits=(wb if c == 0 else ()), inc="T" if c == C - 1 else None)
                ev_ab[buf] = ev_mb
                sl = hc % 2
                ev_sa = P.emit("scalar", lambda e, sl=sl: e.activation(
                    S1b[:, sl * 512:(sl + 1) * 512], self.ps[6][:], AF.Silu),
                    waits=(ev_ma,), inc="S")
                self.free_bank(6, ev_sa)
                ev_hst = P.emit("vector", lambda e, hc=hc, sl=sl: e.tensor_tensor(
                    self.bf(oHST + hc * 512, 512),
                    S1b[:, sl * 512:(sl + 1) * 512], self.ps[7][:], ALU.mult),
                    waits=(ev_sa, ev_mb), inc="V")
                self.free_bank(7, ev_hst)
            ev_y = None
            for hc in range(16):
                buf = hc % 2
                wl = (ev_wo[buf],) if ev_wo[buf] is not None else ()
                ev_l = P.emit("gpsimd", lambda e, hc=hc, buf=buf: e.dma_start(
                    out=self.bf(oWO + buf * 1024, 1024),
                    in_=w_outt[hc * 128:(hc + 1) * 128, :]), waits=wl, inc="LW")
                for dc in range(C):
                    wy = ()
                    if dc == 0:
                        wy = (ev_l, ev_hst)
                        if hc == 0:
                            for dcb in range(8):
                                wy = wy + self.use_bank(dcb, ())
                            self.bank_free = [None] * 8
                    ev_y = P.emit("tensor", lambda e, buf=buf, dc=dc, hc=hc: e.matmul(
                        self.ps[dc][:],
                        self.bf(oWO + buf * 1024 + dc * 128, 128),
                        self.bf(oHST + hc * 512, 512),
                        start=(hc == 0), stop=(hc == 15)),
                        waits=wy, inc="T" if dc == C - 1 else None)
                ev_wo[buf] = ev_y
            for dc in range(C):
                ev_add = P.emit("vector", lambda e, dc=dc, n0=n0: e.tensor_tensor(
                    XT[:, dc * N + n0: dc * N + n0 + 512],
                    XT[:, dc * N + n0: dc * N + n0 + 512], self.ps[dc][:], ALU.add),
                    waits=(ev_y,), inc="V")
                self.free_bank(dc, ev_add)


def make_core_inputs(x_core, W, cst):
    d = dict(cst)
    d["xin"] = np.ascontiguousarray(np.asarray(x_core, np.float32).reshape(NSEQ * N, D))
    d["bvq"] = feat_major_vec(W["L0_attn_bv_q"])
    d["bvk"] = feat_major_vec(W["L0_attn_bv_k"])
    d["bvv"] = feat_major_vec(W["L0_attn_bv_v"])
    d["rw_t"] = np.ascontiguousarray(np.asarray(W["L0_ffn_router_w"], np.float32).T)
    d["l0w1"] = np.ascontiguousarray(np.asarray(W["L0_ffn_w1"], np.float32))
    d["l0w2"] = np.ascontiguousarray(np.asarray(W["L0_ffn_w2"], np.float32))
    d["wq"] = np.ascontiguousarray(np.asarray(W["L1_attn_Wq"], np.float32))
    d["wk"] = np.ascontiguousarray(np.asarray(W["L1_attn_Wk"], np.float32))
    d["wv"] = np.ascontiguousarray(np.asarray(W["L1_attn_Wv"], np.float32))
    d["wo"] = np.ascontiguousarray(np.asarray(W["L1_attn_Wo"], np.float32))
    d["bq"] = feat_major_vec(W["L1_attn_bq"])
    d["bk"] = feat_major_vec(W["L1_attn_bk"])
    d["bv"] = feat_major_vec(W["L1_attn_bv"])
    d["bo"] = feat_major_vec(W["L1_attn_bo"])
    d["w_in_t"] = np.ascontiguousarray(np.asarray(W["L1_ffn_w_in"], np.float32).T)
    d["w_out_t"] = np.ascontiguousarray(np.asarray(W["L1_ffn_w_out"], np.float32).T)
    d["w1_t"] = np.ascontiguousarray(np.asarray(W["L1_ffn_w1"], np.float32).T)
    d["w2_t"] = np.ascontiguousarray(np.asarray(W["L1_ffn_w2"], np.float32).transpose(0, 2, 1))
    d["wg_t"] = np.ascontiguousarray(np.asarray(W["L1_ffn_wg"], np.float32).T)
    d["we_t"] = np.ascontiguousarray(np.asarray(W["L1_ffn_we"], np.float32).T)
    d["gb_b"] = np.tile(np.asarray(W["L1_ffn_gb"], np.float32)[None, :], (128, 1))
    d["eb_b"] = np.tile(np.asarray(W["L1_ffn_eb"], np.float32)[None, :], (128, 1))
    return d



def kernel(**inputs):
    x = np.asarray(inputs["L0_attn_x"], np.float32)
    cst = host_constants()
    b = Builder()
    in_maps = [make_core_inputs(x[2 * i:2 * i + 2], inputs, cst) for i in range(8)]
    res = run_bass_kernel_spmd(b.nc, in_maps, core_ids=list(range(8)))
    out = np.concatenate([res.results[i]["yout"].reshape(NSEQ, N, D) for i in range(8)], axis=0)
    return out.astype(np.float32)


# version 02
# speedup vs baseline: 1.1322x; 1.1322x over previous
import numpy as np
import ml_dtypes
import concourse.bass as bass
import concourse.mybir as mybir
from concourse.bass_utils import run_bass_kernel_spmd
from contextlib import ExitStack

F32 = mybir.dt.float32
F16 = mybir.dt.float16
BF16 = mybir.dt.bfloat16
AF = mybir.ActivationFunctionType
ALU = mybir.AluOpType
AX = mybir.AxisListType
BF = ml_dtypes.bfloat16

N = 2048
D = 1024
C = 8
NT = 16
NSEQ = 2
EPS = 1e-5

ENGINES = ("tensor", "vector", "scalar", "gpsimd", "sync")
COMPUTE_SEM_OWNER = {"T": "tensor", "V": "vector", "S": "scalar", "G": "gpsimd"}
DMA_SEMS = ("LW", "LS", "ST")
ALL_SEMS = tuple(COMPUTE_SEM_OWNER) + DMA_SEMS


class Plan:
    def __init__(self):
        self.steps = {e: [] for e in ENGINES}
        self.cnt = {s: 0 for s in ALL_SEMS}

    def emit(self, eng, fn, waits=(), inc=None):
        waits = tuple(w for w in waits if w is not None)
        ev = None
        amt = 0
        if inc is not None:
            amt = 16 if inc in DMA_SEMS else 1
            self.cnt[inc] += amt
            ev = (inc, self.cnt[inc])
        self.steps[eng].append((waits, fn, inc, amt))
        return ev

    def now(self, *sems):
        return tuple((s, self.cnt[s]) for s in sems)

    def realize(self, nc, sems):
        hw = {e: {s: 0 for s in ALL_SEMS} for e in ENGINES}
        with nc.Block() as block:
            for ename in ENGINES:
                def body(eng, ename=ename):
                    marks = hw[ename]
                    for waits, fn, inc, amt in self.steps[ename]:
                        for (s, v) in waits:
                            if COMPUTE_SEM_OWNER.get(s) == ename:
                                continue
                            if marks[s] < v:
                                eng.wait_ge(sems[s], v)
                                marks[s] = v
                        ret = fn(eng)
                        if inc is not None:
                            inst = ret[-1] if isinstance(ret, (list, tuple)) else ret
                            inst.then_inc(sems[inc], amt)
                getattr(block, ename)(body)



def np_ln(x):
    m = x.mean(-1, keepdims=True)
    v = x.var(-1, keepdims=True)
    return (x - m) / np.sqrt(v + EPS)

def np_softmax(x, axis=-1):
    m = x.max(axis=axis, keepdims=True)
    e = np.exp(x - m)
    return e / e.sum(axis=axis, keepdims=True)

def np_l0_attn(x, bvq, bvk, bvv):
    bq, bk, bv = np.sign(bvq), np.sign(bvk), np.sign(bvv)
    Q = np.sign(x * bq); K = np.sign(x * bk); V = x * bv
    s = (Q @ K.T) * (D ** -0.5)
    n = x.shape[0]
    causal = np.tril(np.ones((n, n), bool))
    a = 1.0 / (1.0 + np.exp(-4.0 * s))
    a = np.where(causal, a, 0.0)
    return (a @ V).astype(np.float32)

def np_l0_ffn(x, router_w, w1, w2):
    logits = x @ router_w.T
    top2 = np.argsort(-logits, axis=1, kind="stable")[:, :2]
    vals = np.take_along_axis(logits, top2, axis=1)
    w = np_softmax(vals, axis=-1)
    G = np.zeros_like(logits)
    np.put_along_axis(G, top2, w, axis=1)
    out = np.zeros_like(x)
    for e in range(8):
        h = np.maximum(x @ w1[e], 0.0) ** 2
        out += (h @ w2[e]) * G[:, e:e + 1]
    return out.astype(np.float32)

def np_l1_attn(x, Wq, bq, Wk, bk, Wv, bv, Wo, bo):
    CH = 256
    n = x.shape[0]; nch = n // CH
    sc = 1.0 / np.sqrt(D)
    Q = (x @ Wq + bq).reshape(nch, CH, D)
    K = (x @ Wk + bk).reshape(nch, CH, D)
    V = (x @ Wv + bv).reshape(nch, CH, D)
    out = np.zeros((nch, CH, D), np.float32)
    for i in range(nch):
        Gs = np.zeros((CH, CH), np.float32)
        for m in range(nch):
            A = np_softmax(Q[i] @ K[m].T * sc, axis=-1)
            Gs += A * (2.0 if m == i else 1.0)
        out[i] = Gs @ V[i]
    return (out.reshape(n, D) @ Wo + bo).astype(np.float32)

def np_l1_ffn(x, w_in, w_out, w1, w2, wg, we, gb, eb):
    def swiglu(h):
        a, b = np.split(h, 2, axis=-1)
        return (a / (1.0 + np.exp(-a))) * b
    shared = swiglu(x @ w_in.T) @ w_out.T
    gl = x @ wg.T + gb
    gprob = np_softmax(gl, axis=-1)
    gidx = gl.argmax(axis=1)
    el = x @ we.T + eb
    egid = np.arange(8) // 4
    masked = np.where(gidx[:, None] == egid[None, :], el, -np.inf)
    eprob = np_softmax(masked, axis=-1)
    pexp = eprob * np.take_along_axis(gprob, gidx[:, None], axis=1)
    top2 = np.argsort(-pexp, axis=1, kind="stable")[:, :2]
    vals = np.take_along_axis(pexp, top2, axis=1)
    Wse = np.zeros_like(el)
    np.put_along_axis(Wse, top2, vals, axis=1)
    h = swiglu(x @ w1.T)
    routed = np.zeros_like(shared)
    for e in range(8):
        routed += (h * Wse[:, e:e + 1]) @ w2[e].T
    return (shared + routed).astype(np.float32)

def np_module(x, W, blocks):
    if "l0a" in blocks:
        x = x + np_l0_attn(np_ln(x), W["L0_attn_bv_q"], W["L0_attn_bv_k"], W["L0_attn_bv_v"])
    if "l0f" in blocks:
        x = x + np_l0_ffn(np_ln(x), W["L0_ffn_router_w"], W["L0_ffn_w1"], W["L0_ffn_w2"])
    if "l1a" in blocks:
        x = x + np_l1_attn(np_ln(x), W["L1_attn_Wq"], W["L1_attn_bq"], W["L1_attn_Wk"], W["L1_attn_bk"],
                           W["L1_attn_Wv"], W["L1_attn_bv"], W["L1_attn_Wo"], W["L1_attn_bo"])
    if "l1f" in blocks:
        x = x + np_l1_ffn(np_ln(x), W["L1_ffn_w_in"], W["L1_ffn_w_out"], W["L1_ffn_w1"], W["L1_ffn_w2"],
                          W["L1_ffn_wg"], W["L1_ffn_we"], W["L1_ffn_gb"], W["L1_ffn_eb"])
    return x



def host_constants():
    cst = {}
    cst["id_bf"] = np.eye(128, dtype=BF)
    cst["id_f32"] = np.eye(128, dtype=np.float32)
    cst["ones128"] = np.ones((128, 128), dtype=BF)
    cst["ones_f32"] = np.ones((128, 128), dtype=np.float32)
    sel = np.zeros((8, 8 * 128), dtype=BF)
    for e in range(8):
        sel[e, e * 128:(e + 1) * 128] = 1
    cst["sel"] = sel
    masks = np.zeros((128, 4 * 512), dtype=BF)
    i = np.arange(128)[:, None]
    j = np.arange(512)[None, :]
    for oi, off in enumerate((0, 128, 256, 384)):
        masks[:, oi * 512:(oi + 1) * 512] = (i + off <= j).astype(BF)
    cst["masks"] = masks
    return cst

def feat_major_vec(v):
    return np.ascontiguousarray(np.asarray(v, np.float32).reshape(C, 128).T)


class Builder:
    def __init__(self, blocks=("l0a", "l0f", "l1a", "l1f"), debug=False):
        self.blocks = blocks
        self.debug = debug
        self.nc = bass.Bass("TRN2", target_bir_lowering=False)
        self.P = Plan()
        self.stack = ExitStack()
        self.dram = {}
        self.sb = {}
        self.ps = []
        self.bank_free = [None] * 8
        self._build()

    def dram_in(self, name, shape, dtype=F32):
        self.dram[name] = self.nc.dram_tensor(name, list(shape), dtype, kind="ExternalInput").ap()
        return self.dram[name]

    def dram_out(self, name, shape, dtype=F32):
        self.dram[name] = self.nc.dram_tensor(name, list(shape), dtype, kind="ExternalOutput").ap()
        return self.dram[name]

    def sbuf(self, name, shape, dtype):
        self.sb[name] = self.stack.enter_context(self.nc.sbuf_tensor(name, list(shape), dtype))
        return self.sb[name]

    def bf(self, lo, w):
        return self.sb["BFA"][:, lo:lo + w]

    def use_bank(self, k, waits=()):
        if self.bank_free[k] is not None:
            waits = tuple(waits) + (self.bank_free[k],)
        return tuple(waits)

    def free_bank(self, k, ev):
        self.bank_free[k] = ev

    def _build(self):
        nc, P = self.nc, self.P
        self.dram_in("xin", [NSEQ * N, D])
        self.dram_out("yout", [NSEQ * N, D])
        self.dram_in("id_bf", [128, 128], BF16)
        self.dram_in("id_f32", [128, 128])
        self.dram_in("ones128", [128, 128], BF16)
        self.dram_in("ones_f32", [128, 128])
        self.dram_in("sel", [8, 8 * 128], BF16)
        self.dram_in("masks", [128, 4 * 512], BF16)
        self.dram_in("bvq", [128, C]); self.dram_in("bvk", [128, C]); self.dram_in("bvv", [128, C])
        self.dram_in("rw_t", [D, 8])
        self.dram_in("l0w1", [8, D, 2048])
        self.dram_in("l0w2", [8, 2048, D])
        for w in ("wq", "wk", "wv", "wo"):
            self.dram_in(w, [D, D])
        for b in ("bq", "bk", "bv", "bo"):
            self.dram_in(b, [128, C])
        self.dram_in("w_in_t", [D, 4096])
        self.dram_in("w_out_t", [2048, D])
        self.dram_in("w1_t", [D, 4096])
        self.dram_in("w2_t", [8, 2048, D])
        self.dram_in("wg_t", [D, 2])
        self.dram_in("we_t", [D, 8])
        self.dram_in("gb_b", [128, 2])
        self.dram_in("eb_b", [128, 8])

        XT = self.sbuf("XT", [128, C * N], F32)
        for nm in ("M", "R", "S1", "S2"):
            self.sbuf(nm, [128, N], F32)
        self.sbuf("IDB", [128, 128], BF16)
        self.sbuf("IDF", [128, 128], F32)
        self.sbuf("ONE", [128, 128], BF16)
        self.sbuf("ONEF", [128, 128], F32)
        self.sbuf("SELS", [8, 8 * 128], BF16)
        self.sbuf("MSK", [128, 4 * 512], BF16)
        self.sbuf("SGN", [128, 4 * C], F32)
        self.sbuf("BQKV", [128, 4 * C], F32)
        self.sbuf("RWT", [128, C * 8], BF16)
        self.sbuf("RWLO", [128, C * 8], BF16)
        self.sbuf("WGLO", [128, C * 2], BF16)
        self.sbuf("WELO", [128, C * 8], BF16)
        self.sbuf("RWTH", [128, C * 8], F16)
        self.sbuf("WGTH", [128, C * 2], F16)
        self.sbuf("WETH", [128, C * 8], F16)
        self.sbuf("WGT", [128, C * 2], BF16)
        self.sbuf("WET", [128, C * 8], BF16)
        self.sbuf("GBB", [128, 2], F32)
        self.sbuf("EBB", [128, 8], F32)
        self.sbuf("GROUT", [128, 128], F32)
        self.sbuf("G0B", [128, 128], F32)
        self.sbuf("GTR", [8, 2048], BF16)
        self.sbuf("RA", [128, 128], F32)
        self.sbuf("RB", [128, 128], F32)
        self.sbuf("RC", [128, 128], F32)
        self.sbuf("RV1", [128, 16], F32)
        self.sbuf("RV2", [128, 16], F32)
        self.sbuf("RV3", [128, 16], F32)
        self.BFW = 49152
        self.sbuf("BFA", [128, self.BFW], BF16)

        for i in range(8):
            self.ps.append(self.stack.enter_context(nc.psum_tensor(f"ps{i}", [128, 512], F32)))

        self.sems = {s: self.stack.enter_context(nc.semaphore(s)) for s in ALL_SEMS}

        def load(dst, src, q="sync"):
            return P.emit(q, lambda e, d=dst, s=src: e.dma_start(out=d, in_=s),
                          inc="LS" if q == "sync" else "LW")
        load(self.sb["IDB"][:], self.dram["id_bf"])
        load(self.sb["IDF"][:], self.dram["id_f32"])
        load(self.sb["ONE"][:], self.dram["ones128"])
        load(self.sb["ONEF"][:], self.dram["ones_f32"])
        load(self.sb["SELS"][:], self.dram["sel"])
        load(self.sb["MSK"][:], self.dram["masks"])
        load(self.sb["BQKV"][:, 0:C], self.dram["bq"])
        load(self.sb["BQKV"][:, C:2 * C], self.dram["bk"])
        load(self.sb["BQKV"][:, 2 * C:3 * C], self.dram["bv"])
        load(self.sb["BQKV"][:, 3 * C:4 * C], self.dram["bo"])
        load(self.sb["GBB"][:], self.dram["gb_b"])
        load(self.sb["EBB"][:], self.dram["eb_b"])
        load(self.sb["RA"][:, 0:C], self.dram["bvq"])
        load(self.sb["RA"][:, C:2 * C], self.dram["bvk"])
        load(self.sb["RA"][:, 2 * C:3 * C], self.dram["bvv"])
        load(self.sb["RWT"][:].rearrange("p (c k) -> p c k", k=8),
             self.dram["rw_t"].rearrange("(c p) k -> p c k", p=128), q="gpsimd")
        load(self.sb["WGT"][:].rearrange("p (c k) -> p c k", k=2),
             self.dram["wg_t"].rearrange("(c p) k -> p c k", p=128), q="gpsimd")
        load(self.sb["WET"][:].rearrange("p (c k) -> p c k", k=8),
             self.dram["we_t"].rearrange("(c p) k -> p c k", p=128), q="gpsimd")
        CST = self.sbuf("CST", [128, 8], F32)
        for i, v in enumerate((EPS, -0.5, 0.125)):
            P.emit("vector", lambda e, v=v, i=i: e.memset(CST[:, i:i + 1], v), inc="V")
            nc.const_aps.register(F32, v, CST[:, i:i + 1])
        load(self.sb["GROUT"][:, 0:64].rearrange("p (c k) -> p c k", k=8),
             self.dram["rw_t"].rearrange("(c p) k -> p c k", p=128))
        load(self.sb["G0B"][:, 0:16].rearrange("p (c k) -> p c k", k=2),
             self.dram["wg_t"].rearrange("(c p) k -> p c k", p=128))
        load(self.sb["RB"][:, 0:64].rearrange("p (c k) -> p c k", k=8),
             self.dram["we_t"].rearrange("(c p) k -> p c k", p=128))
        ev_wf = P.now("LS", "LW")
        P.emit("vector", lambda e: e.tensor_tensor(
            self.sb["RWLO"][:], self.sb["GROUT"][:, 0:64], self.sb["RWT"][:], ALU.subtract),
            waits=ev_wf)
        P.emit("vector", lambda e: e.tensor_tensor(
            self.sb["WGLO"][:], self.sb["G0B"][:, 0:16], self.sb["WGT"][:], ALU.subtract))
        ev_wlo = P.emit("vector", lambda e: e.tensor_tensor(
            self.sb["WELO"][:], self.sb["RB"][:, 0:64], self.sb["WET"][:], ALU.subtract), inc="V")
        P.emit("vector", lambda e: e.tensor_copy(self.sb["RWTH"][:], self.sb["RWT"][:]), waits=ev_wf)
        P.emit("vector", lambda e: e.tensor_copy(self.sb["WGTH"][:], self.sb["WGT"][:]))
        ev_wlo = P.emit("vector", lambda e: e.tensor_copy(self.sb["WETH"][:], self.sb["WET"][:]), inc="V")
        ev_const = P.now("LS", "LW") + (ev_wlo,)
        ev_sgn = P.emit("scalar", lambda e: e.activation(
            self.sb["SGN"][:, 0:3 * C], self.sb["RA"][:, 0:3 * C], AF.Sign),
            waits=ev_const, inc="S")
        self.ev_setup = (P.emit("vector", lambda e: e.tensor_tensor(
            self.sb["SGN"][:, 3 * C:4 * C], self.sb["SGN"][:, 0:C],
            self.sb["SGN"][:, C:2 * C], ALU.mult), waits=(ev_sgn,), inc="V"),) + ev_const

        for seq in range(NSEQ):
            self.build_seq(seq)

        final = P.now("T", "V", "S", "G", "LW", "LS", "ST")
        def _clear_all(e):
            ret = None
            for s in ALL_SEMS:
                ret = e.sem_clear(self.sems[s])
            return ret
        P.emit("sync", lambda e: _clear_all(e), waits=final)
        P.realize(nc, self.sems)
        self.stack.close()

    def build_seq(self, seq):
        self.load_input(seq)
        if "l0a" in self.blocks:
            self.ln_stats()
            self.block_l0_attn(seq)
        if "l0f" in self.blocks:
            self.ln_stats()
            self.block_l0_ffn(seq)
        if "l1a" in self.blocks:
            self.ln_stats()
            self.block_l1_attn(seq)
        if "l1f" in self.blocks:
            self.ln_stats()
            self.block_l1_ffn(seq)
        self.store_output(seq)

    def load_input(self, seq):
        P = self.P
        XT, S1, S2, IDF = self.sb["XT"], self.sb["S1"], self.sb["S2"], self.sb["IDF"]
        xin = self.dram["xin"]
        stage = [S1, S2]
        ev_done = [None, None]
        w0 = P.now("T", "V", "S", "G") + self.ev_setup + (("ST", P.cnt["ST"]),)
        for t in range(NT):
            buf = stage[t % 2]
            w = w0 if t < 2 else (ev_done[t % 2],)
            ev_ld = P.emit("sync", lambda e, b=buf, t=t: e.dma_start(
                out=b[:, 0:1024], in_=xin[seq * N + t * 128: seq * N + (t + 1) * 128, :]),
                waits=w, inc="LS")
            for c in range(C):
                bank = (t * C + c) % 2
                tw = self.use_bank(bank, (ev_ld,))
                ev_tp = P.emit("tensor", lambda e, b=buf, c=c, k=bank: e.transpose(
                    self.ps[k][:, 0:128], b[:, c * 128:(c + 1) * 128], IDF[:]),
                    waits=tw, inc="T")
                ev_ev = P.emit("vector", lambda e, c=c, t=t, k=bank: e.tensor_copy(
                    XT[:, c * N + t * 128: c * N + (t + 1) * 128], self.ps[k][:, 0:128]),
                    waits=(ev_tp,), inc="V")
                self.free_bank(bank, ev_ev)
            ev_done[t % 2] = ev_ev

    def store_output(self, seq):
        P = self.P
        XT, S1, S2, IDF = self.sb["XT"], self.sb["S1"], self.sb["S2"], self.sb["IDF"]
        yout = self.dram["yout"]
        stage = [S1, S2]
        ev_store = [None, None]
        w0 = P.now("T", "V", "S", "G")
        for t in range(NT):
            buf = stage[t % 2]
            for c in range(C):
                bank = (t * C + c) % 2
                w = self.use_bank(bank, w0 if (t == 0 and c == 0) else ())
                if c == 0 and ev_store[t % 2] is not None:
                    w = w + (ev_store[t % 2],)
                ev_tp = P.emit("tensor", lambda e, c=c, t=t, k=bank: e.transpose(
                    self.ps[k][:, 0:128], XT[:, c * N + t * 128: c * N + (t + 1) * 128], IDF[:]),
                    waits=w, inc="T")
                ev_ev = P.emit("vector", lambda e, c=c, k=bank, b=buf: e.tensor_copy(
                    b[:, c * 128:(c + 1) * 128], self.ps[k][:, 0:128]),
                    waits=(ev_tp,), inc="V")
                self.free_bank(bank, ev_ev)
            ev_store[t % 2] = P.emit("sync", lambda e, b=buf, t=t: e.dma_start(
                out=yout[seq * N + t * 128: seq * N + (t + 1) * 128, :], in_=b[:, 0:1024]),
                waits=(ev_ev,), inc="ST")

    def ln_stats(self):
        P = self.P
        XT, M, R, S1, ONE = (self.sb[k] for k in ("XT", "M", "R", "S1", "ONE"))
        ONEF = self.sb["ONEF"]
        off = self.BFW - 4 * N
        XH = [self.bf(off + i * N, N) for i in range(2)]
        SQ = [self.bf(off + (2 + i) * N, N) for i in range(2)]
        w0 = P.now("T", "S", "G")
        ev_sq = [None, None]
        last_mm = None
        for c in range(C):
            b = c % 2
            w = w0 if c < 2 else ()
            if ev_sq[b] is not None:
                w = w + (ev_sq[b],)
            if last_mm is not None and c >= 2:
                w = w + (last_mm,)
            ev_c = P.emit("vector", lambda e, c=c, b=b: e.tensor_copy(
                XH[b], XT[:, c * N:(c + 1) * N]), waits=w, inc="V")
            ev_q = P.emit("scalar", lambda e, b=b: e.activation(SQ[b], XH[b], AF.Square),
                          waits=(ev_c,), inc="S")
            ev_sq[b] = ev_q
            for g in range(4):
                bw = self.use_bank(g, w0) if c == 0 else ()
                P.emit("tensor", lambda e, g=g, c=c: e.matmul(
                    self.ps[g][:], ONEF[:], XT[:, c * N + g * 512: c * N + (g + 1) * 512],
                    start=(c == 0), stop=(c == C - 1)), waits=bw)
                bw2 = self.use_bank(4 + g, (ev_q,)) if c == 0 else (ev_q,)
                evm = P.emit("tensor", lambda e, b=b, g=g, c=c: e.matmul(
                    self.ps[4 + g][:], ONE[:], SQ[b][:, g * 512:(g + 1) * 512],
                    start=(c == 0), stop=(c == C - 1)), waits=bw2,
                    inc="T" if g == 3 else None)
                if g == 3:
                    last_mm = ("T", P.cnt["T"])
        ev_mm = last_mm
        ev_m = []
        for g in range(4):
            ev_m.append(P.emit("vector", lambda e, g=g: e.tensor_scalar(
                M[:, g * 512:(g + 1) * 512], self.ps[g][:], 1.0 / D, None, ALU.mult),
                waits=(ev_mm,), inc="V"))
        P.emit("vector", lambda e: e.drain())
        ev_m2 = P.emit("vector", lambda e: e.tensor_tensor(S1[:], M[:], M[:], ALU.mult), inc="V")
        P.emit("vector", lambda e: e.drain())
        ev_var = []
        for g in range(4):
            ev_var.append(P.emit("vector", lambda e, g=g: e.scalar_tensor_tensor(
                S1[:, g * 512:(g + 1) * 512], self.ps[4 + g][:], 1.0 / D,
                S1[:, g * 512:(g + 1) * 512], ALU.mult, ALU.subtract), inc="V"))
        for g in range(4):
            self.free_bank(g, ev_m[g])
            self.free_bank(4 + g, ev_var[g])
        ev_ln = P.emit("scalar", lambda e: e.activation(S1[:], S1[:], AF.Ln, bias=EPS),
                       waits=tuple(ev_var), inc="S")
        self.ev_lnr = P.emit("scalar", lambda e: e.activation(R[:], S1[:], AF.Exp, scale=-0.5),
                             waits=(ev_ln,), inc="S")
        self.ev_lnm = ev_m[-1]

    def block_l0_attn(self, seq):
        P = self.P
        XT, M, R, S1, S2 = (self.sb[k] for k in ("XT", "M", "R", "S1", "S2"))
        IDB, MSK, SGN = self.sb["IDB"], self.sb["MSK"], self.sb["SGN"]
        oZS, oV, oZQ, oAST, oSG = 0, 16384, 32768, 36864, 45056
        oVSC = oSG + 1024

        ev_prev_zs = None
        ev_prev_tp = None
        ev_vt = None
        for c in range(C):
            w = (self.ev_lnr, self.ev_lnm)
            if ev_prev_zs is not None:
                w = w + (ev_prev_zs,)
            P.emit("vector", lambda e, c=c: e.tensor_tensor(
                S1[:], XT[:, c * N:(c + 1) * N], M[:], ALU.subtract), waits=w)
            ev_rs = P.emit("vector", lambda e, c=c: e.tensor_scalar(
                S2[:], R[:], SGN[:, 2 * C + c:2 * C + c + 1], None, ALU.mult), inc="V")
            P.emit("vector", lambda e: e.drain())
            wv = (ev_prev_tp,) if ev_prev_tp is not None else ()
            ev_vsc = P.emit("vector", lambda e: e.tensor_tensor(
                self.bf(oVSC, 2048), S1[:], S2[:], ALU.mult), waits=wv, inc="V")
            ev_prev_zs = P.emit("scalar", lambda e, c=c: e.activation(
                self.bf(oZS + c * N, N), S1[:], AF.Sign,
                scale=SGN[:, 3 * C + c:3 * C + c + 1]), waits=(ev_rs,), inc="S")
            for t in range(NT):
                bank = (c * NT + t) % 2
                tw = self.use_bank(bank, (ev_vsc,))
                ev_tp = P.emit("tensor", lambda e, t=t, k=bank: e.transpose(
                    self.ps[k][:, 0:64].bitcast(BF16),
                    self.bf(oVSC + t * 128, 128), IDB[:]), waits=tw, inc="T")
                ev_vt = P.emit("vector", lambda e, t=t, c=c, k=bank: e.tensor_copy(
                    self.bf(oV + t * 1024 + c * 128, 128),
                    self.ps[k][:, 0:64].bitcast(BF16)), waits=(ev_tp,), inc="V")
                self.free_bank(bank, ev_vt)
            ev_prev_tp = ("T", P.cnt["T"])
        ev_zs_all = ("S", P.cnt["S"])
        ev_vt_all = ("V", P.cnt["V"])

        for qb in range(4):
            q0 = qb * 512
            nkt = qb * 4 + 4
            wT = P.now("T")
            ev_zq = None
            ev_sub_prev = None
            for c in range(C):
                w = wT + (self.ev_lnr, self.ev_lnm)
                if ev_zq is not None:
                    w = w + (ev_zq,)
                ev_sub = P.emit("vector", lambda e, c=c, q0=q0: e.tensor_tensor(
                    S1[:, 0:512], XT[:, c * N + q0: c * N + q0 + 512],
                    M[:, q0:q0 + 512], ALU.subtract), waits=w, inc="V")
                ev_zq = P.emit("scalar", lambda e, c=c: e.activation(
                    self.bf(oZQ + c * 512, 512), S1[:, 0:512], AF.Sign),
                    waits=(ev_sub,), inc="S")
            ev_mask = [None, None]
            for kt in range(nkt):
                k0 = kt * 128
                bank = kt % 2
                w = self.use_bank(bank, (ev_zs_all, ev_zq))
                ev_mm = None
                for c in range(C):
                    ev_mm = P.emit("tensor", lambda e, c=c, kt=kt, k=bank: e.matmul(
                        self.ps[k][:],
                        self.bf(oZS + c * N + kt * 128, 128),
                        self.bf(oZQ + c * 512, 512),
                        start=(c == 0), stop=(c == C - 1)),
                        waits=(w if c == 0 else ()), inc="T" if c == C - 1 else None)
                off = k0 - q0
                if off < 0:
                    ev_a = P.emit("scalar", lambda e, kt=kt, k=bank: e.activation(
                        self.bf(oAST + kt * 512, 512), self.ps[k][:], AF.Sigmoid,
                        scale=0.125), waits=(ev_mm,), inc="S")
                else:
                    oi = off // 128
                    sb_ = kt % 2
                    ws = (ev_mm,)
                    if ev_mask[sb_] is not None:
                        ws = ws + (ev_mask[sb_],)
                    ev_sg = P.emit("scalar", lambda e, k=bank, sb_=sb_: e.activation(
                        self.bf(oSG + sb_ * 512, 512), self.ps[k][:], AF.Sigmoid, scale=0.125),
                        waits=ws, inc="S")
                    ev_a = P.emit("vector", lambda e, kt=kt, oi=oi, sb_=sb_: e.tensor_tensor(
                        self.bf(oAST + kt * 512, 512), self.bf(oSG + sb_ * 512, 512),
                        MSK[:, oi * 512:(oi + 1) * 512], ALU.mult),
                        waits=(ev_sg,), inc="V")
                    ev_mask[sb_] = ev_a
                self.free_bank(bank, ev_a)
            ev_attn_all = P.now("S", "V")
            for eg in range(2):
                for e4 in range(4):
                    ec = eg * 4 + e4
                    bank = 2 + e4
                    w = self.use_bank(bank, ev_attn_all + (ev_vt_all,))
                    ev_mm = None
                    for kt in range(nkt):
                        ev_mm = P.emit("tensor", lambda e, kt=kt, ec=ec, k=bank, nkt=nkt: e.matmul(
                            self.ps[k][:],
                            self.bf(oV + kt * 1024 + ec * 128, 128),
                            self.bf(oAST + kt * 512, 512),
                            start=(kt == 0), stop=(kt == nkt - 1)),
                            waits=(w if kt == 0 else ()), inc="T" if kt == nkt - 1 else None)
                    ev_add = P.emit("vector", lambda e, ec=ec, k=bank, q0=q0: e.tensor_tensor(
                        XT[:, ec * N + q0: ec * N + q0 + 512],
                        XT[:, ec * N + q0: ec * N + q0 + 512], self.ps[k][:], ALU.add),
                        waits=(ev_mm,), inc="V")
                    self.free_bank(bank, ev_add)

    def xn_snapshot(self, oXN, routers=()):
        P = self.P
        XT, M, R, S1, S2 = (self.sb[k] for k in ("XT", "M", "R", "S1", "S2"))
        IDF = self.sb["IDF"]
        S2h = S2[:].bitcast(F16)
        ev = None
        ev_rt = None
        rbw = {}
        for (whi, wlo, whih, k, banks4, dst) in routers:
            for b in banks4:
                rbw[b] = self.use_bank(b, ())
        for c in range(C):
            w = (self.ev_lnr, self.ev_lnm)
            if routers and c >= 1:
                w = w + (ev_rt,)
            P.emit("vector", lambda e, c=c: e.tensor_tensor(
                S1[:], XT[:, c * N:(c + 1) * N], M[:], ALU.subtract), waits=w)
            P.emit("vector", lambda e: e.drain())
            P.emit("vector", lambda e, c=c: e.tensor_tensor(S1[:], S1[:], R[:], ALU.mult))
            P.emit("vector", lambda e: e.drain())
            ev = P.emit("vector", lambda e, c=c, oXN=oXN: e.tensor_copy(
                self.bf(oXN + c * N, N), S1[:]), inc="V")
            if routers:
                P.emit("vector", lambda e: e.drain())
                ev_lo = P.emit("vector", lambda e, c=c, oXN=oXN: e.tensor_tensor(
                    S2h[:, 0:N], S1[:], self.bf(oXN + c * N, N), ALU.subtract), inc="V")
                for (whi, wlo, whih, k, banks4, dst) in routers:
                    for nb in range(4):
                        bank = banks4[nb]
                        terms = ((whi, True), (wlo, True), (whih, False))
                        nterm = len(terms)
                        for ti, (wp, is_hi) in enumerate(terms):
                            first = (c == 0 and ti == 0)
                            last = (c == C - 1 and ti == nterm - 1)
                            wmm = (rbw[bank] if first else ()) + ((ev,) if is_hi else (ev_lo,))
                            def mk(wp=wp, is_hi=is_hi, c=c, nb=nb, bank=bank, k=k, first=first, last=last, oXN=oXN):
                                def f(e):
                                    rhs = (self.bf(oXN + c * N + nb * 512, 512) if is_hi
                                           else S2h[:, nb * 512:(nb + 1) * 512])
                                    return e.matmul(self.ps[bank][0:k, :], wp[:, c * k:(c + 1) * k],
                                                    rhs, start=first, stop=last)
                                return f
                            ev_rt = P.emit("tensor", mk(),
                                           waits=wmm, inc="T" if (nb == 3 and ti == nterm - 1) else None)
        for (whi, wlo, whih, k, banks4, dst) in routers:
            ev_ev = None
            for nb in range(4):
                ev_ev = P.emit("vector", lambda e, nb=nb, k=k, bank=banks4[nb]: e.tensor_copy(
                    S1[0:k, nb * 512:(nb + 1) * 512], self.ps[bank][0:k, :]),
                    waits=(ev_rt,), inc="V")
                self.free_bank(banks4[nb], ev_ev)
            for t in range(NT):
                tb = (t % 2)
                tw = self.use_bank(tb, (ev_ev,))
                ev_tp = P.emit("tensor", lambda e, t=t, k=k, tb=tb: e.transpose(
                    self.ps[tb][:, 0:k], S1[0:k, t * 128:(t + 1) * 128], IDF[0:k, 0:k]),
                    waits=tw, inc="T")
                ev_rt = P.emit("vector", lambda e, t=t, k=k, tb=tb, dst=dst: e.tensor_copy(
                    dst[:, t * k:(t + 1) * k], self.ps[tb][:, 0:k]),
                    waits=(ev_tp,), inc="V")
                self.free_bank(tb, ev_rt)
        return ev, ev_rt

    def top2_gates(self, ev_logits):
        P = self.P
        NEG = -1e30
        GROUT, G0B = self.sb["GROUT"], self.sb["G0B"]
        RA, RB, RC = self.sb["RA"], self.sb["RB"], self.sb["RC"]
        RV1, RV2, RV3 = self.sb["RV1"], self.sb["RV2"], self.sb["RV3"]
        def g3(ap):
            return ap[:].rearrange("p (t k) -> p t k", k=8)
        def b3(ap):
            return ap[:].rearrange("p (t o) -> p t o", o=1).broadcast_to([128, 16, 8])
        P.emit("vector", lambda e: e.tensor_reduce(RV1[:], g3(GROUT), axis=AX.X, op=ALU.max),
               waits=(ev_logits,))
        P.emit("vector", lambda e: e.drain())
        P.emit("vector", lambda e: e.tensor_tensor(g3(RA), g3(GROUT), b3(RV1), ALU.is_ge))
        P.emit("vector", lambda e: e.drain())
        P.emit("vector", lambda e: e.scalar_tensor_tensor(g3(RB), g3(RA), NEG, g3(GROUT),
                                                          ALU.mult, ALU.add))
        P.emit("vector", lambda e: e.drain())
        P.emit("vector", lambda e: e.tensor_reduce(RV2[:], g3(RB), axis=AX.X, op=ALU.max))
        P.emit("vector", lambda e: e.drain())
        P.emit("vector", lambda e: e.tensor_tensor(g3(RC), g3(RB), b3(RV2), ALU.is_ge))
        ev_sm = P.emit("vector", lambda e: e.tensor_tensor(RV3[:], RV1[:], RV2[:], ALU.subtract),
                       inc="V")
        ev_w1 = P.emit("scalar", lambda e: e.activation(RV2[:], RV3[:], AF.Sigmoid),
                       waits=(ev_sm,), inc="S")
        P.emit("vector", lambda e: e.tensor_tensor(g3(RB), g3(RA), b3(RV2), ALU.mult),
               waits=(ev_w1,))
        P.emit("vector", lambda e: e.drain())
        P.emit("vector", lambda e: e.tensor_tensor(g3(RA), g3(RC), b3(RV2), ALU.mult))
        P.emit("vector", lambda e: e.drain())
        P.emit("vector", lambda e: e.tensor_tensor(RC[:], RC[:], RA[:], ALU.subtract))
        P.emit("vector", lambda e: e.drain())
        return P.emit("vector", lambda e: e.tensor_tensor(G0B[:], RB[:], RC[:], ALU.add), inc="V")

    def gates_transpose(self, ev_gates):
        P = self.P
        G0B, GTR, IDF = self.sb["G0B"], self.sb["GTR"], self.sb["IDF"]
        ev = None
        for t in range(NT):
            bank = 6 + t % 2
            w = self.use_bank(bank, (ev_gates,))
            ev_tp = P.emit("tensor", lambda e, t=t, k=bank: e.transpose(
                self.ps[k][0:8, 0:128], G0B[:, t * 8:(t + 1) * 8], IDF[:]),
                waits=w, inc="T")
            ev = P.emit("vector", lambda e, t=t, k=bank: e.tensor_copy(
                GTR[0:8, t * 128:(t + 1) * 128], self.ps[k][0:8, 0:128]),
                waits=(ev_tp,), inc="V")
            self.free_bank(bank, ev)
        return ev

    def block_l0_ffn(self, seq):
        P = self.P
        XT, M, R, S1, S2 = (self.sb[k] for k in ("XT", "M", "R", "S1", "S2"))
        RWT, GROUT, GTR, SELS = self.sb["RWT"], self.sb["GROUT"], self.sb["GTR"], self.sb["SELS"]
        oXN, oW1, oHG, oW2 = 0, 16384, 32768, 40960
        S1b = S1[:].bitcast(BF16)
        S2b = S2[:].bitcast(BF16)
        l0w1, l0w2 = self.dram["l0w1"], self.dram["l0w2"]

        ev_xn, ev_rt = self.xn_snapshot(oXN, routers=(
            (RWT[:], self.sb["RWLO"][:], self.sb["RWTH"][:], 8, (4, 5, 6, 7), GROUT[:]),))
        ev_lg = ev_rt
        ev_gates = self.top2_gates(ev_lg)
        ev_gtr = self.gates_transpose(ev_gates)

        ev_w1_free = None
        ev_gb_free = None
        ev_w2buf = [None, None]
        ev_hg_rd = None
        for ex in range(8):
            wv = (ev_w1_free,) if ev_w1_free is not None else P.now("T", "V", "S")
            ev_w1l = None
            for c in range(C):
                ev_w1l = P.emit("gpsimd", lambda e, ex=ex, c=c: e.dma_start(
                    out=self.bf(oW1 + c * 2048, 2048),
                    in_=l0w1[ex, c * 128:(c + 1) * 128, :]), waits=wv if c == 0 else (),
                    inc="LW")
            wg = (ev_gtr,) if ex == 0 else ()
            if ev_gb_free is not None:
                wg = wg + (ev_gb_free,)
            ev_gb = None
            for g in range(4):
                bank = 6 + g % 2
                bwg = self.use_bank(bank, wg if g < 2 else ())
                ev_bc = P.emit("tensor", lambda e, ex=ex, g=g, k=bank: e.matmul(
                    self.ps[k][:], SELS[:, ex * 128:(ex + 1) * 128],
                    GTR[:, g * 512:(g + 1) * 512], start=True, stop=True),
                    waits=bwg, inc="T")
                ev_gb = P.emit("vector", lambda e, g=g, k=bank: e.tensor_copy(
                    S2b[:, g * 512:(g + 1) * 512], self.ps[k][:]),
                    waits=(ev_bc,), inc="V")
                self.free_bank(bank, ev_gb)

            for nb in range(4):
                n0 = nb * 512
                ev_hg = None
                for hc in range(16):
                    bank = 6 + hc % 2
                    w = self.use_bank(bank, ((("LW", P.cnt["LW"]),) if hc == 0 and nb == 0 else ()))
                    ev_h = None
                    for c in range(C):
                        ev_h = P.emit("tensor", lambda e, c=c, hc=hc, k=bank, n0=n0: e.matmul(
                            self.ps[k][:],
                            self.bf(oW1 + c * 2048 + hc * 128, 128),
                            self.bf(oXN + c * N + n0, 512),
                            start=(c == 0), stop=(c == C - 1)),
                            waits=(w if c == 0 else ()), inc="T" if c == C - 1 else None)
                    sl = hc % 2
                    ev_rl = P.emit("scalar", lambda e, k=bank, sl=sl: e.activation(
                        S1b[:, sl * 512:(sl + 1) * 512], self.ps[k][:], AF.Relu),
                        waits=(ev_h,), inc="S")
                    self.free_bank(bank, ev_rl)
                    ev_sq = P.emit("vector", lambda e, sl=sl: e.tensor_tensor(
                        S1b[:, (2 + sl) * 512:(3 + sl) * 512],
                        S1b[:, sl * 512:(sl + 1) * 512],
                        S1b[:, sl * 512:(sl + 1) * 512], ALU.mult),
                        waits=(ev_rl,), inc="V")
                    whg = (ev_sq, ev_gb)
                    if ev_hg_rd is not None:
                        whg = whg + (ev_hg_rd,)
                    ev_hg = P.emit("gpsimd", lambda e, hc=hc, sl=sl, n0=n0: e.tensor_tensor(
                        self.bf(oHG + hc * 512, 512),
                        S1b[:, (2 + sl) * 512:(3 + sl) * 512],
                        S2b[:, n0:n0 + 512],
                        ALU.mult), waits=whg, inc="G")
                ev_y = None
                for g in range(4):
                    buf = g % 2
                    wl = (ev_w2buf[buf],) if ev_w2buf[buf] is not None else P.now("V", "S")
                    ev_l = P.emit("gpsimd", lambda e, ex=ex, g=g, buf=buf: e.dma_start(
                        out=self.bf(oW2 + buf * 4096, 4096).rearrange("p (s d) -> p s d", d=1024),
                        in_=l0w2[ex, g * 512:(g + 1) * 512, :].rearrange("(s p) d -> p s d", p=128)),
                        waits=wl, inc="LW")
                    for si in range(4):
                        hc = g * 4 + si
                        for dc in range(C):
                            wy = ()
                            if si == 0 and dc == 0:
                                wy = (ev_l, ("G", P.cnt["G"]))
                                if g == 0:
                                    wy = self.use_bank(dc, wy)
                            elif si == 0 and g == 0:
                                wy = self.use_bank(dc, ())
                            ev_y = P.emit("tensor", lambda e, buf=buf, si=si, dc=dc, hc=hc: e.matmul(
                                self.ps[dc][:],
                                self.bf(oW2 + buf * 4096 + si * 1024 + dc * 128, 128),
                                self.bf(oHG + hc * 512, 512),
                                start=(hc == 0), stop=(hc == 15)),
                                waits=wy, inc="T" if (si == 3 and dc == C - 1) else None)
                    ev_w2buf[buf] = ev_y
                ev_hg_rd = ev_y
                for dc in range(C):
                    ev_add = P.emit("vector", lambda e, dc=dc, n0=n0: e.tensor_tensor(
                        XT[:, dc * N + n0: dc * N + n0 + 512],
                        XT[:, dc * N + n0: dc * N + n0 + 512], self.ps[dc][:], ALU.add),
                        waits=(ev_y,), inc="V")
                    self.free_bank(dc, ev_add)
            ev_w1_free = ("T", P.cnt["T"])
            ev_gb_free = ("G", P.cnt["G"])

    def xn_half(self, oXN, h):
        P = self.P
        XT, M, R, S2 = (self.sb[k] for k in ("XT", "M", "R", "S2"))
        lo = h * 1024
        ev = None
        for c in range(C):
            w = (self.ev_lnr, self.ev_lnm) + (P.now("T", "G") if c == 0 else ())
            P.emit("vector", lambda e, c=c, lo=lo: e.tensor_tensor(
                S2[:, 0:1024], XT[:, c * N + lo: c * N + lo + 1024],
                M[:, lo:lo + 1024], ALU.subtract), waits=w)
            P.emit("vector", lambda e: e.drain())
            ev = P.emit("vector", lambda e, c=c, oXN=oXN, lo=lo: e.tensor_tensor(
                self.bf(oXN + c * 1024, 1024), S2[:, 0:1024], R[:, lo:lo + 1024],
                ALU.mult), inc="V")
        return ev

    def load_wslot(self, oW, wname):
        P = self.P
        wt = self.dram[wname]
        wv = P.now("T", "V", "S", "G")
        ev = None
        for c in range(C):
            ev = P.emit("gpsimd", lambda e, c=c, oW=oW, wt=wt: e.dma_start(
                out=self.bf(oW + c * 1024, 1024), in_=wt[c * 128:(c + 1) * 128, :]),
                waits=wv if c == 0 else (), inc="LW")
        return ev

    def proj_half(self, oW, oXN, dst, h, bias_col, ev_w, ev_xn, to_token_major=False):
        P = self.P
        BQKV, IDB = self.sb["BQKV"], self.sb["IDB"]
        S1b = self.sb["S1"][:].bitcast(BF16)
        ev_out = None
        for eo in range(C):
            for blk in range(2):
                bank = (eo * 2 + blk) % 4
                w = self.use_bank(bank, (ev_w, ev_xn))
                ev_mm = None
                for c in range(C):
                    ev_mm = P.emit("tensor", lambda e, c=c, eo=eo, blk=blk, k=bank, oW=oW, oXN=oXN: e.matmul(
                        self.ps[k][:],
                        self.bf(oW + c * 1024 + eo * 128, 128),
                        self.bf(oXN + c * 1024 + blk * 512, 512),
                        start=(c == 0), stop=(c == C - 1)),
                        waits=(w if c == 0 else ()), inc="T" if c == C - 1 else None)
                if not to_token_major:
                    ev_out = P.emit("scalar", lambda e, eo=eo, blk=blk, k=bank, dst=dst, bias_col=bias_col: e.activation(
                        self.bf(dst + eo * 1024 + blk * 512, 512), self.ps[k][:], AF.Identity,
                        bias=BQKV[:, bias_col * C + eo: bias_col * C + eo + 1]),
                        waits=(ev_mm,), inc="S")
                    self.free_bank(bank, ev_out)
                else:
                    ev_st = P.emit("scalar", lambda e, k=bank, bias_col=bias_col, eo=eo: e.activation(
                        S1b[:, 0:512], self.ps[k][:], AF.Identity,
                        bias=BQKV[:, bias_col * C + eo: bias_col * C + eo + 1]),
                        waits=(ev_mm,), inc="S")
                    self.free_bank(bank, ev_st)
                    for tt in range(4):
                        tb = 4 + (tt % 2)
                        tw = self.use_bank(tb, (ev_st,))
                        ev_tp = P.emit("tensor", lambda e, tt=tt, k=tb: e.transpose(
                            self.ps[k][:, 0:64].bitcast(BF16),
                            S1b[:, tt * 128:(tt + 1) * 128], IDB[:]), waits=tw, inc="T")
                        lt = blk * 4 + tt
                        ev_out = P.emit("vector", lambda e, lt=lt, eo=eo, k=tb, dst=dst: e.tensor_copy(
                            self.bf(dst + lt * 1024 + eo * 128, 128),
                            self.ps[k][:, 0:64].bitcast(BF16)), waits=(ev_tp,), inc="V")
                        self.free_bank(tb, ev_out)
        return ev_out

    def block_l1_attn(self, seq):
        P = self.P
        XT, M, R, S1, S2 = (self.sb[k] for k in ("XT", "M", "R", "S1", "S2"))
        BQKV, IDB, RV1, RV2, RV3 = (self.sb[k] for k in ("BQKV", "IDB", "RV1", "RV2", "RV3"))
        oK, oQ, oV, oW, oXN3 = 0, 16384, 24576, 32768, 40960
        S1b = S1[:].bitcast(BF16)
        SCALE = 1.0 / 32.0

        for kh in range(2):
            ev_xn = self.xn_half(oXN3, kh)
            ev_w = self.load_wslot(oW, "wk")
            for eo in range(C):
                for blk in range(2):
                    bank = (eo * 2 + blk) % 4
                    w = self.use_bank(bank, (ev_w, ev_xn))
                    ev_mm = None
                    for c in range(C):
                        ev_mm = P.emit("tensor", lambda e, c=c, eo=eo, blk=blk, k=bank: e.matmul(
                            self.ps[k][:],
                            self.bf(oW + c * 1024 + eo * 128, 128),
                            self.bf(oXN3 + c * 1024 + blk * 512, 512),
                            start=(c == 0), stop=(c == C - 1)),
                            waits=(w if c == 0 else ()), inc="T" if c == C - 1 else None)
                    ev_k = P.emit("scalar", lambda e, eo=eo, blk=blk, k=bank, kh=kh: e.activation(
                        self.bf(oK + eo * 2048 + kh * 1024 + blk * 512, 512),
                        self.ps[k][:], AF.Identity,
                        bias=BQKV[:, C + eo: C + eo + 1]), waits=(ev_mm,), inc="S")
                    self.free_bank(bank, ev_k)
        ev_k_all = P.now("S")

        for h in range(2):
            ev_xn = self.xn_half(oXN3, h)
            ev_w = self.load_wslot(oW, "wq")
            self.proj_half(oW, oXN3, oQ, h, 0, ev_w, ev_xn)
            ev_w = self.load_wslot(oW, "wv")
            self.proj_half(oW, oXN3, oV, h, 2, ev_w, ev_xn, to_token_major=True)
            ev_w = self.load_wslot(oW, "wo")
            ev_qv = P.now("S", "V")
            for nl in range(4):
                ng = h * 4 + nl
                q0 = nl * 256
                for mg in range(2):
                    for qt in range(2):
                        for mi in range(4):
                            m = mg * 4 + mi
                            bank = qt * 2 + mi // 2
                            boff = (mi % 2) * 256
                            w = self.use_bank(bank, ev_k_all + ev_qv) if mi % 2 == 0 else ()
                            ev_mm = None
                            for c in range(C):
                                ev_mm = P.emit("tensor", lambda e, c=c, qt=qt, m=m, k=bank, boff=boff, q0=q0: e.matmul(
                                    self.ps[k][:, boff:boff + 256],
                                    self.bf(oQ + c * 1024 + q0 + qt * 128, 128),
                                    self.bf(oK + c * 2048 + m * 256, 256),
                                    start=(c == 0), stop=(c == C - 1)),
                                    waits=(w if c == 0 else ()), inc="T" if c == C - 1 else None)
                    ev_sc = ("T", P.cnt["T"])
                    for qt in range(2):
                        pv0, pv1 = self.ps[qt * 2], self.ps[qt * 2 + 1]
                        P.emit("vector", lambda e, pv0=pv0: e.tensor_reduce(
                            RV1[:, 0:2], pv0[:].rearrange("p (g k) -> p g k", k=256),
                            axis=AX.X, op=ALU.max), waits=(ev_sc,))
                        P.emit("vector", lambda e, pv1=pv1: e.tensor_reduce(
                            RV1[:, 2:4], pv1[:].rearrange("p (g k) -> p g k", k=256),
                            axis=AX.X, op=ALU.max))
                        P.emit("vector", lambda e: e.drain())
                        def mk_sub(pv, rcols, dcol):
                            def f(e, pv=pv, rcols=rcols, dcol=dcol):
                                return e.tensor_tensor(
                                    S1[:, dcol:dcol + 512].rearrange("p (g k) -> p g k", k=256),
                                    pv[:].rearrange("p (g k) -> p g k", k=256),
                                    RV1[:, rcols:rcols + 2].rearrange("p (g o) -> p g o", o=1)
                                        .broadcast_to([128, 2, 256]), ALU.subtract)
                            return f
                        P.emit("vector", mk_sub(pv0, 0, 0))
                        ev_d = P.emit("vector", mk_sub(pv1, 2, 512), inc="V")
                        ev_e = P.emit("scalar", lambda e: e.activation(
                            S1[:, 0:1024], S1[:, 0:1024], AF.Exp, scale=SCALE),
                            waits=(ev_d,), inc="S")
                        P.emit("vector", lambda e: e.tensor_reduce(
                            RV2[:, 0:4], S1[:, 0:1024].rearrange("p (g k) -> p g k", k=256),
                            axis=AX.X, op=ALU.add), waits=(ev_e,))
                        P.emit("vector", lambda e: e.drain())
                        P.emit("vector", lambda e: e.reciprocal(RV3[:, 0:4], RV2[:, 0:4]))
                        P.emit("vector", lambda e: e.drain())
                        P.emit("vector", lambda e: e.tensor_tensor(
                            S1[:, 0:1024].rearrange("p (g k) -> p g k", k=256),
                            S1[:, 0:1024].rearrange("p (g k) -> p g k", k=256),
                            RV3[:, 0:4].rearrange("p (g o) -> p g o", o=1)
                                .broadcast_to([128, 4, 256]), ALU.mult))
                        P.emit("vector", lambda e: e.drain())
                        P.emit("vector", lambda e: e.tensor_tensor(
                            S1[:, 1024:1280], S1[:, 0:256], S1[:, 256:512], ALU.add))
                        P.emit("vector", lambda e: e.tensor_tensor(
                            S1[:, 1280:1536], S1[:, 512:768], S1[:, 768:1024], ALU.add))
                        P.emit("vector", lambda e: e.drain())
                        P.emit("vector", lambda e: e.tensor_tensor(
                            S1[:, 1024:1280], S1[:, 1024:1280], S1[:, 1280:1536], ALU.add))
                        P.emit("vector", lambda e: e.drain())
                        if mg * 4 <= ng < mg * 4 + 4:
                            mi = ng - mg * 4
                            P.emit("vector", lambda e, mi=mi: e.tensor_tensor(
                                S1[:, 1024:1280], S1[:, 1024:1280],
                                S1[:, mi * 256:(mi + 1) * 256], ALU.add))
                            P.emit("vector", lambda e: e.drain())
                        if mg == 0:
                            ev_g = P.emit("vector", lambda e, qt=qt: e.tensor_copy(
                                S2[:, qt * 256:(qt + 1) * 256], S1[:, 1024:1280]), inc="V")
                        else:
                            ev_g = P.emit("vector", lambda e, qt=qt: e.tensor_tensor(
                                S2[:, qt * 256:(qt + 1) * 256],
                                S2[:, qt * 256:(qt + 1) * 256], S1[:, 1024:1280], ALU.add),
                                inc="V")
                        self.free_bank(qt * 2, ev_d)
                        self.free_bank(qt * 2 + 1, ev_d)
                ev_g_all = ("V", P.cnt["V"])
                P.emit("vector", lambda e: e.tensor_copy(S1b[:, 0:512], S2[:, 0:512]),
                       waits=(ev_g_all,))
                ev_gb = P.emit("vector", lambda e: e.drain(), inc="V")
                for qt in range(2):
                    for kb in range(2):
                        tb = 4 + (qt * 2 + kb) % 2
                        tw = self.use_bank(tb, (ev_gb,))
                        ev_tp = P.emit("tensor", lambda e, qt=qt, kb=kb, k=tb: e.transpose(
                            self.ps[k][:, 0:64].bitcast(BF16),
                            S1b[:, qt * 256 + kb * 128: qt * 256 + (kb + 1) * 128], IDB[:]),
                            waits=tw, inc="T")
                        ev_gt = P.emit("vector", lambda e, qt=qt, kb=kb, k=tb: e.tensor_copy(
                            S1b[:, 512 + kb * 256 + qt * 128: 512 + kb * 256 + (qt + 1) * 128],
                            self.ps[k][:, 0:64].bitcast(BF16)), waits=(ev_tp,), inc="V")
                        self.free_bank(tb, ev_gt)
                ev_gt_all = ("V", P.cnt["V"])
                for ec in range(C):
                    bank = 4 + ec // 2
                    boff = (ec % 2) * 256
                    w = self.use_bank(bank, (ev_gt_all,)) if ec % 2 == 0 else ()
                    ev_mm = None
                    for kt in range(2):
                        lt = (ng - h * 4) * 2 + kt
                        ev_mm = P.emit("tensor", lambda e, lt=lt, ec=ec, kt=kt, k=bank, boff=boff: e.matmul(
                            self.ps[k][:, boff:boff + 256],
                            self.bf(oV + lt * 1024 + ec * 128, 128),
                            S1b[:, 512 + kt * 256: 512 + (kt + 1) * 256],
                            start=(kt == 0), stop=(kt == 1)),
                            waits=(w if kt == 0 else ()), inc="T" if kt == 1 else None)
                    if ec % 2 == 1:
                        ev_av = P.emit("vector", lambda e, ec=ec, k=bank: e.tensor_copy(
                            S1b[:, 1536 + (ec - 1) * 256: 1536 + (ec + 1) * 256],
                            self.ps[k][:]), waits=(ev_mm,), inc="V")
                        self.free_bank(bank, ev_av)
                ev_av_all = ("V", P.cnt["V"])
                gq = h * 1024 + nl * 256
                for eo in range(C):
                    bank = eo // 2
                    boff = (eo % 2) * 256
                    w = self.use_bank(bank, (ev_av_all, ("LW", P.cnt["LW"]))) if eo % 2 == 0 else ()
                    ev_mm = None
                    for ec in range(C):
                        ev_mm = P.emit("tensor", lambda e, ec=ec, eo=eo, k=bank, boff=boff: e.matmul(
                            self.ps[k][:, boff:boff + 256],
                            self.bf(oW + ec * 1024 + eo * 128, 128),
                            S1b[:, 1536 + ec * 256: 1536 + (ec + 1) * 256],
                            start=(ec == 0), stop=(ec == C - 1)),
                            waits=(w if ec == 0 else ()), inc="T" if ec == C - 1 else None)
                    if eo % 2 == 1:
                        for eo2 in (eo - 1, eo):
                            bo2 = (eo2 % 2) * 256
                            ev_o = P.emit("vector", lambda e, eo2=eo2, k=bank, bo2=bo2, gq=gq: e.scalar_tensor_tensor(
                                XT[:, eo2 * N + gq: eo2 * N + gq + 256],
                                self.ps[k][:, bo2:bo2 + 256],
                                BQKV[:, 3 * C + eo2: 3 * C + eo2 + 1],
                                XT[:, eo2 * N + gq: eo2 * N + gq + 256],
                                ALU.add, ALU.add), waits=(ev_mm,), inc="V")
                        self.free_bank(bank, ev_o)

    def block_l1_ffn(self, seq):
        P = self.P
        XT, M, R, S1, S2 = (self.sb[k] for k in ("XT", "M", "R", "S1", "S2"))
        GROUT, G0B, GTR, SELS = self.sb["GROUT"], self.sb["G0B"], self.sb["GTR"], self.sb["SELS"]
        RA, RB, RC = self.sb["RA"], self.sb["RB"], self.sb["RC"]
        RV1, RV2, RV3 = self.sb["RV1"], self.sb["RV2"], self.sb["RV3"]
        WGT, WET, GBB, EBB = self.sb["WGT"], self.sb["WET"], self.sb["GBB"], self.sb["EBB"]
        S1b = S1[:].bitcast(BF16)
        S2b = S2[:].bitcast(BF16)
        oHR, oXN4 = 0, 32768
        NEG = -1e30
        w1t, w2t, w_int, w_outt = (self.dram[k] for k in ("w1_t", "w2_t", "w_in_t", "w_out_t"))

        ev_xn, ev_rt = self.xn_snapshot(oXN4, routers=(
            (WGT[:], self.sb["WGLO"][:], self.sb["WGTH"][:], 2, (0, 1, 2, 3), RB[:, 0:32]),
            (WET[:], self.sb["WELO"][:], self.sb["WETH"][:], 8, (4, 5, 6, 7), GROUT[:])))
        ev_gl = ev_el = ev_rt
        def g2(ap):
            return ap.rearrange("p (t k) -> p t k", k=2)
        def g8(ap):
            return ap.rearrange("p (t k) -> p t k", k=8)
        def b16(ap, k):
            return ap.rearrange("p (t o) -> p t o", o=1).broadcast_to([128, 16, k])
        ev1 = P.emit("vector", lambda e: e.tensor_tensor(
            g2(RA[:, 0:32]), g2(RB[:, 0:32]),
            GBB[:].rearrange("p (o k) -> p o k", o=1).broadcast_to([128, 16, 2]), ALU.add),
            waits=(ev_gl,), inc="V")
        ev2 = P.emit("vector", lambda e: e.tensor_tensor(
            g8(GROUT[:]), g8(GROUT[:]),
            EBB[:].rearrange("p (o k) -> p o k", o=1).broadcast_to([128, 16, 8]), ALU.add),
            waits=(ev_el,), inc="V")
        P.emit("vector", lambda e: e.drain())
        GL = RA[:, 0:32]
        GEQ = RB[:, 0:32]
        P.emit("vector", lambda e: e.drain())
        P.emit("vector", lambda e: e.tensor_reduce(RV1[:], g2(GL), axis=AX.X, op=ALU.max))
        P.emit("vector", lambda e: e.drain())
        P.emit("vector", lambda e: e.tensor_tensor(g2(GEQ), g2(GL), b16(RV1[:], 2), ALU.is_ge))
        P.emit("vector", lambda e: e.tensor_reduce(RV2[:], g2(GL), axis=AX.X, op=ALU.add))
        P.emit("vector", lambda e: e.drain())
        P.emit("vector", lambda e: e.tensor_tensor(RV3[:], RV1[:], RV2[:], ALU.subtract))
        P.emit("vector", lambda e: e.drain())
        ev_sm = P.emit("vector", lambda e: e.tensor_tensor(RV3[:], RV3[:], RV1[:], ALU.add),
                       inc="V")
        ev_gp = P.emit("scalar", lambda e: e.activation(RV3[:], RV3[:], AF.Sigmoid),
                       waits=(ev_sm,), inc="S")
        geq_b = GEQ.rearrange("p (t g o) -> p t g o", g=2, o=1).broadcast_to([128, 16, 2, 4])
        P.emit("vector", lambda e: e.tensor_scalar(
            RC[:].rearrange("p (t g k) -> p t g k", g=2, k=4), geq_b, 1e30, -1e30,
            ALU.mult, ALU.add))
        P.emit("vector", lambda e: e.drain())
        P.emit("vector", lambda e: e.tensor_tensor(RC[:], GROUT[:], RC[:], ALU.add))
        P.emit("vector", lambda e: e.drain())
        P.emit("vector", lambda e: e.tensor_reduce(RV1[:], g8(RC[:]), axis=AX.X, op=ALU.max))
        P.emit("vector", lambda e: e.drain())
        ev_d = P.emit("vector", lambda e: e.tensor_tensor(
            g8(G0B[:]), g8(RC[:]), b16(RV1[:], 8), ALU.subtract), inc="V")
        ev_e = P.emit("scalar", lambda e: e.activation(G0B[:], G0B[:], AF.Exp),
                      waits=(ev_d,), inc="S")
        P.emit("vector", lambda e: e.tensor_reduce(RV1[:], g8(G0B[:]), axis=AX.X, op=ALU.add),
               waits=(ev_e,))
        P.emit("vector", lambda e: e.drain())
        P.emit("vector", lambda e: e.reciprocal(RV2[:], RV1[:]))
        P.emit("vector", lambda e: e.drain())
        P.emit("vector", lambda e: e.tensor_tensor(g8(G0B[:]), g8(G0B[:]), b16(RV2[:], 8), ALU.mult),
               waits=(ev_gp,))
        P.emit("vector", lambda e: e.drain())
        P.emit("vector", lambda e: e.tensor_tensor(g8(G0B[:]), g8(G0B[:]), b16(RV3[:], 8), ALU.mult))
        P.emit("vector", lambda e: e.drain())
        P.emit("vector", lambda e: e.tensor_reduce(RV1[:], g8(G0B[:]), axis=AX.X, op=ALU.max))
        P.emit("vector", lambda e: e.drain())
        P.emit("vector", lambda e: e.tensor_tensor(g8(RB[:]), g8(G0B[:]), b16(RV1[:], 8), ALU.is_ge))
        P.emit("vector", lambda e: e.drain())
        P.emit("vector", lambda e: e.scalar_tensor_tensor(g8(RC[:]), g8(RB[:]), NEG, g8(G0B[:]),
                                                          ALU.mult, ALU.add))
        P.emit("vector", lambda e: e.drain())
        P.emit("vector", lambda e: e.tensor_reduce(RV2[:], g8(RC[:]), axis=AX.X, op=ALU.max))
        P.emit("vector", lambda e: e.drain())
        P.emit("vector", lambda e: e.tensor_tensor(g8(RA[:]), g8(RC[:]), b16(RV2[:], 8), ALU.is_ge))
        P.emit("vector", lambda e: e.drain())
        P.emit("vector", lambda e: e.tensor_tensor(g8(RB[:]), g8(RB[:]), b16(RV1[:], 8), ALU.mult))
        P.emit("vector", lambda e: e.drain())
        P.emit("vector", lambda e: e.tensor_tensor(g8(RA[:]), g8(RA[:]), b16(RV2[:], 8), ALU.mult))
        P.emit("vector", lambda e: e.drain())
        ev_gates = P.emit("vector", lambda e: e.tensor_tensor(G0B[:], RB[:], RA[:], ALU.add),
                          inc="V")
        ev_gtr = self.gates_transpose(ev_gates)

        ev_stripe = [None, None]
        ev_hr = None
        for hc in range(16):
            buf = hc % 2
            wl = (ev_stripe[buf],) if ev_stripe[buf] is not None else (ev_xn, ("V", P.cnt["V"]))
            ev_la = P.emit("gpsimd", lambda e, hc=hc, buf=buf: e.dma_start(
                out=S2b[:, buf * 2048: buf * 2048 + 1024].rearrange("p (c k) -> p c k", k=128),
                in_=w1t[:, hc * 128:(hc + 1) * 128].rearrange("(c p) k -> p c k", p=128)),
                waits=wl, inc="LW")
            ev_lb = P.emit("gpsimd", lambda e, hc=hc, buf=buf: e.dma_start(
                out=S2b[:, buf * 2048 + 1024: buf * 2048 + 2048].rearrange("p (c k) -> p c k", k=128),
                in_=w1t[:, 2048 + hc * 128: 2048 + (hc + 1) * 128].rearrange("(c p) k -> p c k", p=128)),
                inc="LW")
            for nb in range(4):
                n0 = nb * 512
                bka, bkb = 6, 7
                wa = self.use_bank(bka, (ev_lb,) if nb == 0 else ())
                wb = self.use_bank(bkb, ())
                ev_ma = ev_mb = None
                for c in range(C):
                    ev_ma = P.emit("tensor", lambda e, c=c, buf=buf, n0=n0: e.matmul(
                        self.ps[6][:], S2b[:, buf * 2048 + c * 128: buf * 2048 + (c + 1) * 128],
                        self.bf(oXN4 + c * N + n0, 512),
                        start=(c == 0), stop=(c == C - 1)),
                        waits=(wa if c == 0 else ()), inc="T" if c == C - 1 else None)
                for c in range(C):
                    ev_mb = P.emit("tensor", lambda e, c=c, buf=buf, n0=n0: e.matmul(
                        self.ps[7][:], S2b[:, buf * 2048 + 1024 + c * 128: buf * 2048 + 1024 + (c + 1) * 128],
                        self.bf(oXN4 + c * N + n0, 512),
                        start=(c == 0), stop=(c == C - 1)),
                        waits=(wb if c == 0 else ()), inc="T" if c == C - 1 else None)
                sl = nb % 2
                ev_sa = P.emit("scalar", lambda e, sl=sl: e.activation(
                    S1b[:, sl * 512:(sl + 1) * 512], self.ps[6][:], AF.Silu),
                    waits=(ev_ma,), inc="S")
                self.free_bank(6, ev_sa)
                ev_hr = P.emit("vector", lambda e, hc=hc, n0=n0, sl=sl: e.tensor_tensor(
                    self.bf(oHR + hc * N + n0, 512),
                    S1b[:, sl * 512:(sl + 1) * 512], self.ps[7][:], ALU.mult),
                    waits=(ev_sa, ev_mb), inc="V")
                self.free_bank(7, ev_hr)
            ev_stripe[buf] = ev_hr and ("T", P.cnt["T"])
        ev_hr_all = ("V", P.cnt["V"])
        ev_gb_free = None
        ev_stripe = [None, None]
        ev_hr_mm_done = P.now("T")
        for ex in range(8):
            wg = (ev_gtr,) + ev_hr_mm_done if ex == 0 else ()
            if ev_gb_free is not None:
                wg = wg + (ev_gb_free,)
            ev_gb = None
            for g in range(4):
                bank = 6 + g % 2
                bwg = self.use_bank(bank, wg if g < 2 else ())
                ev_bc = P.emit("tensor", lambda e, ex=ex, g=g, k=bank: e.matmul(
                    self.ps[k][:], SELS[:, ex * 128:(ex + 1) * 128],
                    GTR[:, g * 512:(g + 1) * 512], start=True, stop=True),
                    waits=bwg, inc="T")
                ev_gb = P.emit("vector", lambda e, g=g, k=bank: e.tensor_copy(
                    S2b[:, 2048 + g * 512: 2048 + (g + 1) * 512], self.ps[k][:]),
                    waits=(ev_bc,), inc="V")
                self.free_bank(bank, ev_gb)
            for nb in range(4):
                n0 = nb * 512
                ev_y = None
                for hc in range(16):
                    buf = hc % 2
                    wl = (ev_stripe[buf],) if ev_stripe[buf] is not None else ev_hr_mm_done
                    ev_l = P.emit("gpsimd", lambda e, ex=ex, hc=hc, buf=buf: e.dma_start(
                        out=S2b[:, buf * 1024:(buf + 1) * 1024],
                        in_=w2t[ex, hc * 128:(hc + 1) * 128, :]), waits=wl, inc="LW")
                    sl = hc % 2
                    whg = (ev_gb, ("V", P.cnt["V"]))
                    if ev_y is not None:
                        whg = whg + (ev_y,)
                    ev_hg = P.emit("gpsimd", lambda e, hc=hc, n0=n0, sl=sl: e.tensor_tensor(
                        S1b[:, sl * 512:(sl + 1) * 512],
                        self.bf(oHR + hc * N + n0, 512),
                        S2b[:, 2048 + n0: 2048 + n0 + 512], ALU.mult),
                        waits=whg, inc="G")
                    for dc in range(C):
                        wy = ()
                        if dc == 0:
                            wy = (ev_l, ev_hg)
                            if hc == 0:
                                wy = self.use_bank(dc, wy) + self.use_bank((dc + 1) % 8, ())
                                for dcb in range(2, 8):
                                    wy = wy + self.use_bank(dcb, ())
                                self.bank_free = [None] * 8
                        ev_y = P.emit("tensor", lambda e, buf=buf, dc=dc, hc=hc: e.matmul(
                            self.ps[dc][:],
                            S2b[:, buf * 1024 + dc * 128: buf * 1024 + (dc + 1) * 128],
                            S1b[:, (hc % 2) * 512: (hc % 2 + 1) * 512],
                            start=(hc == 0), stop=(hc == 15)),
                            waits=wy, inc="T" if (dc == C - 1) else None)
                    ev_stripe[buf] = ev_y
                for dc in range(C):
                    ev_add = P.emit("vector", lambda e, dc=dc, n0=n0: e.tensor_tensor(
                        XT[:, dc * N + n0: dc * N + n0 + 512],
                        XT[:, dc * N + n0: dc * N + n0 + 512], self.ps[dc][:], ALU.add),
                        waits=(ev_y,), inc="V")
                    self.free_bank(dc, ev_add)
            ev_gb_free = ("G", P.cnt["G"])

        oHST, oWAB, oWO = 0, 8192, 12288
        ev_hr_free = ("T", P.cnt["T"])
        ev_ab = [None, None]
        ev_wo = [None, None]
        for nb in range(4):
            n0 = nb * 512
            ev_hst = None
            for hc in range(16):
                buf = hc % 2
                wl = (ev_ab[buf],) if ev_ab[buf] is not None else ()
                if nb == 0 and hc < 2:
                    wl = wl + (ev_hr_free,)
                ev_la = P.emit("gpsimd", lambda e, hc=hc, buf=buf: e.dma_start(
                    out=self.bf(oWAB + buf * 2048, 1024).rearrange("p (c k) -> p c k", k=128),
                    in_=w_int[:, hc * 128:(hc + 1) * 128].rearrange("(c p) k -> p c k", p=128)),
                    waits=wl, inc="LW")
                ev_lb = P.emit("gpsimd", lambda e, hc=hc, buf=buf: e.dma_start(
                    out=self.bf(oWAB + buf * 2048 + 1024, 1024).rearrange("p (c k) -> p c k", k=128),
                    in_=w_int[:, 2048 + hc * 128: 2048 + (hc + 1) * 128].rearrange("(c p) k -> p c k", p=128)),
                    inc="LW")
                wa = self.use_bank(6, (ev_lb,))
                wb = self.use_bank(7, ())
                ev_ma = ev_mb = None
                for c in range(C):
                    ev_ma = P.emit("tensor", lambda e, c=c, buf=buf, n0=n0: e.matmul(
                        self.ps[6][:], self.bf(oWAB + buf * 2048 + c * 128, 128),
                        self.bf(oXN4 + c * N + n0, 512),
                        start=(c == 0), stop=(c == C - 1)),
                        waits=(wa if c == 0 else ()), inc="T" if c == C - 1 else None)
                for c in range(C):
                    ev_mb = P.emit("tensor", lambda e, c=c, buf=buf, n0=n0: e.matmul(
                        self.ps[7][:], self.bf(oWAB + buf * 2048 + 1024 + c * 128, 128),
                        self.bf(oXN4 + c * N + n0, 512),
                        start=(c == 0), stop=(c == C - 1)),
                        waits=(wb if c == 0 else ()), inc="T" if c == C - 1 else None)
                ev_ab[buf] = ev_mb
                sl = hc % 2
                ev_sa = P.emit("scalar", lambda e, sl=sl: e.activation(
                    S1b[:, sl * 512:(sl + 1) * 512], self.ps[6][:], AF.Silu),
                    waits=(ev_ma,), inc="S")
                self.free_bank(6, ev_sa)
                ev_hst = P.emit("vector", lambda e, hc=hc, sl=sl: e.tensor_tensor(
                    self.bf(oHST + hc * 512, 512),
                    S1b[:, sl * 512:(sl + 1) * 512], self.ps[7][:], ALU.mult),
                    waits=(ev_sa, ev_mb), inc="V")
                self.free_bank(7, ev_hst)
            ev_y = None
            for hc in range(16):
                buf = hc % 2
                wl = (ev_wo[buf],) if ev_wo[buf] is not None else ()
                ev_l = P.emit("gpsimd", lambda e, hc=hc, buf=buf: e.dma_start(
                    out=self.bf(oWO + buf * 1024, 1024),
                    in_=w_outt[hc * 128:(hc + 1) * 128, :]), waits=wl, inc="LW")
                for dc in range(C):
                    wy = ()
                    if dc == 0:
                        wy = (ev_l, ev_hst)
                        if hc == 0:
                            for dcb in range(8):
                                wy = wy + self.use_bank(dcb, ())
                            self.bank_free = [None] * 8
                    ev_y = P.emit("tensor", lambda e, buf=buf, dc=dc, hc=hc: e.matmul(
                        self.ps[dc][:],
                        self.bf(oWO + buf * 1024 + dc * 128, 128),
                        self.bf(oHST + hc * 512, 512),
                        start=(hc == 0), stop=(hc == 15)),
                        waits=wy, inc="T" if dc == C - 1 else None)
                ev_wo[buf] = ev_y
            for dc in range(C):
                ev_add = P.emit("vector", lambda e, dc=dc, n0=n0: e.tensor_tensor(
                    XT[:, dc * N + n0: dc * N + n0 + 512],
                    XT[:, dc * N + n0: dc * N + n0 + 512], self.ps[dc][:], ALU.add),
                    waits=(ev_y,), inc="V")
                self.free_bank(dc, ev_add)


def make_core_inputs(x_core, W, cst):
    d = dict(cst)
    d["xin"] = np.ascontiguousarray(np.asarray(x_core, np.float32).reshape(NSEQ * N, D))
    d["bvq"] = feat_major_vec(W["L0_attn_bv_q"])
    d["bvk"] = feat_major_vec(W["L0_attn_bv_k"])
    d["bvv"] = feat_major_vec(W["L0_attn_bv_v"])
    d["rw_t"] = np.ascontiguousarray(np.asarray(W["L0_ffn_router_w"], np.float32).T)
    d["l0w1"] = np.ascontiguousarray(np.asarray(W["L0_ffn_w1"], np.float32))
    d["l0w2"] = np.ascontiguousarray(np.asarray(W["L0_ffn_w2"], np.float32))
    d["wq"] = np.ascontiguousarray(np.asarray(W["L1_attn_Wq"], np.float32))
    d["wk"] = np.ascontiguousarray(np.asarray(W["L1_attn_Wk"], np.float32))
    d["wv"] = np.ascontiguousarray(np.asarray(W["L1_attn_Wv"], np.float32))
    d["wo"] = np.ascontiguousarray(np.asarray(W["L1_attn_Wo"], np.float32))
    d["bq"] = feat_major_vec(W["L1_attn_bq"])
    d["bk"] = feat_major_vec(W["L1_attn_bk"])
    d["bv"] = feat_major_vec(W["L1_attn_bv"])
    d["bo"] = feat_major_vec(W["L1_attn_bo"])
    d["w_in_t"] = np.ascontiguousarray(np.asarray(W["L1_ffn_w_in"], np.float32).T)
    d["w_out_t"] = np.ascontiguousarray(np.asarray(W["L1_ffn_w_out"], np.float32).T)
    d["w1_t"] = np.ascontiguousarray(np.asarray(W["L1_ffn_w1"], np.float32).T)
    d["w2_t"] = np.ascontiguousarray(np.asarray(W["L1_ffn_w2"], np.float32).transpose(0, 2, 1))
    d["wg_t"] = np.ascontiguousarray(np.asarray(W["L1_ffn_wg"], np.float32).T)
    d["we_t"] = np.ascontiguousarray(np.asarray(W["L1_ffn_we"], np.float32).T)
    d["gb_b"] = np.tile(np.asarray(W["L1_ffn_gb"], np.float32)[None, :], (128, 1))
    d["eb_b"] = np.tile(np.asarray(W["L1_ffn_eb"], np.float32)[None, :], (128, 1))
    d["ones_f32"] = cst["ones_f32"]
    return d



def kernel(**inputs):
    x = np.asarray(inputs["L0_attn_x"], np.float32)
    cst = host_constants()
    b = Builder()
    in_maps = [make_core_inputs(x[2 * i:2 * i + 2], inputs, cst) for i in range(8)]
    res = run_bass_kernel_spmd(b.nc, in_maps, core_ids=list(range(8)))
    out = np.concatenate([res.results[i]["yout"].reshape(NSEQ, N, D) for i in range(8)], axis=0)
    return out.astype(np.float32)


# version 07
# speedup vs baseline: 1.1731x; 1.0361x over previous
import numpy as np
import ml_dtypes
import concourse.bass as bass
import concourse.mybir as mybir
from concourse.bass_utils import run_bass_kernel_spmd
from contextlib import ExitStack

F32 = mybir.dt.float32
F16 = mybir.dt.float16
BF16 = mybir.dt.bfloat16
AF = mybir.ActivationFunctionType
ALU = mybir.AluOpType
AX = mybir.AxisListType
BF = ml_dtypes.bfloat16

N = 2048
D = 1024
C = 8
NT = 16
NSEQ = 2
EPS = 1e-5

ENGINES = ("tensor", "vector", "scalar", "gpsimd", "sync")
COMPUTE_SEM_OWNER = {"T": "tensor", "V": "vector", "S": "scalar", "G": "gpsimd"}
DMA_SEMS = ("LW", "LS", "ST", "X0", "X1", "X2", "X3", "X4", "X5", "X6", "X7", "X8", "X9")
ALL_SEMS = tuple(COMPUTE_SEM_OWNER) + DMA_SEMS


class Plan:
    def __init__(self):
        self.steps = {e: [] for e in ENGINES}
        self.cnt = {s: 0 for s in ALL_SEMS}

    def emit(self, eng, fn, waits=(), inc=None):
        waits = tuple(w for w in waits if w is not None)
        ev = None
        amt = 0
        if inc is not None:
            amt = 16 if inc in DMA_SEMS else 1
            self.cnt[inc] += amt
            ev = (inc, self.cnt[inc])
        self.steps[eng].append((waits, fn, inc, amt))
        return ev

    def now(self, *sems):
        return tuple((s, self.cnt[s]) for s in sems)

    def realize(self, nc, sems, go):
        hw = {e: {s: 0 for s in ALL_SEMS} for e in ENGINES}
        with nc.Block() as block:
            for ename in ENGINES:
                def body(eng, ename=ename):
                    if ename == "sync":
                        eng.sem_clear(go)
                        for s in ALL_SEMS:
                            eng.sem_clear(sems[s])
                        eng.nop().then_inc(go, 16)
                    else:
                        eng.wait_ge(go, 16)
                    marks = hw[ename]
                    for waits, fn, inc, amt in self.steps[ename]:
                        for (s, v) in waits:
                            if COMPUTE_SEM_OWNER.get(s) == ename:
                                continue
                            if marks[s] < v:
                                eng.wait_ge(sems[s], v)
                                marks[s] = v
                        ret = fn(eng)
                        if inc is not None:
                            inst = ret[-1] if isinstance(ret, (list, tuple)) else ret
                            inst.then_inc(sems[inc], amt)
                getattr(block, ename)(body)



def np_ln(x):
    m = x.mean(-1, keepdims=True)
    v = x.var(-1, keepdims=True)
    return (x - m) / np.sqrt(v + EPS)

def np_softmax(x, axis=-1):
    m = x.max(axis=axis, keepdims=True)
    e = np.exp(x - m)
    return e / e.sum(axis=axis, keepdims=True)

def np_l0_attn(x, bvq, bvk, bvv):
    bq, bk, bv = np.sign(bvq), np.sign(bvk), np.sign(bvv)
    Q = np.sign(x * bq); K = np.sign(x * bk); V = x * bv
    s = (Q @ K.T) * (D ** -0.5)
    n = x.shape[0]
    causal = np.tril(np.ones((n, n), bool))
    a = 1.0 / (1.0 + np.exp(-4.0 * s))
    a = np.where(causal, a, 0.0)
    return (a @ V).astype(np.float32)

def np_l0_ffn(x, router_w, w1, w2):
    logits = x @ router_w.T
    top2 = np.argsort(-logits, axis=1, kind="stable")[:, :2]
    vals = np.take_along_axis(logits, top2, axis=1)
    w = np_softmax(vals, axis=-1)
    G = np.zeros_like(logits)
    np.put_along_axis(G, top2, w, axis=1)
    out = np.zeros_like(x)
    for e in range(8):
        h = np.maximum(x @ w1[e], 0.0) ** 2
        out += (h @ w2[e]) * G[:, e:e + 1]
    return out.astype(np.float32)

def np_l1_attn(x, Wq, bq, Wk, bk, Wv, bv, Wo, bo):
    CH = 256
    n = x.shape[0]; nch = n // CH
    sc = 1.0 / np.sqrt(D)
    Q = (x @ Wq + bq).reshape(nch, CH, D)
    K = (x @ Wk + bk).reshape(nch, CH, D)
    V = (x @ Wv + bv).reshape(nch, CH, D)
    out = np.zeros((nch, CH, D), np.float32)
    for i in range(nch):
        Gs = np.zeros((CH, CH), np.float32)
        for m in range(nch):
            A = np_softmax(Q[i] @ K[m].T * sc, axis=-1)
            Gs += A * (2.0 if m == i else 1.0)
        out[i] = Gs @ V[i]
    return (out.reshape(n, D) @ Wo + bo).astype(np.float32)

def np_l1_ffn(x, w_in, w_out, w1, w2, wg, we, gb, eb):
    def swiglu(h):
        a, b = np.split(h, 2, axis=-1)
        return (a / (1.0 + np.exp(-a))) * b
    shared = swiglu(x @ w_in.T) @ w_out.T
    gl = x @ wg.T + gb
    gprob = np_softmax(gl, axis=-1)
    gidx = gl.argmax(axis=1)
    el = x @ we.T + eb
    egid = np.arange(8) // 4
    masked = np.where(gidx[:, None] == egid[None, :], el, -np.inf)
    eprob = np_softmax(masked, axis=-1)
    pexp = eprob * np.take_along_axis(gprob, gidx[:, None], axis=1)
    top2 = np.argsort(-pexp, axis=1, kind="stable")[:, :2]
    vals = np.take_along_axis(pexp, top2, axis=1)
    Wse = np.zeros_like(el)
    np.put_along_axis(Wse, top2, vals, axis=1)
    h = swiglu(x @ w1.T)
    routed = np.zeros_like(shared)
    for e in range(8):
        routed += (h * Wse[:, e:e + 1]) @ w2[e].T
    return (shared + routed).astype(np.float32)

def np_module(x, W, blocks):
    if "l0a" in blocks:
        x = x + np_l0_attn(np_ln(x), W["L0_attn_bv_q"], W["L0_attn_bv_k"], W["L0_attn_bv_v"])
    if "l0f" in blocks:
        x = x + np_l0_ffn(np_ln(x), W["L0_ffn_router_w"], W["L0_ffn_w1"], W["L0_ffn_w2"])
    if "l1a" in blocks:
        x = x + np_l1_attn(np_ln(x), W["L1_attn_Wq"], W["L1_attn_bq"], W["L1_attn_Wk"], W["L1_attn_bk"],
                           W["L1_attn_Wv"], W["L1_attn_bv"], W["L1_attn_Wo"], W["L1_attn_bo"])
    if "l1f" in blocks:
        x = x + np_l1_ffn(np_ln(x), W["L1_ffn_w_in"], W["L1_ffn_w_out"], W["L1_ffn_w1"], W["L1_ffn_w2"],
                          W["L1_ffn_wg"], W["L1_ffn_we"], W["L1_ffn_gb"], W["L1_ffn_eb"])
    return x



def host_constants():
    cst = {}
    cst["id_bf"] = np.eye(128, dtype=BF)
    cst["id_f32"] = np.eye(128, dtype=np.float32)
    cst["ones128"] = np.ones((128, 128), dtype=BF)
    cst["ones_f32"] = np.ones((128, 128), dtype=np.float32)
    sel = np.zeros((8, 8 * 128), dtype=BF)
    for e in range(8):
        sel[e, e * 128:(e + 1) * 128] = 1
    cst["sel"] = sel
    masks = np.zeros((128, 4 * 512), dtype=BF)
    i = np.arange(128)[:, None]
    j = np.arange(512)[None, :]
    for oi, off in enumerate((0, 128, 256, 384)):
        masks[:, oi * 512:(oi + 1) * 512] = (i + off <= j).astype(BF)
    cst["masks"] = masks
    return cst

def feat_major_vec(v):
    return np.ascontiguousarray(np.asarray(v, np.float32).reshape(C, 128).T)


class Builder:
    def __init__(self, blocks=("l0a", "l0f", "l1a", "l1f"), debug=False):
        self.blocks = blocks
        self.debug = debug
        self.nc = bass.Bass("TRN2", target_bir_lowering=False)
        self.P = Plan()
        self.stack = ExitStack()
        self.dram = {}
        self.sb = {}
        self.ps = []
        self.bank_free = [None] * 8
        self._build()

    def dram_in(self, name, shape, dtype=F32):
        self.dram[name] = self.nc.dram_tensor(name, list(shape), dtype, kind="ExternalInput").ap()
        return self.dram[name]

    def dram_out(self, name, shape, dtype=F32):
        self.dram[name] = self.nc.dram_tensor(name, list(shape), dtype, kind="ExternalOutput").ap()
        return self.dram[name]

    def sbuf(self, name, shape, dtype):
        self.sb[name] = self.stack.enter_context(self.nc.sbuf_tensor(name, list(shape), dtype))
        return self.sb[name]

    def bf(self, lo, w):
        return self.sb["BFA"][:, lo:lo + w]

    def use_bank(self, k, waits=()):
        if self.bank_free[k] is not None:
            waits = tuple(waits) + (self.bank_free[k],)
        return tuple(waits)

    def free_bank(self, k, ev):
        self.bank_free[k] = ev

    def _build(self):
        nc, P = self.nc, self.P
        self.dram_in("xin", [NSEQ * N, D])
        self.dram_out("yout", [NSEQ * N, D])
        self.dram_in("id_bf", [128, 128], BF16)
        self.dram_in("id_f32", [128, 128])
        self.dram_in("ones128", [128, 128], BF16)
        self.dram_in("ones_f32", [128, 128])
        self.dram_in("sel", [8, 8 * 128], BF16)
        self.dram_in("masks", [128, 4 * 512], BF16)
        self.dram_in("bvq", [128, C]); self.dram_in("bvk", [128, C]); self.dram_in("bvv", [128, C])
        self.dram_in("rw_t", [D, 8])
        self.dram_in("l0w1", [8, D, 2048])
        self.dram_in("l0w2", [8, 2048, D])
        for w in ("wq", "wk", "wv", "wo"):
            self.dram_in(w, [D, D])
        for b in ("bq", "bk", "bv", "bo"):
            self.dram_in(b, [128, C])
        self.dram_in("w_in_t", [D, 4096])
        self.dram_in("w_out_t", [2048, D])
        self.dram_in("w1_t", [D, 4096])
        self.dram_in("w2_t", [8, 2048, D])
        self.dram_in("wg_t", [D, 2])
        self.dram_in("we_t", [D, 8])
        self.dram_in("gb_b", [128, 2])
        self.dram_in("eb_b", [128, 8])

        XT = self.sbuf("XT", [128, C * N], F32)
        for nm in ("M", "R", "S1", "S2"):
            self.sbuf(nm, [128, N], F32)
        self.sbuf("IDB", [128, 128], BF16)
        self.sbuf("IDF", [128, 128], F32)
        self.sbuf("ONE", [128, 128], BF16)
        self.sbuf("ONEF", [128, 128], F32)
        self.sbuf("SELS", [8, 8 * 128], BF16)
        self.sbuf("MSK", [128, 4 * 512], BF16)
        self.sbuf("SGN", [128, 4 * C], F32)
        self.sbuf("BQKV", [128, 4 * C], F32)
        self.sbuf("RWT", [128, C * 8], BF16)
        self.sbuf("RWLO", [128, C * 8], BF16)
        self.sbuf("WGLO", [128, C * 2], BF16)
        self.sbuf("WELO", [128, C * 8], BF16)
        self.sbuf("RWTH", [128, C * 8], F16)
        self.sbuf("WGTH", [128, C * 2], F16)
        self.sbuf("WETH", [128, C * 8], F16)
        self.sbuf("WGT", [128, C * 2], BF16)
        self.sbuf("WET", [128, C * 8], BF16)
        self.sbuf("GBB", [128, 2], F32)
        self.sbuf("EBB", [128, 8], F32)
        self.sbuf("GROUT", [128, 128], F32)
        self.sbuf("G0B", [128, 128], F32)
        self.sbuf("GTR", [8, 2048], BF16)
        self.sbuf("RA", [128, 128], F32)
        self.sbuf("RB", [128, 128], F32)
        self.sbuf("RC", [128, 128], F32)
        self.sbuf("RV1", [128, 16], F32)
        self.sbuf("RV2", [128, 16], F32)
        self.sbuf("RV3", [128, 16], F32)
        self.BFW = 49152
        self.sbuf("BFA", [128, self.BFW], BF16)

        for i in range(8):
            self.ps.append(self.stack.enter_context(nc.psum_tensor(f"ps{i}", [128, 512], F32)))

        self.sems = {s: self.stack.enter_context(nc.semaphore(s)) for s in ALL_SEMS}

        def load(dst, src, q="sync"):
            return P.emit(q, lambda e, d=dst, s=src: e.dma_start(out=d, in_=s),
                          inc="LS" if q == "sync" else "LW")
        load(self.sb["IDB"][:], self.dram["id_bf"])
        load(self.sb["IDF"][:], self.dram["id_f32"])
        load(self.sb["ONE"][:], self.dram["ones128"])
        load(self.sb["ONEF"][:], self.dram["ones_f32"])
        load(self.sb["SELS"][:], self.dram["sel"])
        load(self.sb["MSK"][:], self.dram["masks"])
        load(self.sb["BQKV"][:, 0:C], self.dram["bq"])
        load(self.sb["BQKV"][:, C:2 * C], self.dram["bk"])
        load(self.sb["BQKV"][:, 2 * C:3 * C], self.dram["bv"])
        load(self.sb["BQKV"][:, 3 * C:4 * C], self.dram["bo"])
        load(self.sb["GBB"][:], self.dram["gb_b"])
        load(self.sb["EBB"][:], self.dram["eb_b"])
        load(self.sb["RA"][:, 0:C], self.dram["bvq"])
        load(self.sb["RA"][:, C:2 * C], self.dram["bvk"])
        load(self.sb["RA"][:, 2 * C:3 * C], self.dram["bvv"])
        load(self.sb["RWT"][:].rearrange("p (c k) -> p c k", k=8),
             self.dram["rw_t"].rearrange("(c p) k -> p c k", p=128), q="gpsimd")
        load(self.sb["WGT"][:].rearrange("p (c k) -> p c k", k=2),
             self.dram["wg_t"].rearrange("(c p) k -> p c k", p=128), q="gpsimd")
        load(self.sb["WET"][:].rearrange("p (c k) -> p c k", k=8),
             self.dram["we_t"].rearrange("(c p) k -> p c k", p=128), q="gpsimd")
        CST = self.sbuf("CST", [128, 8], F32)
        for i, v in enumerate((EPS, -0.5, 0.125)):
            P.emit("vector", lambda e, v=v, i=i: e.memset(CST[:, i:i + 1], v), inc="V")
            nc.const_aps.register(F32, v, CST[:, i:i + 1])
        load(self.sb["GROUT"][:, 0:64].rearrange("p (c k) -> p c k", k=8),
             self.dram["rw_t"].rearrange("(c p) k -> p c k", p=128))
        load(self.sb["G0B"][:, 0:16].rearrange("p (c k) -> p c k", k=2),
             self.dram["wg_t"].rearrange("(c p) k -> p c k", p=128))
        load(self.sb["RB"][:, 0:64].rearrange("p (c k) -> p c k", k=8),
             self.dram["we_t"].rearrange("(c p) k -> p c k", p=128))
        ev_wf = P.now("LS", "LW")
        P.emit("vector", lambda e: e.tensor_tensor(
            self.sb["RWLO"][:], self.sb["GROUT"][:, 0:64], self.sb["RWT"][:], ALU.subtract),
            waits=ev_wf)
        P.emit("vector", lambda e: e.tensor_tensor(
            self.sb["WGLO"][:], self.sb["G0B"][:, 0:16], self.sb["WGT"][:], ALU.subtract))
        ev_wlo = P.emit("vector", lambda e: e.tensor_tensor(
            self.sb["WELO"][:], self.sb["RB"][:, 0:64], self.sb["WET"][:], ALU.subtract), inc="V")
        P.emit("vector", lambda e: e.tensor_copy(self.sb["RWTH"][:], self.sb["RWT"][:]), waits=ev_wf)
        P.emit("vector", lambda e: e.tensor_copy(self.sb["WGTH"][:], self.sb["WGT"][:]))
        ev_wlo = P.emit("vector", lambda e: e.tensor_copy(self.sb["WETH"][:], self.sb["WET"][:]), inc="V")
        ev_const = P.now("LS", "LW") + (ev_wlo,)
        ev_sgn = P.emit("scalar", lambda e: e.activation(
            self.sb["SGN"][:, 0:3 * C], self.sb["RA"][:, 0:3 * C], AF.Sign),
            waits=ev_const, inc="S")
        self.ev_setup = (P.emit("vector", lambda e: e.tensor_tensor(
            self.sb["SGN"][:, 3 * C:4 * C], self.sb["SGN"][:, 0:C],
            self.sb["SGN"][:, C:2 * C], ALU.mult), waits=(ev_sgn,), inc="V"),) + ev_const

        for seq in range(NSEQ):
            self.build_seq(seq)

        final = P.now("T", "V", "S", "G", "LW", "LS", "ST",
                      "X0", "X1", "X2", "X3", "X4", "X5", "X6", "X7", "X8", "X9")
        def _clear_all(e):
            ret = None
            for s in ALL_SEMS:
                ret = e.sem_clear(self.sems[s])
            return ret
        P.emit("sync", lambda e: _clear_all(e), waits=final)
        self.go_sem = self.stack.enter_context(nc.semaphore("GO"))
        P.realize(nc, self.sems, self.go_sem)
        self.stack.close()

    def build_seq(self, seq):
        self.load_input(seq)
        if "l0a" in self.blocks:
            self.ln_stats()
            self.block_l0_attn(seq)
        if "l0f" in self.blocks:
            self.ln_stats()
            self.block_l0_ffn(seq)
        if "l1a" in self.blocks:
            self.ln_stats()
            self.block_l1_attn(seq)
        if "l1f" in self.blocks:
            self.ln_stats()
            self.block_l1_ffn(seq)
        self.store_output(seq)

    def load_input(self, seq):
        P = self.P
        XT, IDF = self.sb["XT"], self.sb["IDF"]
        xin = self.dram["xin"]
        BFAF = self.sb["BFA"][:].bitcast(F32)
        nstg = 6
        stage = [BFAF[:, i * 1024:(i + 1) * 1024] for i in range(nstg)]
        ev_done = [None] * nstg
        w0 = (P.now("T", "V", "S", "G") + self.ev_setup
              + P.now("X0", "X1", "X2", "X3", "X4", "X5"))
        for t in range(NT):
            buf = stage[t % nstg]
            w = w0 if t < nstg else (ev_done[t % nstg],)
            ev_ld = P.emit("sync", lambda e, b=buf, t=t: e.dma_start(
                out=b, in_=xin[seq * N + t * 128: seq * N + (t + 1) * 128, :]),
                waits=w, inc=f"X{t % nstg}")
            for c in range(C):
                bank = (t * C + c) % 4
                tw = self.use_bank(bank, (ev_ld,))
                ev_tp = P.emit("tensor", lambda e, b=buf, c=c, k=bank: e.transpose(
                    self.ps[k][:, 0:128], b[:, c * 128:(c + 1) * 128], IDF[:]),
                    waits=tw, inc="T")
                ev_ev = P.emit("vector", lambda e, c=c, t=t, k=bank: e.tensor_copy(
                    XT[:, c * N + t * 128: c * N + (t + 1) * 128], self.ps[k][:, 0:128]),
                    waits=(ev_tp,), inc="V")
                self.free_bank(bank, ev_ev)
            ev_done[t % nstg] = ev_ev

    def store_output(self, seq):
        P = self.P
        XT, IDF = self.sb["XT"], self.sb["IDF"]
        yout = self.dram["yout"]
        BFAF = self.sb["BFA"][:].bitcast(F32)
        nstg = 6
        stage = [BFAF[:, i * 1024:(i + 1) * 1024] for i in range(nstg)]
        ev_store = [None] * nstg
        w0 = P.now("T", "V", "S", "G")
        for t in range(NT):
            buf = stage[t % nstg]
            for c in range(C):
                bank = (t * C + c) % 4
                w = self.use_bank(bank, w0 if (t == 0 and c == 0) else ())
                if c == 0 and ev_store[t % nstg] is not None:
                    w = w + (ev_store[t % nstg],)
                ev_tp = P.emit("tensor", lambda e, c=c, t=t, k=bank: e.transpose(
                    self.ps[k][:, 0:128], XT[:, c * N + t * 128: c * N + (t + 1) * 128], IDF[:]),
                    waits=w, inc="T")
                ev_ev = P.emit("vector", lambda e, c=c, k=bank, b=buf: e.tensor_copy(
                    b[:, c * 128:(c + 1) * 128], self.ps[k][:, 0:128]),
                    waits=(ev_tp,), inc="V")
                self.free_bank(bank, ev_ev)
            ev_store[t % nstg] = P.emit("sync", lambda e, b=buf, t=t: e.dma_start(
                out=yout[seq * N + t * 128: seq * N + (t + 1) * 128, :], in_=b),
                waits=(ev_ev,), inc=f"X{t % nstg}")

    def ln_stats(self):
        P = self.P
        XT, M, R, S1, ONE = (self.sb[k] for k in ("XT", "M", "R", "S1", "ONE"))
        ONEF = self.sb["ONEF"]
        off = self.BFW - 4 * N
        XH = [self.bf(off + i * N, N) for i in range(2)]
        SQ = [self.bf(off + (2 + i) * N, N) for i in range(2)]
        w0 = P.now("T", "S", "G")
        ev_sq = [None, None]
        last_mm = None
        for c in range(C):
            b = c % 2
            w = w0 if c < 2 else ()
            if ev_sq[b] is not None:
                w = w + (ev_sq[b],)
            if last_mm is not None and c >= 2:
                w = w + (last_mm,)
            ev_c = P.emit("vector", lambda e, c=c, b=b: e.tensor_copy(
                XH[b], XT[:, c * N:(c + 1) * N]), waits=w, inc="V")
            ev_q = P.emit("scalar", lambda e, b=b: e.activation(SQ[b], XH[b], AF.Square),
                          waits=(ev_c,), inc="S")
            ev_sq[b] = ev_q
            for g in range(4):
                bw = self.use_bank(g, w0) if c == 0 else ()
                P.emit("tensor", lambda e, g=g, c=c: e.matmul(
                    self.ps[g][:], ONEF[:], XT[:, c * N + g * 512: c * N + (g + 1) * 512],
                    start=(c == 0), stop=(c == C - 1)), waits=bw)
                bw2 = self.use_bank(4 + g, (ev_q,)) if c == 0 else (ev_q,)
                evm = P.emit("tensor", lambda e, b=b, g=g, c=c: e.matmul(
                    self.ps[4 + g][:], ONE[:], SQ[b][:, g * 512:(g + 1) * 512],
                    start=(c == 0), stop=(c == C - 1)), waits=bw2,
                    inc="T" if g == 3 else None)
                if g == 3:
                    last_mm = ("T", P.cnt["T"])
        ev_mm = last_mm
        ev_m = []
        for g in range(4):
            ev_m.append(P.emit("vector", lambda e, g=g: e.tensor_scalar(
                M[:, g * 512:(g + 1) * 512], self.ps[g][:], 1.0 / D, None, ALU.mult),
                waits=(ev_mm,), inc="V"))
        P.emit("vector", lambda e: e.drain())
        ev_m2 = P.emit("vector", lambda e: e.tensor_tensor(S1[:], M[:], M[:], ALU.mult), inc="V")
        P.emit("vector", lambda e: e.drain())
        ev_var = []
        for g in range(4):
            ev_var.append(P.emit("vector", lambda e, g=g: e.scalar_tensor_tensor(
                S1[:, g * 512:(g + 1) * 512], self.ps[4 + g][:], 1.0 / D,
                S1[:, g * 512:(g + 1) * 512], ALU.mult, ALU.subtract), inc="V"))
        for g in range(4):
            self.free_bank(g, ev_m[g])
            self.free_bank(4 + g, ev_var[g])
        ev_ln = P.emit("scalar", lambda e: e.activation(S1[:], S1[:], AF.Ln, bias=EPS),
                       waits=tuple(ev_var), inc="S")
        self.ev_lnr = P.emit("scalar", lambda e: e.activation(R[:], S1[:], AF.Exp, scale=-0.5),
                             waits=(ev_ln,), inc="S")
        self.ev_lnm = ev_m[-1]

    def block_l0_attn(self, seq):
        P = self.P
        XT, M, R, S1, S2 = (self.sb[k] for k in ("XT", "M", "R", "S1", "S2"))
        IDB, MSK, SGN = self.sb["IDB"], self.sb["MSK"], self.sb["SGN"]
        oZS, oV, oZQ, oAST, oSG = 0, 16384, 32768, 36864, 45056
        oVSC = oSG + 1024

        ev_prev_zs = None
        ev_prev_tp = None
        ev_vt = None
        for c in range(C):
            w = (self.ev_lnr, self.ev_lnm)
            if ev_prev_zs is not None:
                w = w + (ev_prev_zs,)
            P.emit("vector", lambda e, c=c: e.tensor_tensor(
                S1[:], XT[:, c * N:(c + 1) * N], M[:], ALU.subtract), waits=w)
            ev_rs = P.emit("vector", lambda e, c=c: e.tensor_scalar(
                S2[:], R[:], SGN[:, 2 * C + c:2 * C + c + 1], None, ALU.mult), inc="V")
            P.emit("vector", lambda e: e.drain())
            wv = (ev_prev_tp,) if ev_prev_tp is not None else ()
            ev_vsc = P.emit("vector", lambda e: e.tensor_tensor(
                self.bf(oVSC, 2048), S1[:], S2[:], ALU.mult), waits=wv, inc="V")
            ev_prev_zs = P.emit("scalar", lambda e, c=c: e.activation(
                self.bf(oZS + c * N, N), S1[:], AF.Sign,
                scale=SGN[:, 3 * C + c:3 * C + c + 1]), waits=(ev_rs,), inc="S")
            for t in range(NT):
                bank = (c * NT + t) % 2
                tw = self.use_bank(bank, (ev_vsc,))
                ev_tp = P.emit("tensor", lambda e, t=t, k=bank: e.transpose(
                    self.ps[k][:, 0:64].bitcast(BF16),
                    self.bf(oVSC + t * 128, 128), IDB[:]), waits=tw, inc="T")
                ev_vt = P.emit("vector", lambda e, t=t, c=c, k=bank: e.tensor_copy(
                    self.bf(oV + t * 1024 + c * 128, 128),
                    self.ps[k][:, 0:64].bitcast(BF16)), waits=(ev_tp,), inc="V")
                self.free_bank(bank, ev_vt)
            ev_prev_tp = ("T", P.cnt["T"])
        ev_zs_all = ("S", P.cnt["S"])
        ev_vt_all = ("V", P.cnt["V"])

        for qb in range(4):
            q0 = qb * 512
            nkt = qb * 4 + 4
            wT = P.now("T")
            ev_zq = None
            ev_sub_prev = None
            for c in range(C):
                w = wT + (self.ev_lnr, self.ev_lnm)
                if ev_zq is not None:
                    w = w + (ev_zq,)
                ev_sub = P.emit("vector", lambda e, c=c, q0=q0: e.tensor_tensor(
                    S1[:, 0:512], XT[:, c * N + q0: c * N + q0 + 512],
                    M[:, q0:q0 + 512], ALU.subtract), waits=w, inc="V")
                ev_zq = P.emit("scalar", lambda e, c=c: e.activation(
                    self.bf(oZQ + c * 512, 512), S1[:, 0:512], AF.Sign),
                    waits=(ev_sub,), inc="S")
            ev_mask = [None, None]
            SCB = (0, 1, 6, 7)
            for kt in range(nkt):
                k0 = kt * 128
                bank = SCB[kt % 4]
                w = self.use_bank(bank, (ev_zs_all, ev_zq))
                ev_mm = None
                for c in range(C):
                    ev_mm = P.emit("tensor", lambda e, c=c, kt=kt, k=bank: e.matmul(
                        self.ps[k][:],
                        self.bf(oZS + c * N + kt * 128, 128),
                        self.bf(oZQ + c * 512, 512),
                        start=(c == 0), stop=(c == C - 1)),
                        waits=(w if c == 0 else ()), inc="T" if c == C - 1 else None)
                off = k0 - q0
                if off < 0:
                    ev_a = P.emit("scalar", lambda e, kt=kt, k=bank: e.activation(
                        self.bf(oAST + kt * 512, 512), self.ps[k][:], AF.Sigmoid,
                        scale=0.125), waits=(ev_mm,), inc="S")
                else:
                    oi = off // 128
                    sb_ = kt % 2
                    ws = (ev_mm,)
                    if ev_mask[sb_] is not None:
                        ws = ws + (ev_mask[sb_],)
                    ev_sg = P.emit("scalar", lambda e, k=bank, sb_=sb_: e.activation(
                        self.bf(oSG + sb_ * 512, 512), self.ps[k][:], AF.Sigmoid, scale=0.125),
                        waits=ws, inc="S")
                    ev_a = P.emit("vector", lambda e, kt=kt, oi=oi, sb_=sb_: e.tensor_tensor(
                        self.bf(oAST + kt * 512, 512), self.bf(oSG + sb_ * 512, 512),
                        MSK[:, oi * 512:(oi + 1) * 512], ALU.mult),
                        waits=(ev_sg,), inc="V")
                    ev_mask[sb_] = ev_a
                self.free_bank(bank, ev_a)
            ev_attn_all = P.now("S", "V")
            for eg in range(2):
                for e4 in range(4):
                    ec = eg * 4 + e4
                    bank = 2 + e4
                    w = self.use_bank(bank, ev_attn_all + (ev_vt_all,))
                    ev_mm = None
                    for kt in range(nkt):
                        ev_mm = P.emit("tensor", lambda e, kt=kt, ec=ec, k=bank, nkt=nkt: e.matmul(
                            self.ps[k][:],
                            self.bf(oV + kt * 1024 + ec * 128, 128),
                            self.bf(oAST + kt * 512, 512),
                            start=(kt == 0), stop=(kt == nkt - 1)),
                            waits=(w if kt == 0 else ()), inc="T" if kt == nkt - 1 else None)
                    ev_add = P.emit("vector", lambda e, ec=ec, k=bank, q0=q0: e.tensor_tensor(
                        XT[:, ec * N + q0: ec * N + q0 + 512],
                        XT[:, ec * N + q0: ec * N + q0 + 512], self.ps[k][:], ALU.add),
                        waits=(ev_mm,), inc="V")
                    self.free_bank(bank, ev_add)

    def xn_snapshot(self, oXN, routers=()):
        P = self.P
        XT, M, R, S1, S2 = (self.sb[k] for k in ("XT", "M", "R", "S1", "S2"))
        IDF = self.sb["IDF"]
        S2h = S2[:].bitcast(F16)
        ev = None
        ev_rt = None
        rbw = {}
        for (whi, wlo, whih, k, banks4, dst) in routers:
            for b in banks4:
                rbw[b] = self.use_bank(b, ())
        for c in range(C):
            w = (self.ev_lnr, self.ev_lnm)
            if routers and c >= 1:
                w = w + (ev_rt,)
            P.emit("vector", lambda e, c=c: e.tensor_tensor(
                S1[:], XT[:, c * N:(c + 1) * N], M[:], ALU.subtract), waits=w)
            P.emit("vector", lambda e: e.drain())
            P.emit("vector", lambda e, c=c: e.tensor_tensor(S1[:], S1[:], R[:], ALU.mult))
            P.emit("vector", lambda e: e.drain())
            ev = P.emit("vector", lambda e, c=c, oXN=oXN: e.tensor_copy(
                self.bf(oXN + c * N, N), S1[:]), inc="V")
            if routers:
                P.emit("vector", lambda e: e.drain())
                ev_lo = P.emit("vector", lambda e, c=c, oXN=oXN: e.tensor_tensor(
                    S2h[:, 0:N], S1[:], self.bf(oXN + c * N, N), ALU.subtract), inc="V")
                for (whi, wlo, whih, k, banks4, dst) in routers:
                    for nb in range(4):
                        bank = banks4[nb]
                        terms = ((whi, True), (wlo, True), (whih, False))
                        nterm = len(terms)
                        for ti, (wp, is_hi) in enumerate(terms):
                            first = (c == 0 and ti == 0)
                            last = (c == C - 1 and ti == nterm - 1)
                            wmm = (rbw[bank] if first else ()) + ((ev,) if is_hi else (ev_lo,))
                            def mk(wp=wp, is_hi=is_hi, c=c, nb=nb, bank=bank, k=k, first=first, last=last, oXN=oXN):
                                def f(e):
                                    rhs = (self.bf(oXN + c * N + nb * 512, 512) if is_hi
                                           else S2h[:, nb * 512:(nb + 1) * 512])
                                    return e.matmul(self.ps[bank][0:k, :], wp[:, c * k:(c + 1) * k],
                                                    rhs, start=first, stop=last)
                                return f
                            ev_rt = P.emit("tensor", mk(),
                                           waits=wmm, inc="T" if (nb == 3 and ti == nterm - 1) else None)
        for (whi, wlo, whih, k, banks4, dst) in routers:
            ev_ev = None
            for nb in range(4):
                ev_ev = P.emit("vector", lambda e, nb=nb, k=k, bank=banks4[nb]: e.tensor_copy(
                    S1[0:k, nb * 512:(nb + 1) * 512], self.ps[bank][0:k, :]),
                    waits=(ev_rt,), inc="V")
                self.free_bank(banks4[nb], ev_ev)
            for t in range(NT):
                tb = (t % 2)
                tw = self.use_bank(tb, (ev_ev,))
                ev_tp = P.emit("tensor", lambda e, t=t, k=k, tb=tb: e.transpose(
                    self.ps[tb][:, 0:k], S1[0:k, t * 128:(t + 1) * 128], IDF[0:k, 0:k]),
                    waits=tw, inc="T")
                ev_rt = P.emit("vector", lambda e, t=t, k=k, tb=tb, dst=dst: e.tensor_copy(
                    dst[:, t * k:(t + 1) * k], self.ps[tb][:, 0:k]),
                    waits=(ev_tp,), inc="V")
                self.free_bank(tb, ev_rt)
        return ev, ev_rt

    def top2_gates(self, ev_logits):
        P = self.P
        NEG = -1e30
        GROUT, G0B = self.sb["GROUT"], self.sb["G0B"]
        RA, RB, RC = self.sb["RA"], self.sb["RB"], self.sb["RC"]
        RV1, RV2, RV3 = self.sb["RV1"], self.sb["RV2"], self.sb["RV3"]
        def g3(ap):
            return ap[:].rearrange("p (t k) -> p t k", k=8)
        def b3(ap):
            return ap[:].rearrange("p (t o) -> p t o", o=1).broadcast_to([128, 16, 8])
        P.emit("vector", lambda e: e.tensor_reduce(RV1[:], g3(GROUT), axis=AX.X, op=ALU.max),
               waits=(ev_logits,))
        P.emit("vector", lambda e: e.drain())
        P.emit("vector", lambda e: e.tensor_tensor(g3(RA), g3(GROUT), b3(RV1), ALU.is_ge))
        P.emit("vector", lambda e: e.drain())
        P.emit("vector", lambda e: e.scalar_tensor_tensor(g3(RB), g3(RA), NEG, g3(GROUT),
                                                          ALU.mult, ALU.add))
        P.emit("vector", lambda e: e.drain())
        P.emit("vector", lambda e: e.tensor_reduce(RV2[:], g3(RB), axis=AX.X, op=ALU.max))
        P.emit("vector", lambda e: e.drain())
        P.emit("vector", lambda e: e.tensor_tensor(g3(RC), g3(RB), b3(RV2), ALU.is_ge))
        ev_sm = P.emit("vector", lambda e: e.tensor_tensor(RV3[:], RV1[:], RV2[:], ALU.subtract),
                       inc="V")
        ev_w1 = P.emit("scalar", lambda e: e.activation(RV2[:], RV3[:], AF.Sigmoid),
                       waits=(ev_sm,), inc="S")
        P.emit("vector", lambda e: e.tensor_tensor(g3(RB), g3(RA), b3(RV2), ALU.mult),
               waits=(ev_w1,))
        P.emit("vector", lambda e: e.drain())
        P.emit("vector", lambda e: e.tensor_tensor(g3(RA), g3(RC), b3(RV2), ALU.mult))
        P.emit("vector", lambda e: e.drain())
        P.emit("vector", lambda e: e.tensor_tensor(RC[:], RC[:], RA[:], ALU.subtract))
        P.emit("vector", lambda e: e.drain())
        return P.emit("vector", lambda e: e.tensor_tensor(G0B[:], RB[:], RC[:], ALU.add), inc="V")

    def gates_transpose(self, ev_gates):
        P = self.P
        G0B, GTR, IDF = self.sb["G0B"], self.sb["GTR"], self.sb["IDF"]
        ev = None
        for t in range(NT):
            bank = 6 + t % 2
            w = self.use_bank(bank, (ev_gates,))
            ev_tp = P.emit("tensor", lambda e, t=t, k=bank: e.transpose(
                self.ps[k][0:8, 0:128], G0B[:, t * 8:(t + 1) * 8], IDF[:]),
                waits=w, inc="T")
            ev = P.emit("vector", lambda e, t=t, k=bank: e.tensor_copy(
                GTR[0:8, t * 128:(t + 1) * 128], self.ps[k][0:8, 0:128]),
                waits=(ev_tp,), inc="V")
            self.free_bank(bank, ev)
        return ev

    def block_l0_ffn(self, seq):
        P = self.P
        XT, M, R, S1, S2 = (self.sb[k] for k in ("XT", "M", "R", "S1", "S2"))
        RWT, GROUT, GTR, SELS = self.sb["RWT"], self.sb["GROUT"], self.sb["GTR"], self.sb["SELS"]
        oXN, oW1, oHG, oW2 = 0, 16384, 32768, 40960
        S1b = S1[:].bitcast(BF16)
        S2b = S2[:].bitcast(BF16)
        l0w1, l0w2 = self.dram["l0w1"], self.dram["l0w2"]

        ev_xn, ev_rt = self.xn_snapshot(oXN, routers=(
            (RWT[:], self.sb["RWLO"][:], self.sb["RWTH"][:], 8, (4, 5, 6, 7), GROUT[:]),))
        ev_lg = ev_rt
        ev_gates = self.top2_gates(ev_lg)
        ev_gtr = self.gates_transpose(ev_gates)

        ev_w1_free = None
        ev_gb_free = None
        ev_w2buf = [None, None]
        ev_hg_rd = None
        for ex in range(8):
            wv = (ev_w1_free,) if ev_w1_free is not None else P.now("T", "V", "S")
            ev_w1l = None
            for c in range(C):
                ev_w1l = P.emit("gpsimd", lambda e, ex=ex, c=c: e.dma_start(
                    out=self.bf(oW1 + c * 2048, 2048),
                    in_=l0w1[ex, c * 128:(c + 1) * 128, :]), waits=wv if c == 0 else (),
                    inc="X6")
            wg = (ev_gtr,) if ex == 0 else ()
            if ev_gb_free is not None:
                wg = wg + (ev_gb_free,)
            ev_gb = None
            for g in range(4):
                bank = 6 + g % 2
                bwg = self.use_bank(bank, wg if g < 2 else ())
                ev_bc = P.emit("tensor", lambda e, ex=ex, g=g, k=bank: e.matmul(
                    self.ps[k][:], SELS[:, ex * 128:(ex + 1) * 128],
                    GTR[:, g * 512:(g + 1) * 512], start=True, stop=True),
                    waits=bwg, inc="T")
                ev_gb = P.emit("vector", lambda e, g=g, k=bank: e.tensor_copy(
                    S2b[:, g * 512:(g + 1) * 512], self.ps[k][:]),
                    waits=(ev_bc,), inc="V")
                self.free_bank(bank, ev_gb)

            for nb in range(4):
                n0 = nb * 512
                ev_hg = None
                ev_hg_list = []
                for hc in range(16):
                    bank = 4 + hc % 4
                    w = self.use_bank(bank, ((("X6", P.cnt["X6"]),) if hc == 0 and nb == 0 else ()))
                    ev_h = None
                    for c in range(C):
                        ev_h = P.emit("tensor", lambda e, c=c, hc=hc, k=bank, n0=n0: e.matmul(
                            self.ps[k][:],
                            self.bf(oW1 + c * 2048 + hc * 128, 128),
                            self.bf(oXN + c * N + n0, 512),
                            start=(c == 0), stop=(c == C - 1)),
                            waits=(w if c == 0 else ()), inc="T" if c == C - 1 else None)
                    if nb == 3 and hc == 15:
                        self._w1_last_use = ev_h
                    sl = hc % 2
                    ev_rl = P.emit("scalar", lambda e, k=bank, sl=sl: e.activation(
                        S1b[:, sl * 512:(sl + 1) * 512], self.ps[k][:], AF.Relu),
                        waits=(ev_h,), inc="S")
                    self.free_bank(bank, ev_rl)
                    ev_sq = P.emit("vector", lambda e, sl=sl: e.tensor_tensor(
                        S1b[:, (2 + sl) * 512:(3 + sl) * 512],
                        S1b[:, sl * 512:(sl + 1) * 512],
                        S1b[:, sl * 512:(sl + 1) * 512], ALU.mult),
                        waits=(ev_rl,), inc="V")
                    whg = (ev_sq, ev_gb)
                    if ev_hg_rd is not None:
                        whg = whg + (ev_hg_rd,)
                    ev_hg = P.emit("gpsimd", lambda e, hc=hc, sl=sl, n0=n0: e.tensor_tensor(
                        self.bf(oHG + hc * 512, 512),
                        S1b[:, (2 + sl) * 512:(3 + sl) * 512],
                        S2b[:, n0:n0 + 512],
                        ALU.mult), waits=whg, inc="G")
                    ev_hg_list.append(ev_hg)
                ev_y = None
                for g in range(4):
                    buf = g % 2
                    wl = (ev_w2buf[buf],) if ev_w2buf[buf] is not None else P.now("V", "S")
                    ev_l = P.emit("gpsimd", lambda e, ex=ex, g=g, buf=buf: e.dma_start(
                        out=self.bf(oW2 + buf * 4096, 4096).rearrange("p (s d) -> p s d", d=1024),
                        in_=l0w2[ex, g * 512:(g + 1) * 512, :].rearrange("(s p) d -> p s d", p=128)),
                        waits=wl, inc="X7" if buf == 0 else "X8")
                    for si in range(4):
                        hc = g * 4 + si
                        for dc in range(C):
                            wy = ()
                            if si == 0 and dc == 0:
                                wy = (ev_l, ("G", P.cnt["G"]))
                                if g == 0:
                                    wy = self.use_bank(dc, wy)
                            elif si == 0 and g == 0:
                                wy = self.use_bank(dc, ())
                            ev_y = P.emit("tensor", lambda e, buf=buf, si=si, dc=dc, hc=hc: e.matmul(
                                self.ps[dc][:],
                                self.bf(oW2 + buf * 4096 + si * 1024 + dc * 128, 128),
                                self.bf(oHG + hc * 512, 512),
                                start=(hc == 0), stop=(hc == 15)),
                                waits=wy, inc="T" if (si == 3 and dc == C - 1) else None)
                    ev_w2buf[buf] = ev_y
                ev_hg_rd = ev_y
                for dc in (7, 6, 5, 4, 3, 2, 1, 0):
                    ev_add = P.emit("vector", lambda e, dc=dc, n0=n0: e.tensor_tensor(
                        XT[:, dc * N + n0: dc * N + n0 + 512],
                        XT[:, dc * N + n0: dc * N + n0 + 512], self.ps[dc][:], ALU.add),
                        waits=(ev_y,), inc="V")
                    self.free_bank(dc, ev_add)
            ev_w1_free = self._w1_last_use
            ev_gb_free = ("G", P.cnt["G"])

    def xn_half(self, oXN, h):
        P = self.P
        XT, M, R, S2 = (self.sb[k] for k in ("XT", "M", "R", "S2"))
        lo = h * 1024
        ev = None
        for c in range(C):
            w = (self.ev_lnr, self.ev_lnm) + (P.now("T", "G") if c == 0 else ())
            P.emit("vector", lambda e, c=c, lo=lo: e.tensor_tensor(
                S2[:, 0:1024], XT[:, c * N + lo: c * N + lo + 1024],
                M[:, lo:lo + 1024], ALU.subtract), waits=w)
            P.emit("vector", lambda e: e.drain())
            ev = P.emit("vector", lambda e, c=c, oXN=oXN, lo=lo: e.tensor_tensor(
                self.bf(oXN + c * 1024, 1024), S2[:, 0:1024], R[:, lo:lo + 1024],
                ALU.mult), inc="V")
        return ev

    def load_wslot(self, oW, wname):
        P = self.P
        wt = self.dram[wname]
        wv = P.now("T", "V", "S", "G")
        ev = None
        for c in range(C):
            ev = P.emit("gpsimd", lambda e, c=c, oW=oW, wt=wt: e.dma_start(
                out=self.bf(oW + c * 1024, 1024), in_=wt[c * 128:(c + 1) * 128, :]),
                waits=wv if c == 0 else (), inc="X6")
        return ev

    def proj_half(self, oW, oXN, dst, h, bias_col, ev_w, ev_xn, to_token_major=False):
        P = self.P
        BQKV, IDB = self.sb["BQKV"], self.sb["IDB"]
        S1b = self.sb["S1"][:].bitcast(BF16)
        ev_out = None
        for eo in range(C):
            for blk in range(2):
                bank = (eo * 2 + blk) % 4
                w = self.use_bank(bank, (ev_w, ev_xn))
                ev_mm = None
                for c in range(C):
                    ev_mm = P.emit("tensor", lambda e, c=c, eo=eo, blk=blk, k=bank, oW=oW, oXN=oXN: e.matmul(
                        self.ps[k][:],
                        self.bf(oW + c * 1024 + eo * 128, 128),
                        self.bf(oXN + c * 1024 + blk * 512, 512),
                        start=(c == 0), stop=(c == C - 1)),
                        waits=(w if c == 0 else ()), inc="T" if c == C - 1 else None)
                if not to_token_major:
                    ev_out = P.emit("scalar", lambda e, eo=eo, blk=blk, k=bank, dst=dst, bias_col=bias_col: e.activation(
                        self.bf(dst + eo * 1024 + blk * 512, 512), self.ps[k][:], AF.Identity,
                        bias=BQKV[:, bias_col * C + eo: bias_col * C + eo + 1]),
                        waits=(ev_mm,), inc="S")
                    self.free_bank(bank, ev_out)
                else:
                    ev_st = P.emit("scalar", lambda e, k=bank, bias_col=bias_col, eo=eo: e.activation(
                        S1b[:, 0:512], self.ps[k][:], AF.Identity,
                        bias=BQKV[:, bias_col * C + eo: bias_col * C + eo + 1]),
                        waits=(ev_mm,), inc="S")
                    self.free_bank(bank, ev_st)
                    for tt in range(4):
                        tb = 4 + (tt % 2)
                        tw = self.use_bank(tb, (ev_st,))
                        ev_tp = P.emit("tensor", lambda e, tt=tt, k=tb: e.transpose(
                            self.ps[k][:, 0:64].bitcast(BF16),
                            S1b[:, tt * 128:(tt + 1) * 128], IDB[:]), waits=tw, inc="T")
                        lt = blk * 4 + tt
                        ev_out = P.emit("vector", lambda e, lt=lt, eo=eo, k=tb, dst=dst: e.tensor_copy(
                            self.bf(dst + lt * 1024 + eo * 128, 128),
                            self.ps[k][:, 0:64].bitcast(BF16)), waits=(ev_tp,), inc="V")
                        self.free_bank(tb, ev_out)
        return ev_out

    def block_l1_attn(self, seq):
        P = self.P
        XT, M, R, S1, S2 = (self.sb[k] for k in ("XT", "M", "R", "S1", "S2"))
        BQKV, IDB, RV1, RV2, RV3 = (self.sb[k] for k in ("BQKV", "IDB", "RV1", "RV2", "RV3"))
        oK, oQ, oV, oW, oXN3 = 0, 16384, 24576, 32768, 40960
        S1b = S1[:].bitcast(BF16)
        SCALE = 1.0 / 32.0

        for kh in range(2):
            ev_xn = self.xn_half(oXN3, kh)
            ev_w = self.load_wslot(oW, "wk")
            for eo in range(C):
                for blk in range(2):
                    bank = (eo * 2 + blk) % 4
                    w = self.use_bank(bank, (ev_w, ev_xn))
                    ev_mm = None
                    for c in range(C):
                        ev_mm = P.emit("tensor", lambda e, c=c, eo=eo, blk=blk, k=bank: e.matmul(
                            self.ps[k][:],
                            self.bf(oW + c * 1024 + eo * 128, 128),
                            self.bf(oXN3 + c * 1024 + blk * 512, 512),
                            start=(c == 0), stop=(c == C - 1)),
                            waits=(w if c == 0 else ()), inc="T" if c == C - 1 else None)
                    ev_k = P.emit("scalar", lambda e, eo=eo, blk=blk, k=bank, kh=kh: e.activation(
                        self.bf(oK + eo * 2048 + kh * 1024 + blk * 512, 512),
                        self.ps[k][:], AF.Identity,
                        bias=BQKV[:, C + eo: C + eo + 1]), waits=(ev_mm,), inc="S")
                    self.free_bank(bank, ev_k)
        ev_k_all = P.now("S")

        for h in range(2):
            ev_xn = self.xn_half(oXN3, h)
            ev_w = self.load_wslot(oW, "wq")
            self.proj_half(oW, oXN3, oQ, h, 0, ev_w, ev_xn)
            ev_w = self.load_wslot(oW, "wv")
            self.proj_half(oW, oXN3, oV, h, 2, ev_w, ev_xn, to_token_major=True)
            ev_w = self.load_wslot(oW, "wo")
            ev_qv = P.now("S", "V")
            for nl in range(4):
                ng = h * 4 + nl
                q0 = nl * 256
                for mg in range(2):
                    for qt in range(2):
                        for mi in range(4):
                            m = mg * 4 + mi
                            bank = qt * 2 + mi // 2
                            boff = (mi % 2) * 256
                            w = self.use_bank(bank, ev_k_all + ev_qv) if mi % 2 == 0 else ()
                            ev_mm = None
                            for c in range(C):
                                ev_mm = P.emit("tensor", lambda e, c=c, qt=qt, m=m, k=bank, boff=boff, q0=q0: e.matmul(
                                    self.ps[k][:, boff:boff + 256],
                                    self.bf(oQ + c * 1024 + q0 + qt * 128, 128),
                                    self.bf(oK + c * 2048 + m * 256, 256),
                                    start=(c == 0), stop=(c == C - 1)),
                                    waits=(w if c == 0 else ()), inc="T" if c == C - 1 else None)
                    ev_sc = ("T", P.cnt["T"])
                    for qt in range(2):
                        pv0, pv1 = self.ps[qt * 2], self.ps[qt * 2 + 1]
                        P.emit("vector", lambda e, pv0=pv0: e.tensor_reduce(
                            RV1[:, 0:2], pv0[:].rearrange("p (g k) -> p g k", k=256),
                            axis=AX.X, op=ALU.max), waits=(ev_sc,))
                        P.emit("vector", lambda e, pv1=pv1: e.tensor_reduce(
                            RV1[:, 2:4], pv1[:].rearrange("p (g k) -> p g k", k=256),
                            axis=AX.X, op=ALU.max))
                        P.emit("vector", lambda e: e.drain())
                        def mk_sub(pv, rcols, dcol):
                            def f(e, pv=pv, rcols=rcols, dcol=dcol):
                                return e.tensor_tensor(
                                    S1[:, dcol:dcol + 512].rearrange("p (g k) -> p g k", k=256),
                                    pv[:].rearrange("p (g k) -> p g k", k=256),
                                    RV1[:, rcols:rcols + 2].rearrange("p (g o) -> p g o", o=1)
                                        .broadcast_to([128, 2, 256]), ALU.subtract)
                            return f
                        P.emit("vector", mk_sub(pv0, 0, 0))
                        ev_d = P.emit("vector", mk_sub(pv1, 2, 512), inc="V")
                        ev_e = P.emit("scalar", lambda e: e.activation(
                            S1[:, 0:1024], S1[:, 0:1024], AF.Exp, scale=SCALE),
                            waits=(ev_d,), inc="S")
                        P.emit("vector", lambda e: e.tensor_reduce(
                            RV2[:, 0:4], S1[:, 0:1024].rearrange("p (g k) -> p g k", k=256),
                            axis=AX.X, op=ALU.add), waits=(ev_e,))
                        P.emit("vector", lambda e: e.drain())
                        P.emit("vector", lambda e: e.reciprocal(RV3[:, 0:4], RV2[:, 0:4]))
                        P.emit("vector", lambda e: e.drain())
                        P.emit("vector", lambda e: e.tensor_tensor(
                            S1[:, 0:1024].rearrange("p (g k) -> p g k", k=256),
                            S1[:, 0:1024].rearrange("p (g k) -> p g k", k=256),
                            RV3[:, 0:4].rearrange("p (g o) -> p g o", o=1)
                                .broadcast_to([128, 4, 256]), ALU.mult))
                        P.emit("vector", lambda e: e.drain())
                        P.emit("vector", lambda e: e.tensor_tensor(
                            S1[:, 1024:1280], S1[:, 0:256], S1[:, 256:512], ALU.add))
                        P.emit("vector", lambda e: e.tensor_tensor(
                            S1[:, 1280:1536], S1[:, 512:768], S1[:, 768:1024], ALU.add))
                        P.emit("vector", lambda e: e.drain())
                        P.emit("vector", lambda e: e.tensor_tensor(
                            S1[:, 1024:1280], S1[:, 1024:1280], S1[:, 1280:1536], ALU.add))
                        P.emit("vector", lambda e: e.drain())
                        if mg * 4 <= ng < mg * 4 + 4:
                            mi = ng - mg * 4
                            P.emit("vector", lambda e, mi=mi: e.tensor_tensor(
                                S1[:, 1024:1280], S1[:, 1024:1280],
                                S1[:, mi * 256:(mi + 1) * 256], ALU.add))
                            P.emit("vector", lambda e: e.drain())
                        if mg == 0:
                            ev_g = P.emit("vector", lambda e, qt=qt: e.tensor_copy(
                                S2[:, qt * 256:(qt + 1) * 256], S1[:, 1024:1280]), inc="V")
                        else:
                            ev_g = P.emit("vector", lambda e, qt=qt: e.tensor_tensor(
                                S2[:, qt * 256:(qt + 1) * 256],
                                S2[:, qt * 256:(qt + 1) * 256], S1[:, 1024:1280], ALU.add),
                                inc="V")
                        self.free_bank(qt * 2, ev_d)
                        self.free_bank(qt * 2 + 1, ev_d)
                ev_g_all = ("V", P.cnt["V"])
                P.emit("vector", lambda e: e.tensor_copy(S1b[:, 0:512], S2[:, 0:512]),
                       waits=(ev_g_all,))
                ev_gb = P.emit("vector", lambda e: e.drain(), inc="V")
                for qt in range(2):
                    for kb in range(2):
                        tb = 4 + (qt * 2 + kb) % 2
                        tw = self.use_bank(tb, (ev_gb,))
                        ev_tp = P.emit("tensor", lambda e, qt=qt, kb=kb, k=tb: e.transpose(
                            self.ps[k][:, 0:64].bitcast(BF16),
                            S1b[:, qt * 256 + kb * 128: qt * 256 + (kb + 1) * 128], IDB[:]),
                            waits=tw, inc="T")
                        ev_gt = P.emit("vector", lambda e, qt=qt, kb=kb, k=tb: e.tensor_copy(
                            S1b[:, 512 + kb * 256 + qt * 128: 512 + kb * 256 + (qt + 1) * 128],
                            self.ps[k][:, 0:64].bitcast(BF16)), waits=(ev_tp,), inc="V")
                        self.free_bank(tb, ev_gt)
                ev_gt_all = ("V", P.cnt["V"])
                for ec in range(C):
                    bank = 4 + ec // 2
                    boff = (ec % 2) * 256
                    w = self.use_bank(bank, (ev_gt_all,)) if ec % 2 == 0 else ()
                    ev_mm = None
                    for kt in range(2):
                        lt = (ng - h * 4) * 2 + kt
                        ev_mm = P.emit("tensor", lambda e, lt=lt, ec=ec, kt=kt, k=bank, boff=boff: e.matmul(
                            self.ps[k][:, boff:boff + 256],
                            self.bf(oV + lt * 1024 + ec * 128, 128),
                            S1b[:, 512 + kt * 256: 512 + (kt + 1) * 256],
                            start=(kt == 0), stop=(kt == 1)),
                            waits=(w if kt == 0 else ()), inc="T" if kt == 1 else None)
                    if ec % 2 == 1:
                        ev_av = P.emit("vector", lambda e, ec=ec, k=bank: e.tensor_copy(
                            S1b[:, 1536 + (ec - 1) * 256: 1536 + (ec + 1) * 256],
                            self.ps[k][:]), waits=(ev_mm,), inc="V")
                        self.free_bank(bank, ev_av)
                ev_av_all = ("V", P.cnt["V"])
                gq = h * 1024 + nl * 256
                for eo in range(C):
                    bank = eo // 2
                    boff = (eo % 2) * 256
                    w = self.use_bank(bank, (ev_av_all, ("X6", P.cnt["X6"]))) if eo % 2 == 0 else ()
                    ev_mm = None
                    for ec in range(C):
                        ev_mm = P.emit("tensor", lambda e, ec=ec, eo=eo, k=bank, boff=boff: e.matmul(
                            self.ps[k][:, boff:boff + 256],
                            self.bf(oW + ec * 1024 + eo * 128, 128),
                            S1b[:, 1536 + ec * 256: 1536 + (ec + 1) * 256],
                            start=(ec == 0), stop=(ec == C - 1)),
                            waits=(w if ec == 0 else ()), inc="T" if ec == C - 1 else None)
                    if eo % 2 == 1:
                        for eo2 in (eo - 1, eo):
                            bo2 = (eo2 % 2) * 256
                            ev_o = P.emit("vector", lambda e, eo2=eo2, k=bank, bo2=bo2, gq=gq: e.scalar_tensor_tensor(
                                XT[:, eo2 * N + gq: eo2 * N + gq + 256],
                                self.ps[k][:, bo2:bo2 + 256],
                                BQKV[:, 3 * C + eo2: 3 * C + eo2 + 1],
                                XT[:, eo2 * N + gq: eo2 * N + gq + 256],
                                ALU.add, ALU.add), waits=(ev_mm,), inc="V")
                        self.free_bank(bank, ev_o)

    def block_l1_ffn(self, seq):
        P = self.P
        XT, M, R, S1, S2 = (self.sb[k] for k in ("XT", "M", "R", "S1", "S2"))
        GROUT, G0B, GTR, SELS = self.sb["GROUT"], self.sb["G0B"], self.sb["GTR"], self.sb["SELS"]
        RA, RB, RC = self.sb["RA"], self.sb["RB"], self.sb["RC"]
        RV1, RV2, RV3 = self.sb["RV1"], self.sb["RV2"], self.sb["RV3"]
        WGT, WET, GBB, EBB = self.sb["WGT"], self.sb["WET"], self.sb["GBB"], self.sb["EBB"]
        S1b = S1[:].bitcast(BF16)
        S2b = S2[:].bitcast(BF16)
        oHR, oXN4 = 0, 32768
        NEG = -1e30
        w1t, w2t, w_int, w_outt = (self.dram[k] for k in ("w1_t", "w2_t", "w_in_t", "w_out_t"))

        ev_xn, ev_rt = self.xn_snapshot(oXN4, routers=(
            (WGT[:], self.sb["WGLO"][:], self.sb["WGTH"][:], 2, (0, 1, 2, 3), RB[:, 0:32]),
            (WET[:], self.sb["WELO"][:], self.sb["WETH"][:], 8, (4, 5, 6, 7), GROUT[:])))
        ev_gl = ev_el = ev_rt
        def g2(ap):
            return ap.rearrange("p (t k) -> p t k", k=2)
        def g8(ap):
            return ap.rearrange("p (t k) -> p t k", k=8)
        def b16(ap, k):
            return ap.rearrange("p (t o) -> p t o", o=1).broadcast_to([128, 16, k])
        ev1 = P.emit("vector", lambda e: e.tensor_tensor(
            g2(RA[:, 0:32]), g2(RB[:, 0:32]),
            GBB[:].rearrange("p (o k) -> p o k", o=1).broadcast_to([128, 16, 2]), ALU.add),
            waits=(ev_gl,), inc="V")
        ev2 = P.emit("vector", lambda e: e.tensor_tensor(
            g8(GROUT[:]), g8(GROUT[:]),
            EBB[:].rearrange("p (o k) -> p o k", o=1).broadcast_to([128, 16, 8]), ALU.add),
            waits=(ev_el,), inc="V")
        P.emit("vector", lambda e: e.drain())
        GL = RA[:, 0:32]
        GEQ = RB[:, 0:32]
        P.emit("vector", lambda e: e.drain())
        P.emit("vector", lambda e: e.tensor_reduce(RV1[:], g2(GL), axis=AX.X, op=ALU.max))
        P.emit("vector", lambda e: e.drain())
        P.emit("vector", lambda e: e.tensor_tensor(g2(GEQ), g2(GL), b16(RV1[:], 2), ALU.is_ge))
        P.emit("vector", lambda e: e.tensor_reduce(RV2[:], g2(GL), axis=AX.X, op=ALU.add))
        P.emit("vector", lambda e: e.drain())
        P.emit("vector", lambda e: e.tensor_tensor(RV3[:], RV1[:], RV2[:], ALU.subtract))
        P.emit("vector", lambda e: e.drain())
        ev_sm = P.emit("vector", lambda e: e.tensor_tensor(RV3[:], RV3[:], RV1[:], ALU.add),
                       inc="V")
        ev_gp = P.emit("scalar", lambda e: e.activation(RV3[:], RV3[:], AF.Sigmoid),
                       waits=(ev_sm,), inc="S")
        geq_b = GEQ.rearrange("p (t g o) -> p t g o", g=2, o=1).broadcast_to([128, 16, 2, 4])
        P.emit("vector", lambda e: e.tensor_scalar(
            RC[:].rearrange("p (t g k) -> p t g k", g=2, k=4), geq_b, 1e30, -1e30,
            ALU.mult, ALU.add))
        P.emit("vector", lambda e: e.drain())
        P.emit("vector", lambda e: e.tensor_tensor(RC[:], GROUT[:], RC[:], ALU.add))
        P.emit("vector", lambda e: e.drain())
        P.emit("vector", lambda e: e.tensor_reduce(RV1[:], g8(RC[:]), axis=AX.X, op=ALU.max))
        P.emit("vector", lambda e: e.drain())
        ev_d = P.emit("vector", lambda e: e.tensor_tensor(
            g8(G0B[:]), g8(RC[:]), b16(RV1[:], 8), ALU.subtract), inc="V")
        ev_e = P.emit("scalar", lambda e: e.activation(G0B[:], G0B[:], AF.Exp),
                      waits=(ev_d,), inc="S")
        P.emit("vector", lambda e: e.tensor_reduce(RV1[:], g8(G0B[:]), axis=AX.X, op=ALU.add),
               waits=(ev_e,))
        P.emit("vector", lambda e: e.drain())
        P.emit("vector", lambda e: e.reciprocal(RV2[:], RV1[:]))
        P.emit("vector", lambda e: e.drain())
        P.emit("vector", lambda e: e.tensor_tensor(g8(G0B[:]), g8(G0B[:]), b16(RV2[:], 8), ALU.mult),
               waits=(ev_gp,))
        P.emit("vector", lambda e: e.drain())
        P.emit("vector", lambda e: e.tensor_tensor(g8(G0B[:]), g8(G0B[:]), b16(RV3[:], 8), ALU.mult))
        P.emit("vector", lambda e: e.drain())
        P.emit("vector", lambda e: e.tensor_reduce(RV1[:], g8(G0B[:]), axis=AX.X, op=ALU.max))
        P.emit("vector", lambda e: e.drain())
        P.emit("vector", lambda e: e.tensor_tensor(g8(RB[:]), g8(G0B[:]), b16(RV1[:], 8), ALU.is_ge))
        P.emit("vector", lambda e: e.drain())
        P.emit("vector", lambda e: e.scalar_tensor_tensor(g8(RC[:]), g8(RB[:]), NEG, g8(G0B[:]),
                                                          ALU.mult, ALU.add))
        P.emit("vector", lambda e: e.drain())
        P.emit("vector", lambda e: e.tensor_reduce(RV2[:], g8(RC[:]), axis=AX.X, op=ALU.max))
        P.emit("vector", lambda e: e.drain())
        P.emit("vector", lambda e: e.tensor_tensor(g8(RA[:]), g8(RC[:]), b16(RV2[:], 8), ALU.is_ge))
        P.emit("vector", lambda e: e.drain())
        P.emit("vector", lambda e: e.tensor_tensor(g8(RB[:]), g8(RB[:]), b16(RV1[:], 8), ALU.mult))
        P.emit("vector", lambda e: e.drain())
        P.emit("vector", lambda e: e.tensor_tensor(g8(RA[:]), g8(RA[:]), b16(RV2[:], 8), ALU.mult))
        P.emit("vector", lambda e: e.drain())
        ev_gates = P.emit("vector", lambda e: e.tensor_tensor(G0B[:], RB[:], RA[:], ALU.add),
                          inc="V")
        ev_gtr = self.gates_transpose(ev_gates)

        ev_stripe = [None, None]
        ev_hr = None
        for hc in range(16):
            buf = hc % 2
            wl = (ev_stripe[buf],) if ev_stripe[buf] is not None else (ev_xn, ("V", P.cnt["V"]))
            bs = "X7" if buf == 0 else "X8"
            ev_la = P.emit("gpsimd", lambda e, hc=hc, buf=buf: e.dma_start(
                out=S2b[:, buf * 2048: buf * 2048 + 1024].rearrange("p (c k) -> p c k", k=128),
                in_=w1t[:, hc * 128:(hc + 1) * 128].rearrange("(c p) k -> p c k", p=128)),
                waits=wl, inc=bs)
            ev_lb = P.emit("gpsimd", lambda e, hc=hc, buf=buf: e.dma_start(
                out=S2b[:, buf * 2048 + 1024: buf * 2048 + 2048].rearrange("p (c k) -> p c k", k=128),
                in_=w1t[:, 2048 + hc * 128: 2048 + (hc + 1) * 128].rearrange("(c p) k -> p c k", p=128)),
                inc=bs)
            for nb in range(4):
                n0 = nb * 512
                bka = 4 + (nb % 2) * 2
                bkb = bka + 1
                wa = self.use_bank(bka, (ev_lb,) if nb == 0 else ())
                wb = self.use_bank(bkb, ())
                ev_ma = ev_mb = None
                for c in range(C):
                    ev_ma = P.emit("tensor", lambda e, c=c, buf=buf, n0=n0, bka=bka: e.matmul(
                        self.ps[bka][:], S2b[:, buf * 2048 + c * 128: buf * 2048 + (c + 1) * 128],
                        self.bf(oXN4 + c * N + n0, 512),
                        start=(c == 0), stop=(c == C - 1)),
                        waits=(wa if c == 0 else ()), inc="T" if c == C - 1 else None)
                for c in range(C):
                    ev_mb = P.emit("tensor", lambda e, c=c, buf=buf, n0=n0, bkb=bkb: e.matmul(
                        self.ps[bkb][:], S2b[:, buf * 2048 + 1024 + c * 128: buf * 2048 + 1024 + (c + 1) * 128],
                        self.bf(oXN4 + c * N + n0, 512),
                        start=(c == 0), stop=(c == C - 1)),
                        waits=(wb if c == 0 else ()), inc="T" if c == C - 1 else None)
                sl = nb % 2
                ev_sa = P.emit("scalar", lambda e, sl=sl, bka=bka: e.activation(
                    S1b[:, sl * 512:(sl + 1) * 512], self.ps[bka][:], AF.Silu),
                    waits=(ev_ma,), inc="S")
                self.free_bank(bka, ev_sa)
                ev_hr = P.emit("vector", lambda e, hc=hc, n0=n0, sl=sl, bkb=bkb: e.tensor_tensor(
                    self.bf(oHR + hc * N + n0, 512),
                    S1b[:, sl * 512:(sl + 1) * 512], self.ps[bkb][:], ALU.mult),
                    waits=(ev_sa, ev_mb), inc="V")
                self.free_bank(bkb, ev_hr)
            ev_stripe[buf] = ev_hr and ("T", P.cnt["T"])
        ev_hr_all = ("V", P.cnt["V"])
        ev_gb_free = None
        ev_stripe = [None, None]
        ev_hr_mm_done = P.now("T")
        for ex in range(8):
            wg = (ev_gtr,) + ev_hr_mm_done if ex == 0 else ()
            if ev_gb_free is not None:
                wg = wg + (ev_gb_free,)
            ev_gb = None
            for g in range(4):
                bank = 6 + g % 2
                bwg = self.use_bank(bank, wg if g < 2 else ())
                ev_bc = P.emit("tensor", lambda e, ex=ex, g=g, k=bank: e.matmul(
                    self.ps[k][:], SELS[:, ex * 128:(ex + 1) * 128],
                    GTR[:, g * 512:(g + 1) * 512], start=True, stop=True),
                    waits=bwg, inc="T")
                ev_gb = P.emit("vector", lambda e, g=g, k=bank: e.tensor_copy(
                    S2b[:, 2048 + g * 512: 2048 + (g + 1) * 512], self.ps[k][:]),
                    waits=(ev_bc,), inc="V")
                self.free_bank(bank, ev_gb)
            for nb in range(4):
                n0 = nb * 512
                ev_y = None
                for hc in range(16):
                    buf = hc % 2
                    wl = (ev_stripe[buf],) if ev_stripe[buf] is not None else ev_hr_mm_done
                    ev_l = P.emit("gpsimd", lambda e, ex=ex, hc=hc, buf=buf: e.dma_start(
                        out=S2b[:, buf * 1024:(buf + 1) * 1024],
                        in_=w2t[ex, hc * 128:(hc + 1) * 128, :]), waits=wl,
                        inc="X7" if buf == 0 else "X8")
                    sl = hc % 2
                    whg = (ev_gb, ("V", P.cnt["V"]))
                    if ev_y is not None:
                        whg = whg + (ev_y,)
                    ev_hg = P.emit("gpsimd", lambda e, hc=hc, n0=n0, sl=sl: e.tensor_tensor(
                        S1b[:, sl * 512:(sl + 1) * 512],
                        self.bf(oHR + hc * N + n0, 512),
                        S2b[:, 2048 + n0: 2048 + n0 + 512], ALU.mult),
                        waits=whg, inc="G")
                    for dc in range(C):
                        wy = ()
                        if dc == 0:
                            wy = (ev_l, ev_hg)
                            if hc == 0:
                                wy = self.use_bank(dc, wy) + self.use_bank((dc + 1) % 8, ())
                                for dcb in range(2, 8):
                                    wy = wy + self.use_bank(dcb, ())
                                self.bank_free = [None] * 8
                        ev_y = P.emit("tensor", lambda e, buf=buf, dc=dc, hc=hc: e.matmul(
                            self.ps[dc][:],
                            S2b[:, buf * 1024 + dc * 128: buf * 1024 + (dc + 1) * 128],
                            S1b[:, (hc % 2) * 512: (hc % 2 + 1) * 512],
                            start=(hc == 0), stop=(hc == 15)),
                            waits=wy, inc="T" if (dc == C - 1) else None)
                    ev_stripe[buf] = ev_y
                for dc in range(C):
                    ev_add = P.emit("vector", lambda e, dc=dc, n0=n0: e.tensor_tensor(
                        XT[:, dc * N + n0: dc * N + n0 + 512],
                        XT[:, dc * N + n0: dc * N + n0 + 512], self.ps[dc][:], ALU.add),
                        waits=(ev_y,), inc="V")
                    self.free_bank(dc, ev_add)
            ev_gb_free = ("G", P.cnt["G"])

        oHST, oWAB, oWO = 0, 8192, 12288
        ev_hr_free = ("T", P.cnt["T"])
        ev_ab = [None, None]
        ev_wo = [None, None]
        for nb in range(4):
            n0 = nb * 512
            ev_hst = None
            for hc in range(16):
                buf = hc % 2
                wl = (ev_ab[buf],) if ev_ab[buf] is not None else ()
                if nb == 0 and hc < 2:
                    wl = wl + (ev_hr_free,)
                bs = "X7" if buf == 0 else "X8"
                ev_la = P.emit("gpsimd", lambda e, hc=hc, buf=buf: e.dma_start(
                    out=self.bf(oWAB + buf * 2048, 1024).rearrange("p (c k) -> p c k", k=128),
                    in_=w_int[:, hc * 128:(hc + 1) * 128].rearrange("(c p) k -> p c k", p=128)),
                    waits=wl, inc=bs)
                ev_lb = P.emit("gpsimd", lambda e, hc=hc, buf=buf: e.dma_start(
                    out=self.bf(oWAB + buf * 2048 + 1024, 1024).rearrange("p (c k) -> p c k", k=128),
                    in_=w_int[:, 2048 + hc * 128: 2048 + (hc + 1) * 128].rearrange("(c p) k -> p c k", p=128)),
                    inc=bs)
                wa = self.use_bank(6, (ev_lb,))
                wb = self.use_bank(7, ())
                ev_ma = ev_mb = None
                for c in range(C):
                    ev_ma = P.emit("tensor", lambda e, c=c, buf=buf, n0=n0: e.matmul(
                        self.ps[6][:], self.bf(oWAB + buf * 2048 + c * 128, 128),
                        self.bf(oXN4 + c * N + n0, 512),
                        start=(c == 0), stop=(c == C - 1)),
                        waits=(wa if c == 0 else ()), inc="T" if c == C - 1 else None)
                for c in range(C):
                    ev_mb = P.emit("tensor", lambda e, c=c, buf=buf, n0=n0: e.matmul(
                        self.ps[7][:], self.bf(oWAB + buf * 2048 + 1024 + c * 128, 128),
                        self.bf(oXN4 + c * N + n0, 512),
                        start=(c == 0), stop=(c == C - 1)),
                        waits=(wb if c == 0 else ()), inc="T" if c == C - 1 else None)
                ev_ab[buf] = ev_mb
                sl = hc % 2
                ev_sa = P.emit("scalar", lambda e, sl=sl: e.activation(
                    S1b[:, sl * 512:(sl + 1) * 512], self.ps[6][:], AF.Silu),
                    waits=(ev_ma,), inc="S")
                self.free_bank(6, ev_sa)
                ev_hst = P.emit("vector", lambda e, hc=hc, sl=sl: e.tensor_tensor(
                    self.bf(oHST + hc * 512, 512),
                    S1b[:, sl * 512:(sl + 1) * 512], self.ps[7][:], ALU.mult),
                    waits=(ev_sa, ev_mb), inc="V")
                self.free_bank(7, ev_hst)
            ev_y = None
            for hc in range(16):
                buf = hc % 2
                wl = (ev_wo[buf],) if ev_wo[buf] is not None else ()
                ev_l = P.emit("gpsimd", lambda e, hc=hc, buf=buf: e.dma_start(
                    out=self.bf(oWO + buf * 1024, 1024),
                    in_=w_outt[hc * 128:(hc + 1) * 128, :]), waits=wl,
                    inc="X9" if buf == 0 else "X5")
                for dc in range(C):
                    wy = ()
                    if dc == 0:
                        wy = (ev_l, ev_hst)
                        if hc == 0:
                            for dcb in range(8):
                                wy = wy + self.use_bank(dcb, ())
                            self.bank_free = [None] * 8
                    ev_y = P.emit("tensor", lambda e, buf=buf, dc=dc, hc=hc: e.matmul(
                        self.ps[dc][:],
                        self.bf(oWO + buf * 1024 + dc * 128, 128),
                        self.bf(oHST + hc * 512, 512),
                        start=(hc == 0), stop=(hc == 15)),
                        waits=wy, inc="T" if dc == C - 1 else None)
                ev_wo[buf] = ev_y
            for dc in range(C):
                ev_add = P.emit("vector", lambda e, dc=dc, n0=n0: e.tensor_tensor(
                    XT[:, dc * N + n0: dc * N + n0 + 512],
                    XT[:, dc * N + n0: dc * N + n0 + 512], self.ps[dc][:], ALU.add),
                    waits=(ev_y,), inc="V")
                self.free_bank(dc, ev_add)


def make_core_inputs(x_core, W, cst):
    d = dict(cst)
    d["xin"] = np.ascontiguousarray(np.asarray(x_core, np.float32).reshape(NSEQ * N, D))
    d["bvq"] = feat_major_vec(W["L0_attn_bv_q"])
    d["bvk"] = feat_major_vec(W["L0_attn_bv_k"])
    d["bvv"] = feat_major_vec(W["L0_attn_bv_v"])
    d["rw_t"] = np.ascontiguousarray(np.asarray(W["L0_ffn_router_w"], np.float32).T)
    d["l0w1"] = np.ascontiguousarray(np.asarray(W["L0_ffn_w1"], np.float32))
    d["l0w2"] = np.ascontiguousarray(np.asarray(W["L0_ffn_w2"], np.float32))
    d["wq"] = np.ascontiguousarray(np.asarray(W["L1_attn_Wq"], np.float32))
    d["wk"] = np.ascontiguousarray(np.asarray(W["L1_attn_Wk"], np.float32))
    d["wv"] = np.ascontiguousarray(np.asarray(W["L1_attn_Wv"], np.float32))
    d["wo"] = np.ascontiguousarray(np.asarray(W["L1_attn_Wo"], np.float32))
    d["bq"] = feat_major_vec(W["L1_attn_bq"])
    d["bk"] = feat_major_vec(W["L1_attn_bk"])
    d["bv"] = feat_major_vec(W["L1_attn_bv"])
    d["bo"] = feat_major_vec(W["L1_attn_bo"])
    d["w_in_t"] = np.ascontiguousarray(np.asarray(W["L1_ffn_w_in"], np.float32).T)
    d["w_out_t"] = np.ascontiguousarray(np.asarray(W["L1_ffn_w_out"], np.float32).T)
    d["w1_t"] = np.ascontiguousarray(np.asarray(W["L1_ffn_w1"], np.float32).T)
    d["w2_t"] = np.ascontiguousarray(np.asarray(W["L1_ffn_w2"], np.float32).transpose(0, 2, 1))
    d["wg_t"] = np.ascontiguousarray(np.asarray(W["L1_ffn_wg"], np.float32).T)
    d["we_t"] = np.ascontiguousarray(np.asarray(W["L1_ffn_we"], np.float32).T)
    d["gb_b"] = np.tile(np.asarray(W["L1_ffn_gb"], np.float32)[None, :], (128, 1))
    d["eb_b"] = np.tile(np.asarray(W["L1_ffn_eb"], np.float32)[None, :], (128, 1))
    d["ones_f32"] = cst["ones_f32"]
    return d



def kernel(**inputs):
    x = np.asarray(inputs["L0_attn_x"], np.float32)
    cst = host_constants()
    b = Builder()
    in_maps = [make_core_inputs(x[2 * i:2 * i + 2], inputs, cst) for i in range(8)]
    res = run_bass_kernel_spmd(b.nc, in_maps, core_ids=list(range(8)))
    out = np.concatenate([res.results[i]["yout"].reshape(NSEQ, N, D) for i in range(8)], axis=0)
    return out.astype(np.float32)


# version 08 (final)
# speedup vs baseline: 1.2129x; 1.0339x over previous
import numpy as np
import ml_dtypes
import concourse.bass as bass
import concourse.mybir as mybir
from concourse.bass_utils import run_bass_kernel_spmd
from contextlib import ExitStack

F32 = mybir.dt.float32
F16 = mybir.dt.float16
BF16 = mybir.dt.bfloat16
AF = mybir.ActivationFunctionType
ALU = mybir.AluOpType
AX = mybir.AxisListType
BF = ml_dtypes.bfloat16

N = 2048
D = 1024
C = 8
NT = 16
NSEQ = 2
EPS = 1e-5

ENGINES = ("tensor", "vector", "scalar", "gpsimd", "sync")
COMPUTE_SEM_OWNER = {"T": "tensor", "V": "vector", "S": "scalar", "G": "gpsimd"}
DMA_SEMS = ("LW", "LS", "ST", "X0", "X1", "X2", "X3", "X4", "X5", "X6", "X7", "X8", "X9")
ALL_SEMS = tuple(COMPUTE_SEM_OWNER) + DMA_SEMS


class Plan:
    def __init__(self):
        self.steps = {e: [] for e in ENGINES}
        self.cnt = {s: 0 for s in ALL_SEMS}

    def emit(self, eng, fn, waits=(), inc=None):
        waits = tuple(w for w in waits if w is not None)
        ev = None
        amt = 0
        if inc is not None:
            amt = 16 if inc in DMA_SEMS else 1
            self.cnt[inc] += amt
            ev = (inc, self.cnt[inc])
        self.steps[eng].append((waits, fn, inc, amt))
        return ev

    def now(self, *sems):
        return tuple((s, self.cnt[s]) for s in sems)

    def realize(self, nc, sems, go):
        hw = {e: {s: 0 for s in ALL_SEMS} for e in ENGINES}
        with nc.Block() as block:
            for ename in ENGINES:
                def body(eng, ename=ename):
                    if ename == "sync":
                        eng.sem_clear(go)
                        for s in ALL_SEMS:
                            eng.sem_clear(sems[s])
                        eng.nop().then_inc(go, 16)
                    else:
                        eng.wait_ge(go, 16)
                    marks = hw[ename]
                    for waits, fn, inc, amt in self.steps[ename]:
                        for (s, v) in waits:
                            if COMPUTE_SEM_OWNER.get(s) == ename:
                                continue
                            if marks[s] < v:
                                eng.wait_ge(sems[s], v)
                                marks[s] = v
                        ret = fn(eng)
                        if inc is not None:
                            inst = ret[-1] if isinstance(ret, (list, tuple)) else ret
                            inst.then_inc(sems[inc], amt)
                getattr(block, ename)(body)



def np_ln(x):
    m = x.mean(-1, keepdims=True)
    v = x.var(-1, keepdims=True)
    return (x - m) / np.sqrt(v + EPS)

def np_softmax(x, axis=-1):
    m = x.max(axis=axis, keepdims=True)
    e = np.exp(x - m)
    return e / e.sum(axis=axis, keepdims=True)

def np_l0_attn(x, bvq, bvk, bvv):
    bq, bk, bv = np.sign(bvq), np.sign(bvk), np.sign(bvv)
    Q = np.sign(x * bq); K = np.sign(x * bk); V = x * bv
    s = (Q @ K.T) * (D ** -0.5)
    n = x.shape[0]
    causal = np.tril(np.ones((n, n), bool))
    a = 1.0 / (1.0 + np.exp(-4.0 * s))
    a = np.where(causal, a, 0.0)
    return (a @ V).astype(np.float32)

def np_l0_ffn(x, router_w, w1, w2):
    logits = x @ router_w.T
    top2 = np.argsort(-logits, axis=1, kind="stable")[:, :2]
    vals = np.take_along_axis(logits, top2, axis=1)
    w = np_softmax(vals, axis=-1)
    G = np.zeros_like(logits)
    np.put_along_axis(G, top2, w, axis=1)
    out = np.zeros_like(x)
    for e in range(8):
        h = np.maximum(x @ w1[e], 0.0) ** 2
        out += (h @ w2[e]) * G[:, e:e + 1]
    return out.astype(np.float32)

def np_l1_attn(x, Wq, bq, Wk, bk, Wv, bv, Wo, bo):
    CH = 256
    n = x.shape[0]; nch = n // CH
    sc = 1.0 / np.sqrt(D)
    Q = (x @ Wq + bq).reshape(nch, CH, D)
    K = (x @ Wk + bk).reshape(nch, CH, D)
    V = (x @ Wv + bv).reshape(nch, CH, D)
    out = np.zeros((nch, CH, D), np.float32)
    for i in range(nch):
        Gs = np.zeros((CH, CH), np.float32)
        for m in range(nch):
            A = np_softmax(Q[i] @ K[m].T * sc, axis=-1)
            Gs += A * (2.0 if m == i else 1.0)
        out[i] = Gs @ V[i]
    return (out.reshape(n, D) @ Wo + bo).astype(np.float32)

def np_l1_ffn(x, w_in, w_out, w1, w2, wg, we, gb, eb):
    def swiglu(h):
        a, b = np.split(h, 2, axis=-1)
        return (a / (1.0 + np.exp(-a))) * b
    shared = swiglu(x @ w_in.T) @ w_out.T
    gl = x @ wg.T + gb
    gprob = np_softmax(gl, axis=-1)
    gidx = gl.argmax(axis=1)
    el = x @ we.T + eb
    egid = np.arange(8) // 4
    masked = np.where(gidx[:, None] == egid[None, :], el, -np.inf)
    eprob = np_softmax(masked, axis=-1)
    pexp = eprob * np.take_along_axis(gprob, gidx[:, None], axis=1)
    top2 = np.argsort(-pexp, axis=1, kind="stable")[:, :2]
    vals = np.take_along_axis(pexp, top2, axis=1)
    Wse = np.zeros_like(el)
    np.put_along_axis(Wse, top2, vals, axis=1)
    h = swiglu(x @ w1.T)
    routed = np.zeros_like(shared)
    for e in range(8):
        routed += (h * Wse[:, e:e + 1]) @ w2[e].T
    return (shared + routed).astype(np.float32)

def np_module(x, W, blocks):
    if "l0a" in blocks:
        x = x + np_l0_attn(np_ln(x), W["L0_attn_bv_q"], W["L0_attn_bv_k"], W["L0_attn_bv_v"])
    if "l0f" in blocks:
        x = x + np_l0_ffn(np_ln(x), W["L0_ffn_router_w"], W["L0_ffn_w1"], W["L0_ffn_w2"])
    if "l1a" in blocks:
        x = x + np_l1_attn(np_ln(x), W["L1_attn_Wq"], W["L1_attn_bq"], W["L1_attn_Wk"], W["L1_attn_bk"],
                           W["L1_attn_Wv"], W["L1_attn_bv"], W["L1_attn_Wo"], W["L1_attn_bo"])
    if "l1f" in blocks:
        x = x + np_l1_ffn(np_ln(x), W["L1_ffn_w_in"], W["L1_ffn_w_out"], W["L1_ffn_w1"], W["L1_ffn_w2"],
                          W["L1_ffn_wg"], W["L1_ffn_we"], W["L1_ffn_gb"], W["L1_ffn_eb"])
    return x



def host_constants():
    cst = {}
    cst["id_bf"] = np.eye(128, dtype=BF)
    cst["id_f32"] = np.eye(128, dtype=np.float32)
    cst["ones128"] = np.ones((128, 128), dtype=BF)
    cst["ones_f32"] = np.ones((128, 128), dtype=np.float32)
    sel = np.zeros((8, 8 * 128), dtype=BF)
    for e in range(8):
        sel[e, e * 128:(e + 1) * 128] = 1
    cst["sel"] = sel
    masks = np.zeros((128, 4 * 512), dtype=BF)
    i = np.arange(128)[:, None]
    j = np.arange(512)[None, :]
    for oi, off in enumerate((0, 128, 256, 384)):
        masks[:, oi * 512:(oi + 1) * 512] = (i + off <= j).astype(BF)
    cst["masks"] = masks
    return cst

def feat_major_vec(v):
    return np.ascontiguousarray(np.asarray(v, np.float32).reshape(C, 128).T)


class Builder:
    def __init__(self, blocks=("l0a", "l0f", "l1a", "l1f"), debug=False):
        self.blocks = blocks
        self.debug = debug
        self.nc = bass.Bass("TRN2", target_bir_lowering=False)
        self.P = Plan()
        self.stack = ExitStack()
        self.dram = {}
        self.sb = {}
        self.ps = []
        self.bank_free = [None] * 8
        self._build()

    def dram_in(self, name, shape, dtype=F32):
        self.dram[name] = self.nc.dram_tensor(name, list(shape), dtype, kind="ExternalInput").ap()
        return self.dram[name]

    def dram_out(self, name, shape, dtype=F32):
        self.dram[name] = self.nc.dram_tensor(name, list(shape), dtype, kind="ExternalOutput").ap()
        return self.dram[name]

    def sbuf(self, name, shape, dtype):
        self.sb[name] = self.stack.enter_context(self.nc.sbuf_tensor(name, list(shape), dtype))
        return self.sb[name]

    def bf(self, lo, w):
        return self.sb["BFA"][:, lo:lo + w]

    def use_bank(self, k, waits=()):
        if self.bank_free[k] is not None:
            waits = tuple(waits) + (self.bank_free[k],)
        return tuple(waits)

    def free_bank(self, k, ev):
        self.bank_free[k] = ev

    def _build(self):
        nc, P = self.nc, self.P
        self.dram_in("xin", [NSEQ * N, D])
        self.dram_out("yout", [NSEQ * N, D])
        self.dram_in("id_bf", [128, 128], BF16)
        self.dram_in("id_f32", [128, 128])
        self.dram_in("ones128", [128, 128], BF16)
        self.dram_in("ones_f32", [128, 128])
        self.dram_in("sel", [8, 8 * 128], BF16)
        self.dram_in("masks", [128, 4 * 512], BF16)
        self.dram_in("bvq", [128, C]); self.dram_in("bvk", [128, C]); self.dram_in("bvv", [128, C])
        self.dram_in("rw_t", [D, 8])
        self.dram_in("l0w1", [8, D, 2048])
        self.dram_in("l0w2", [8, 2048, D])
        for w in ("wq", "wk", "wv", "wo"):
            self.dram_in(w, [D, D])
        for b in ("bq", "bk", "bv", "bo"):
            self.dram_in(b, [128, C])
        self.dram_in("w_in_t", [D, 4096])
        self.dram_in("w_out_t", [2048, D])
        self.dram_in("w1_t", [D, 4096])
        self.dram_in("w2_t", [8, 2048, D])
        self.dram_in("wg_t", [D, 2])
        self.dram_in("we_t", [D, 8])
        self.dram_in("gb_b", [128, 2])
        self.dram_in("eb_b", [128, 8])

        XT = self.sbuf("XT", [128, C * N], F32)
        for nm in ("M", "R", "S1", "S2"):
            self.sbuf(nm, [128, N], F32)
        self.sbuf("IDB", [128, 128], BF16)
        self.sbuf("IDF", [128, 128], F32)
        self.sbuf("ONE", [128, 128], BF16)
        self.sbuf("ONEF", [128, 128], F32)
        self.sbuf("SELS", [8, 8 * 128], BF16)
        self.sbuf("MSK", [128, 4 * 512], BF16)
        self.sbuf("SGN", [128, 4 * C], F32)
        self.sbuf("BQKV", [128, 4 * C], F32)
        self.sbuf("RWT", [128, C * 8], BF16)
        self.sbuf("RWLO", [128, C * 8], BF16)
        self.sbuf("WGLO", [128, C * 2], BF16)
        self.sbuf("WELO", [128, C * 8], BF16)
        self.sbuf("RWTH", [128, C * 8], F16)
        self.sbuf("WGTH", [128, C * 2], F16)
        self.sbuf("WETH", [128, C * 8], F16)
        self.sbuf("WGT", [128, C * 2], BF16)
        self.sbuf("WET", [128, C * 8], BF16)
        self.sbuf("GBB", [128, 2], F32)
        self.sbuf("EBB", [128, 8], F32)
        self.sbuf("GROUT", [128, 128], F32)
        self.sbuf("G0B", [128, 128], F32)
        self.sbuf("GTR", [8, 2048], BF16)
        self.sbuf("RA", [128, 128], F32)
        self.sbuf("RB", [128, 128], F32)
        self.sbuf("RC", [128, 128], F32)
        self.sbuf("RV1", [128, 16], F32)
        self.sbuf("RV2", [128, 16], F32)
        self.sbuf("RV3", [128, 16], F32)
        self.BFW = 49152
        self.sbuf("BFA", [128, self.BFW], BF16)

        for i in range(8):
            self.ps.append(self.stack.enter_context(nc.psum_tensor(f"ps{i}", [128, 512], F32)))

        self.sems = {s: self.stack.enter_context(nc.semaphore(s)) for s in ALL_SEMS}

        def load(dst, src, q="sync"):
            return P.emit(q, lambda e, d=dst, s=src: e.dma_start(out=d, in_=s),
                          inc="LS" if q == "sync" else "LW")
        load(self.sb["IDB"][:], self.dram["id_bf"])
        load(self.sb["IDF"][:], self.dram["id_f32"])
        load(self.sb["ONE"][:], self.dram["ones128"])
        load(self.sb["ONEF"][:], self.dram["ones_f32"])
        load(self.sb["SELS"][:], self.dram["sel"])
        load(self.sb["MSK"][:], self.dram["masks"])
        load(self.sb["BQKV"][:, 0:C], self.dram["bq"])
        load(self.sb["BQKV"][:, C:2 * C], self.dram["bk"])
        load(self.sb["BQKV"][:, 2 * C:3 * C], self.dram["bv"])
        load(self.sb["BQKV"][:, 3 * C:4 * C], self.dram["bo"])
        load(self.sb["GBB"][:], self.dram["gb_b"])
        load(self.sb["EBB"][:], self.dram["eb_b"])
        load(self.sb["RA"][:, 0:C], self.dram["bvq"])
        load(self.sb["RA"][:, C:2 * C], self.dram["bvk"])
        load(self.sb["RA"][:, 2 * C:3 * C], self.dram["bvv"])
        load(self.sb["RWT"][:].rearrange("p (c k) -> p c k", k=8),
             self.dram["rw_t"].rearrange("(c p) k -> p c k", p=128), q="gpsimd")
        load(self.sb["WGT"][:].rearrange("p (c k) -> p c k", k=2),
             self.dram["wg_t"].rearrange("(c p) k -> p c k", p=128), q="gpsimd")
        load(self.sb["WET"][:].rearrange("p (c k) -> p c k", k=8),
             self.dram["we_t"].rearrange("(c p) k -> p c k", p=128), q="gpsimd")
        CST = self.sbuf("CST", [128, 8], F32)
        for i, v in enumerate((EPS, -0.5, 0.125)):
            P.emit("vector", lambda e, v=v, i=i: e.memset(CST[:, i:i + 1], v), inc="V")
            nc.const_aps.register(F32, v, CST[:, i:i + 1])
        load(self.sb["GROUT"][:, 0:64].rearrange("p (c k) -> p c k", k=8),
             self.dram["rw_t"].rearrange("(c p) k -> p c k", p=128))
        load(self.sb["G0B"][:, 0:16].rearrange("p (c k) -> p c k", k=2),
             self.dram["wg_t"].rearrange("(c p) k -> p c k", p=128))
        load(self.sb["RB"][:, 0:64].rearrange("p (c k) -> p c k", k=8),
             self.dram["we_t"].rearrange("(c p) k -> p c k", p=128))
        ev_wf = P.now("LS", "LW")
        P.emit("vector", lambda e: e.tensor_tensor(
            self.sb["RWLO"][:], self.sb["GROUT"][:, 0:64], self.sb["RWT"][:], ALU.subtract),
            waits=ev_wf)
        P.emit("vector", lambda e: e.tensor_tensor(
            self.sb["WGLO"][:], self.sb["G0B"][:, 0:16], self.sb["WGT"][:], ALU.subtract))
        ev_wlo = P.emit("vector", lambda e: e.tensor_tensor(
            self.sb["WELO"][:], self.sb["RB"][:, 0:64], self.sb["WET"][:], ALU.subtract), inc="V")
        P.emit("vector", lambda e: e.tensor_copy(self.sb["RWTH"][:], self.sb["RWT"][:]), waits=ev_wf)
        P.emit("vector", lambda e: e.tensor_copy(self.sb["WGTH"][:], self.sb["WGT"][:]))
        ev_wlo = P.emit("vector", lambda e: e.tensor_copy(self.sb["WETH"][:], self.sb["WET"][:]), inc="V")
        ev_const = P.now("LS", "LW") + (ev_wlo,)
        ev_sgn = P.emit("scalar", lambda e: e.activation(
            self.sb["SGN"][:, 0:3 * C], self.sb["RA"][:, 0:3 * C], AF.Sign),
            waits=ev_const, inc="S")
        self.ev_setup = (P.emit("vector", lambda e: e.tensor_tensor(
            self.sb["SGN"][:, 3 * C:4 * C], self.sb["SGN"][:, 0:C],
            self.sb["SGN"][:, C:2 * C], ALU.mult), waits=(ev_sgn,), inc="V"),) + ev_const

        for seq in range(NSEQ):
            self.build_seq(seq)

        final = P.now("T", "V", "S", "G", "LW", "LS", "ST",
                      "X0", "X1", "X2", "X3", "X4", "X5", "X6", "X7", "X8", "X9")
        def _clear_all(e):
            ret = None
            for s in ALL_SEMS:
                ret = e.sem_clear(self.sems[s])
            return ret
        P.emit("sync", lambda e: _clear_all(e), waits=final)
        self.go_sem = self.stack.enter_context(nc.semaphore("GO"))
        P.realize(nc, self.sems, self.go_sem)
        self.stack.close()

    def build_seq(self, seq):
        self.load_input(seq)
        if "l0a" in self.blocks:
            self.ln_stats()
            self.block_l0_attn(seq)
        if "l0f" in self.blocks:
            self.ln_stats()
            self.block_l0_ffn(seq)
        if "l1a" in self.blocks:
            self.ln_stats()
            self.block_l1_attn(seq)
        if "l1f" in self.blocks:
            self.ln_stats()
            self.block_l1_ffn(seq)
        self.store_output(seq)

    def load_input(self, seq):
        P = self.P
        XT, IDF = self.sb["XT"], self.sb["IDF"]
        xin = self.dram["xin"]
        BFAF = self.sb["BFA"][:].bitcast(F32)
        nstg = 6
        stage = [BFAF[:, i * 1024:(i + 1) * 1024] for i in range(nstg)]
        ev_done = [None] * nstg
        w0 = (P.now("T", "V", "S", "G") + self.ev_setup
              + P.now("X0", "X1", "X2", "X3", "X4", "X5"))
        for t in range(NT):
            buf = stage[t % nstg]
            w = w0 if t < nstg else (ev_done[t % nstg],)
            ev_ld = P.emit("sync", lambda e, b=buf, t=t: e.dma_start(
                out=b, in_=xin[seq * N + t * 128: seq * N + (t + 1) * 128, :]),
                waits=w, inc=f"X{t % nstg}")
            for c in range(C):
                bank = (t * C + c) % 4
                tw = self.use_bank(bank, (ev_ld,))
                ev_tp = P.emit("tensor", lambda e, b=buf, c=c, k=bank: e.transpose(
                    self.ps[k][:, 0:128], b[:, c * 128:(c + 1) * 128], IDF[:]),
                    waits=tw, inc="T")
                ev_ev = P.emit("vector", lambda e, c=c, t=t, k=bank: e.tensor_copy(
                    XT[:, c * N + t * 128: c * N + (t + 1) * 128], self.ps[k][:, 0:128]),
                    waits=(ev_tp,), inc="V")
                self.free_bank(bank, ev_ev)
            ev_done[t % nstg] = ev_ev

    def store_output(self, seq):
        P = self.P
        XT, IDF = self.sb["XT"], self.sb["IDF"]
        yout = self.dram["yout"]
        BFAF = self.sb["BFA"][:].bitcast(F32)
        nstg = 6
        stage = [BFAF[:, i * 1024:(i + 1) * 1024] for i in range(nstg)]
        ev_store = [None] * nstg
        w0 = P.now("T", "V", "S", "G")
        for t in range(NT):
            buf = stage[t % nstg]
            for c in range(C):
                bank = (t * C + c) % 4
                w = self.use_bank(bank, w0 if (t == 0 and c == 0) else ())
                if c == 0 and ev_store[t % nstg] is not None:
                    w = w + (ev_store[t % nstg],)
                ev_tp = P.emit("tensor", lambda e, c=c, t=t, k=bank: e.transpose(
                    self.ps[k][:, 0:128], XT[:, c * N + t * 128: c * N + (t + 1) * 128], IDF[:]),
                    waits=w, inc="T")
                ev_ev = P.emit("vector", lambda e, c=c, k=bank, b=buf: e.tensor_copy(
                    b[:, c * 128:(c + 1) * 128], self.ps[k][:, 0:128]),
                    waits=(ev_tp,), inc="V")
                self.free_bank(bank, ev_ev)
            ev_store[t % nstg] = P.emit("sync", lambda e, b=buf, t=t: e.dma_start(
                out=yout[seq * N + t * 128: seq * N + (t + 1) * 128, :], in_=b),
                waits=(ev_ev,), inc=f"X{t % nstg}")

    def ln_stats(self):
        P = self.P
        XT, M, R, S1, ONE = (self.sb[k] for k in ("XT", "M", "R", "S1", "ONE"))
        ONEF = self.sb["ONEF"]
        off = self.BFW - 4 * N
        XH = [self.bf(off + i * N, N) for i in range(2)]
        SQ = [self.bf(off + (2 + i) * N, N) for i in range(2)]
        w0 = P.now("T", "S", "G")
        ev_sq = [None, None]
        last_mm = None
        for c in range(C):
            b = c % 2
            w = w0 if c < 2 else ()
            if ev_sq[b] is not None:
                w = w + (ev_sq[b],)
            if last_mm is not None and c >= 2:
                w = w + (last_mm,)
            ev_c = P.emit("vector", lambda e, c=c, b=b: e.tensor_copy(
                XH[b], XT[:, c * N:(c + 1) * N]), waits=w, inc="V")
            ev_q = P.emit("scalar", lambda e, b=b: e.activation(SQ[b], XH[b], AF.Square),
                          waits=(ev_c,), inc="S")
            ev_sq[b] = ev_q
            for g in range(4):
                bw = self.use_bank(g, w0) if c == 0 else ()
                P.emit("tensor", lambda e, g=g, c=c: e.matmul(
                    self.ps[g][:], ONEF[:], XT[:, c * N + g * 512: c * N + (g + 1) * 512],
                    start=(c == 0), stop=(c == C - 1)), waits=bw)
                bw2 = self.use_bank(4 + g, (ev_q,)) if c == 0 else (ev_q,)
                evm = P.emit("tensor", lambda e, b=b, g=g, c=c: e.matmul(
                    self.ps[4 + g][:], ONE[:], SQ[b][:, g * 512:(g + 1) * 512],
                    start=(c == 0), stop=(c == C - 1)), waits=bw2,
                    inc="T" if g == 3 else None)
                if g == 3:
                    last_mm = ("T", P.cnt["T"])
        ev_mm = last_mm
        ev_m = []
        for g in range(4):
            ev_m.append(P.emit("vector", lambda e, g=g: e.tensor_scalar(
                M[:, g * 512:(g + 1) * 512], self.ps[g][:], 1.0 / D, None, ALU.mult),
                waits=(ev_mm,), inc="V"))
        P.emit("vector", lambda e: e.drain())
        ev_m2 = P.emit("vector", lambda e: e.tensor_tensor(S1[:], M[:], M[:], ALU.mult), inc="V")
        P.emit("vector", lambda e: e.drain())
        ev_var = []
        for g in range(4):
            ev_var.append(P.emit("vector", lambda e, g=g: e.scalar_tensor_tensor(
                S1[:, g * 512:(g + 1) * 512], self.ps[4 + g][:], 1.0 / D,
                S1[:, g * 512:(g + 1) * 512], ALU.mult, ALU.subtract), inc="V"))
        for g in range(4):
            self.free_bank(g, ev_m[g])
            self.free_bank(4 + g, ev_var[g])
        ev_ln = P.emit("scalar", lambda e: e.activation(S1[:], S1[:], AF.Ln, bias=EPS),
                       waits=tuple(ev_var), inc="S")
        self.ev_lnr = P.emit("scalar", lambda e: e.activation(R[:], S1[:], AF.Exp, scale=-0.5),
                             waits=(ev_ln,), inc="S")
        self.ev_lnm = ev_m[-1]

    def block_l0_attn(self, seq):
        P = self.P
        XT, M, R, S1, S2 = (self.sb[k] for k in ("XT", "M", "R", "S1", "S2"))
        IDB, MSK, SGN = self.sb["IDB"], self.sb["MSK"], self.sb["SGN"]
        oZS, oV, oZQ, oAST, oSG = 0, 16384, 32768, 36864, 45056
        oVSC = oSG + 1024

        ev_prev_zs = None
        ev_prev_tp = None
        ev_vt = None
        for c in range(C):
            w = (self.ev_lnr, self.ev_lnm)
            if ev_prev_zs is not None:
                w = w + (ev_prev_zs,)
            P.emit("vector", lambda e, c=c: e.tensor_tensor(
                S1[:], XT[:, c * N:(c + 1) * N], M[:], ALU.subtract), waits=w)
            ev_rs = P.emit("vector", lambda e, c=c: e.tensor_scalar(
                S2[:], R[:], SGN[:, 2 * C + c:2 * C + c + 1], None, ALU.mult), inc="V")
            P.emit("vector", lambda e: e.drain())
            wv = (ev_prev_tp,) if ev_prev_tp is not None else ()
            ev_vsc = P.emit("vector", lambda e: e.tensor_tensor(
                self.bf(oVSC, 2048), S1[:], S2[:], ALU.mult), waits=wv, inc="V")
            ev_prev_zs = P.emit("scalar", lambda e, c=c: e.activation(
                self.bf(oZS + c * N, N), S1[:], AF.Sign,
                scale=SGN[:, 3 * C + c:3 * C + c + 1]), waits=(ev_rs,), inc="S")
            for t in range(NT):
                bank = (c * NT + t) % 2
                tw = self.use_bank(bank, (ev_vsc,))
                ev_tp = P.emit("tensor", lambda e, t=t, k=bank: e.transpose(
                    self.ps[k][:, 0:64].bitcast(BF16),
                    self.bf(oVSC + t * 128, 128), IDB[:]), waits=tw, inc="T")
                ev_vt = P.emit("vector", lambda e, t=t, c=c, k=bank: e.tensor_copy(
                    self.bf(oV + t * 1024 + c * 128, 128),
                    self.ps[k][:, 0:64].bitcast(BF16)), waits=(ev_tp,), inc="V")
                self.free_bank(bank, ev_vt)
            ev_prev_tp = ("T", P.cnt["T"])
        ev_zs_all = ("S", P.cnt["S"])
        ev_vt_all = ("V", P.cnt["V"])

        for qb in range(4):
            q0 = qb * 512
            nkt = qb * 4 + 4
            wT = P.now("T")
            ev_zq = None
            ev_sub_prev = None
            for c in range(C):
                w = wT + (self.ev_lnr, self.ev_lnm)
                if ev_zq is not None:
                    w = w + (ev_zq,)
                ev_sub = P.emit("vector", lambda e, c=c, q0=q0: e.tensor_tensor(
                    S1[:, 0:512], XT[:, c * N + q0: c * N + q0 + 512],
                    M[:, q0:q0 + 512], ALU.subtract), waits=w, inc="V")
                ev_zq = P.emit("scalar", lambda e, c=c: e.activation(
                    self.bf(oZQ + c * 512, 512), S1[:, 0:512], AF.Sign),
                    waits=(ev_sub,), inc="S")
            ev_mask = [None, None]
            SCB = (0, 1, 6, 7)
            for kt in range(nkt):
                k0 = kt * 128
                bank = SCB[kt % 4]
                w = self.use_bank(bank, (ev_zs_all, ev_zq))
                ev_mm = None
                for c in range(C):
                    ev_mm = P.emit("tensor", lambda e, c=c, kt=kt, k=bank: e.matmul(
                        self.ps[k][:],
                        self.bf(oZS + c * N + kt * 128, 128),
                        self.bf(oZQ + c * 512, 512),
                        start=(c == 0), stop=(c == C - 1)),
                        waits=(w if c == 0 else ()), inc="T" if c == C - 1 else None)
                off = k0 - q0
                if off < 0:
                    ev_a = P.emit("scalar", lambda e, kt=kt, k=bank: e.activation(
                        self.bf(oAST + kt * 512, 512), self.ps[k][:], AF.Sigmoid,
                        scale=0.125), waits=(ev_mm,), inc="S")
                else:
                    oi = off // 128
                    sb_ = kt % 2
                    ws = (ev_mm,)
                    if ev_mask[sb_] is not None:
                        ws = ws + (ev_mask[sb_],)
                    ev_sg = P.emit("scalar", lambda e, k=bank, sb_=sb_: e.activation(
                        self.bf(oSG + sb_ * 512, 512), self.ps[k][:], AF.Sigmoid, scale=0.125),
                        waits=ws, inc="S")
                    ev_a = P.emit("vector", lambda e, kt=kt, oi=oi, sb_=sb_: e.tensor_tensor(
                        self.bf(oAST + kt * 512, 512), self.bf(oSG + sb_ * 512, 512),
                        MSK[:, oi * 512:(oi + 1) * 512], ALU.mult),
                        waits=(ev_sg,), inc="V")
                    ev_mask[sb_] = ev_a
                self.free_bank(bank, ev_a)
            ev_attn_all = P.now("S", "V")
            for eg in range(2):
                for e4 in range(4):
                    ec = eg * 4 + e4
                    bank = 2 + e4
                    w = self.use_bank(bank, ev_attn_all + (ev_vt_all,))
                    ev_mm = None
                    for kt in range(nkt):
                        ev_mm = P.emit("tensor", lambda e, kt=kt, ec=ec, k=bank, nkt=nkt: e.matmul(
                            self.ps[k][:],
                            self.bf(oV + kt * 1024 + ec * 128, 128),
                            self.bf(oAST + kt * 512, 512),
                            start=(kt == 0), stop=(kt == nkt - 1)),
                            waits=(w if kt == 0 else ()), inc="T" if kt == nkt - 1 else None)
                    ev_add = P.emit("vector", lambda e, ec=ec, k=bank, q0=q0: e.tensor_tensor(
                        XT[:, ec * N + q0: ec * N + q0 + 512],
                        XT[:, ec * N + q0: ec * N + q0 + 512], self.ps[k][:], ALU.add),
                        waits=(ev_mm,), inc="V")
                    self.free_bank(bank, ev_add)

    def xn_snapshot(self, oXN, routers=()):
        P = self.P
        XT, M, R, S1, S2 = (self.sb[k] for k in ("XT", "M", "R", "S1", "S2"))
        IDF = self.sb["IDF"]
        S2h = S2[:].bitcast(F16)
        ev = None
        ev_rt = None
        rbw = {}
        for (whi, wlo, whih, k, banks4, dst) in routers:
            for b in banks4:
                rbw[b] = self.use_bank(b, ())
        for c in range(C):
            w = (self.ev_lnr, self.ev_lnm)
            if routers and c >= 1:
                w = w + (ev_rt,)
            P.emit("vector", lambda e, c=c: e.tensor_tensor(
                S1[:], XT[:, c * N:(c + 1) * N], M[:], ALU.subtract), waits=w)
            P.emit("vector", lambda e: e.drain())
            P.emit("vector", lambda e, c=c: e.tensor_tensor(S1[:], S1[:], R[:], ALU.mult))
            P.emit("vector", lambda e: e.drain())
            ev = P.emit("vector", lambda e, c=c, oXN=oXN: e.tensor_copy(
                self.bf(oXN + c * N, N), S1[:]), inc="V")
            if routers:
                P.emit("vector", lambda e: e.drain())
                ev_lo = P.emit("vector", lambda e, c=c, oXN=oXN: e.tensor_tensor(
                    S2h[:, 0:N], S1[:], self.bf(oXN + c * N, N), ALU.subtract), inc="V")
                for (whi, wlo, whih, k, banks4, dst) in routers:
                    for nb in range(4):
                        bank = banks4[nb]
                        terms = ((whi, True), (wlo, True), (whih, False))
                        nterm = len(terms)
                        for ti, (wp, is_hi) in enumerate(terms):
                            first = (c == 0 and ti == 0)
                            last = (c == C - 1 and ti == nterm - 1)
                            wmm = (rbw[bank] if first else ()) + ((ev,) if is_hi else (ev_lo,))
                            def mk(wp=wp, is_hi=is_hi, c=c, nb=nb, bank=bank, k=k, first=first, last=last, oXN=oXN):
                                def f(e):
                                    rhs = (self.bf(oXN + c * N + nb * 512, 512) if is_hi
                                           else S2h[:, nb * 512:(nb + 1) * 512])
                                    return e.matmul(self.ps[bank][0:k, :], wp[:, c * k:(c + 1) * k],
                                                    rhs, start=first, stop=last)
                                return f
                            ev_rt = P.emit("tensor", mk(),
                                           waits=wmm, inc="T" if (nb == 3 and ti == nterm - 1) else None)
        for (whi, wlo, whih, k, banks4, dst) in routers:
            ev_ev = None
            for nb in range(4):
                ev_ev = P.emit("vector", lambda e, nb=nb, k=k, bank=banks4[nb]: e.tensor_copy(
                    S1[0:k, nb * 512:(nb + 1) * 512], self.ps[bank][0:k, :]),
                    waits=(ev_rt,), inc="V")
                self.free_bank(banks4[nb], ev_ev)
            for t in range(NT):
                tb = (t % 2)
                tw = self.use_bank(tb, (ev_ev,))
                ev_tp = P.emit("tensor", lambda e, t=t, k=k, tb=tb: e.transpose(
                    self.ps[tb][:, 0:k], S1[0:k, t * 128:(t + 1) * 128], IDF[0:k, 0:k]),
                    waits=tw, inc="T")
                ev_rt = P.emit("vector", lambda e, t=t, k=k, tb=tb, dst=dst: e.tensor_copy(
                    dst[:, t * k:(t + 1) * k], self.ps[tb][:, 0:k]),
                    waits=(ev_tp,), inc="V")
                self.free_bank(tb, ev_rt)
        return ev, ev_rt

    def top2_gates(self, ev_logits):
        P = self.P
        NEG = -1e30
        GROUT, G0B = self.sb["GROUT"], self.sb["G0B"]
        RA, RB, RC = self.sb["RA"], self.sb["RB"], self.sb["RC"]
        RV1, RV2, RV3 = self.sb["RV1"], self.sb["RV2"], self.sb["RV3"]
        def g3(ap):
            return ap[:].rearrange("p (t k) -> p t k", k=8)
        def b3(ap):
            return ap[:].rearrange("p (t o) -> p t o", o=1).broadcast_to([128, 16, 8])
        P.emit("vector", lambda e: e.tensor_reduce(RV1[:], g3(GROUT), axis=AX.X, op=ALU.max),
               waits=(ev_logits,))
        P.emit("vector", lambda e: e.drain())
        P.emit("vector", lambda e: e.tensor_tensor(g3(RA), g3(GROUT), b3(RV1), ALU.is_ge))
        P.emit("vector", lambda e: e.drain())
        P.emit("vector", lambda e: e.scalar_tensor_tensor(g3(RB), g3(RA), NEG, g3(GROUT),
                                                          ALU.mult, ALU.add))
        P.emit("vector", lambda e: e.drain())
        P.emit("vector", lambda e: e.tensor_reduce(RV2[:], g3(RB), axis=AX.X, op=ALU.max))
        P.emit("vector", lambda e: e.drain())
        P.emit("vector", lambda e: e.tensor_tensor(g3(RC), g3(RB), b3(RV2), ALU.is_ge))
        ev_sm = P.emit("vector", lambda e: e.tensor_tensor(RV3[:], RV1[:], RV2[:], ALU.subtract),
                       inc="V")
        ev_w1 = P.emit("scalar", lambda e: e.activation(RV2[:], RV3[:], AF.Sigmoid),
                       waits=(ev_sm,), inc="S")
        P.emit("vector", lambda e: e.tensor_tensor(g3(RB), g3(RA), b3(RV2), ALU.mult),
               waits=(ev_w1,))
        P.emit("vector", lambda e: e.drain())
        P.emit("vector", lambda e: e.tensor_tensor(g3(RA), g3(RC), b3(RV2), ALU.mult))
        P.emit("vector", lambda e: e.drain())
        P.emit("vector", lambda e: e.tensor_tensor(RC[:], RC[:], RA[:], ALU.subtract))
        P.emit("vector", lambda e: e.drain())
        return P.emit("vector", lambda e: e.tensor_tensor(G0B[:], RB[:], RC[:], ALU.add), inc="V")

    def gates_transpose(self, ev_gates):
        P = self.P
        G0B, GTR, IDF = self.sb["G0B"], self.sb["GTR"], self.sb["IDF"]
        ev = None
        for t in range(NT):
            bank = 6 + t % 2
            w = self.use_bank(bank, (ev_gates,))
            ev_tp = P.emit("tensor", lambda e, t=t, k=bank: e.transpose(
                self.ps[k][0:8, 0:128], G0B[:, t * 8:(t + 1) * 8], IDF[:]),
                waits=w, inc="T")
            ev = P.emit("vector", lambda e, t=t, k=bank: e.tensor_copy(
                GTR[0:8, t * 128:(t + 1) * 128], self.ps[k][0:8, 0:128]),
                waits=(ev_tp,), inc="V")
            self.free_bank(bank, ev)
        return ev

    def block_l0_ffn(self, seq):
        P = self.P
        XT, M, R, S1, S2 = (self.sb[k] for k in ("XT", "M", "R", "S1", "S2"))
        RWT, GROUT, GTR, SELS = self.sb["RWT"], self.sb["GROUT"], self.sb["GTR"], self.sb["SELS"]
        oXN, oW1, oHG, oW2 = 0, 16384, 32768, 40960
        S1b = S1[:].bitcast(BF16)
        S2b = S2[:].bitcast(BF16)
        l0w1, l0w2 = self.dram["l0w1"], self.dram["l0w2"]

        ev_xn, ev_rt = self.xn_snapshot(oXN, routers=(
            (RWT[:], self.sb["RWLO"][:], self.sb["RWTH"][:], 8, (4, 5, 6, 7), GROUT[:]),))
        ev_lg = ev_rt
        ev_gates = self.top2_gates(ev_lg)
        ev_gtr = self.gates_transpose(ev_gates)

        ev_w1_free = None
        ev_gb_free = None
        ev_w2buf = [None, None]
        ev_hg_rd = None
        for ex in range(8):
            wv = (ev_w1_free,) if ev_w1_free is not None else P.now("T", "V", "S")
            ev_w1l = None
            for c in range(C):
                ev_w1l = P.emit("gpsimd", lambda e, ex=ex, c=c: e.dma_start(
                    out=self.bf(oW1 + c * 2048, 2048),
                    in_=l0w1[ex, c * 128:(c + 1) * 128, :]), waits=wv if c == 0 else (),
                    inc="X6")
            wg = (ev_gtr,) if ex == 0 else ()
            if ev_gb_free is not None:
                wg = wg + (ev_gb_free,)
            ev_gb = None
            for g in range(4):
                bank = 6 + g % 2
                bwg = self.use_bank(bank, wg if g < 2 else ())
                ev_bc = P.emit("tensor", lambda e, ex=ex, g=g, k=bank: e.matmul(
                    self.ps[k][:], SELS[:, ex * 128:(ex + 1) * 128],
                    GTR[:, g * 512:(g + 1) * 512], start=True, stop=True),
                    waits=bwg, inc="T")
                ev_gb = P.emit("vector", lambda e, g=g, k=bank: e.tensor_copy(
                    S2b[:, g * 512:(g + 1) * 512], self.ps[k][:]),
                    waits=(ev_bc,), inc="V")
                self.free_bank(bank, ev_gb)

            for nb in range(4):
                n0 = nb * 512
                ev_l_list = [None] * 4
                for g in range(2):
                    buf = g % 2
                    wl = (ev_w2buf[buf],) if ev_w2buf[buf] is not None else P.now("V", "S")
                    ev_l_list[g] = P.emit("gpsimd", lambda e, ex=ex, g=g, buf=buf: e.dma_start(
                        out=self.bf(oW2 + buf * 4096, 4096).rearrange("p (s d) -> p s d", d=1024),
                        in_=l0w2[ex, g * 512:(g + 1) * 512, :].rearrange("(s p) d -> p s d", p=128)),
                        waits=wl, inc="X7" if buf == 0 else "X8")
                ev_hg = None
                ev_hg_list = []
                for hc in range(16):
                    bank = 4 + hc % 4
                    w = self.use_bank(bank, ((("X6", P.cnt["X6"]),) if hc == 0 and nb == 0 else ()))
                    ev_h = None
                    for c in range(C):
                        ev_h = P.emit("tensor", lambda e, c=c, hc=hc, k=bank, n0=n0: e.matmul(
                            self.ps[k][:],
                            self.bf(oW1 + c * 2048 + hc * 128, 128),
                            self.bf(oXN + c * N + n0, 512),
                            start=(c == 0), stop=(c == C - 1)),
                            waits=(w if c == 0 else ()), inc="T" if c == C - 1 else None)
                    if nb == 3 and hc == 15:
                        self._w1_last_use = ev_h
                    sl = hc % 2
                    ev_rl = P.emit("scalar", lambda e, k=bank, sl=sl: e.activation(
                        S1b[:, sl * 512:(sl + 1) * 512], self.ps[k][:], AF.Relu),
                        waits=(ev_h,), inc="S")
                    self.free_bank(bank, ev_rl)
                    ev_sq = P.emit("vector", lambda e, sl=sl: e.tensor_tensor(
                        S1b[:, (2 + sl) * 512:(3 + sl) * 512],
                        S1b[:, sl * 512:(sl + 1) * 512],
                        S1b[:, sl * 512:(sl + 1) * 512], ALU.mult),
                        waits=(ev_rl,), inc="V")
                    whg = (ev_sq, ev_gb)
                    if ev_hg_rd is not None:
                        whg = whg + (ev_hg_rd,)
                    ev_hg = P.emit("gpsimd", lambda e, hc=hc, sl=sl, n0=n0: e.tensor_tensor(
                        self.bf(oHG + hc * 512, 512),
                        S1b[:, (2 + sl) * 512:(3 + sl) * 512],
                        S2b[:, n0:n0 + 512],
                        ALU.mult), waits=whg, inc="G")
                    ev_hg_list.append(ev_hg)
                ev_y = None
                for g in range(4):
                    buf = g % 2
                    if ev_l_list[g] is None:
                        ev_l_list[g] = P.emit("gpsimd", lambda e, ex=ex, g=g, buf=buf: e.dma_start(
                            out=self.bf(oW2 + buf * 4096, 4096).rearrange("p (s d) -> p s d", d=1024),
                            in_=l0w2[ex, g * 512:(g + 1) * 512, :].rearrange("(s p) d -> p s d", p=128)),
                            waits=(ev_w2buf[buf],) if ev_w2buf[buf] is not None else (),
                            inc="X7" if buf == 0 else "X8")
                    ev_l = ev_l_list[g]
                    for si in range(4):
                        hc = g * 4 + si
                        for dc in range(C):
                            wy = ()
                            if si == 0 and dc == 0:
                                wy = (ev_l, ("G", P.cnt["G"]))
                                if g == 0:
                                    wy = self.use_bank(dc, wy)
                            elif si == 0 and g == 0:
                                wy = self.use_bank(dc, ())
                            ev_y = P.emit("tensor", lambda e, buf=buf, si=si, dc=dc, hc=hc: e.matmul(
                                self.ps[dc][:],
                                self.bf(oW2 + buf * 4096 + si * 1024 + dc * 128, 128),
                                self.bf(oHG + hc * 512, 512),
                                start=(hc == 0), stop=(hc == 15)),
                                waits=wy, inc="T" if (si == 3 and dc == C - 1) else None)
                    ev_w2buf[buf] = ev_y
                ev_hg_rd = ev_y
                for dc in (7, 6, 5, 4, 3, 2, 1, 0):
                    ev_add = P.emit("vector", lambda e, dc=dc, n0=n0: e.tensor_tensor(
                        XT[:, dc * N + n0: dc * N + n0 + 512],
                        XT[:, dc * N + n0: dc * N + n0 + 512], self.ps[dc][:], ALU.add),
                        waits=(ev_y,), inc="V")
                    self.free_bank(dc, ev_add)
            ev_w1_free = self._w1_last_use
            ev_gb_free = ("G", P.cnt["G"])

    def xn_half(self, oXN, h):
        P = self.P
        XT, M, R, S2 = (self.sb[k] for k in ("XT", "M", "R", "S2"))
        lo = h * 1024
        ev = None
        for c in range(C):
            w = (self.ev_lnr, self.ev_lnm) + (P.now("T", "G") if c == 0 else ())
            P.emit("vector", lambda e, c=c, lo=lo: e.tensor_tensor(
                S2[:, 0:1024], XT[:, c * N + lo: c * N + lo + 1024],
                M[:, lo:lo + 1024], ALU.subtract), waits=w)
            P.emit("vector", lambda e: e.drain())
            ev = P.emit("vector", lambda e, c=c, oXN=oXN, lo=lo: e.tensor_tensor(
                self.bf(oXN + c * 1024, 1024), S2[:, 0:1024], R[:, lo:lo + 1024],
                ALU.mult), inc="V")
        return ev

    def load_wslot(self, oW, wname):
        P = self.P
        wt = self.dram[wname]
        wv = P.now("T", "V", "S", "G")
        ev = None
        for c in range(C):
            ev = P.emit("gpsimd", lambda e, c=c, oW=oW, wt=wt: e.dma_start(
                out=self.bf(oW + c * 1024, 1024), in_=wt[c * 128:(c + 1) * 128, :]),
                waits=wv if c == 0 else (), inc="X6")
        return ev

    def proj_half(self, oW, oXN, dst, h, bias_col, ev_w, ev_xn, to_token_major=False):
        P = self.P
        BQKV, IDB = self.sb["BQKV"], self.sb["IDB"]
        S1b = self.sb["S1"][:].bitcast(BF16)
        ev_out = None
        for eo in range(C):
            for blk in range(2):
                bank = (eo * 2 + blk) % 4
                w = self.use_bank(bank, (ev_w, ev_xn))
                ev_mm = None
                for c in range(C):
                    ev_mm = P.emit("tensor", lambda e, c=c, eo=eo, blk=blk, k=bank, oW=oW, oXN=oXN: e.matmul(
                        self.ps[k][:],
                        self.bf(oW + c * 1024 + eo * 128, 128),
                        self.bf(oXN + c * 1024 + blk * 512, 512),
                        start=(c == 0), stop=(c == C - 1)),
                        waits=(w if c == 0 else ()), inc="T" if c == C - 1 else None)
                if not to_token_major:
                    ev_out = P.emit("scalar", lambda e, eo=eo, blk=blk, k=bank, dst=dst, bias_col=bias_col: e.activation(
                        self.bf(dst + eo * 1024 + blk * 512, 512), self.ps[k][:], AF.Identity,
                        bias=BQKV[:, bias_col * C + eo: bias_col * C + eo + 1]),
                        waits=(ev_mm,), inc="S")
                    self.free_bank(bank, ev_out)
                else:
                    ev_st = P.emit("scalar", lambda e, k=bank, bias_col=bias_col, eo=eo: e.activation(
                        S1b[:, 0:512], self.ps[k][:], AF.Identity,
                        bias=BQKV[:, bias_col * C + eo: bias_col * C + eo + 1]),
                        waits=(ev_mm,), inc="S")
                    self.free_bank(bank, ev_st)
                    for tt in range(4):
                        tb = 4 + (tt % 2)
                        tw = self.use_bank(tb, (ev_st,))
                        ev_tp = P.emit("tensor", lambda e, tt=tt, k=tb: e.transpose(
                            self.ps[k][:, 0:64].bitcast(BF16),
                            S1b[:, tt * 128:(tt + 1) * 128], IDB[:]), waits=tw, inc="T")
                        lt = blk * 4 + tt
                        ev_out = P.emit("vector", lambda e, lt=lt, eo=eo, k=tb, dst=dst: e.tensor_copy(
                            self.bf(dst + lt * 1024 + eo * 128, 128),
                            self.ps[k][:, 0:64].bitcast(BF16)), waits=(ev_tp,), inc="V")
                        self.free_bank(tb, ev_out)
        return ev_out

    def block_l1_attn(self, seq):
        P = self.P
        XT, M, R, S1, S2 = (self.sb[k] for k in ("XT", "M", "R", "S1", "S2"))
        BQKV, IDB, RV1, RV2, RV3 = (self.sb[k] for k in ("BQKV", "IDB", "RV1", "RV2", "RV3"))
        oK, oQ, oV, oW, oXN3 = 0, 16384, 24576, 32768, 40960
        S1b = S1[:].bitcast(BF16)
        SCALE = 1.0 / 32.0

        for kh in range(2):
            ev_xn = self.xn_half(oXN3, kh)
            ev_w = self.load_wslot(oW, "wk")
            for eo in range(C):
                for blk in range(2):
                    bank = (eo * 2 + blk) % 4
                    w = self.use_bank(bank, (ev_w, ev_xn))
                    ev_mm = None
                    for c in range(C):
                        ev_mm = P.emit("tensor", lambda e, c=c, eo=eo, blk=blk, k=bank: e.matmul(
                            self.ps[k][:],
                            self.bf(oW + c * 1024 + eo * 128, 128),
                            self.bf(oXN3 + c * 1024 + blk * 512, 512),
                            start=(c == 0), stop=(c == C - 1)),
                            waits=(w if c == 0 else ()), inc="T" if c == C - 1 else None)
                    ev_k = P.emit("scalar", lambda e, eo=eo, blk=blk, k=bank, kh=kh: e.activation(
                        self.bf(oK + eo * 2048 + kh * 1024 + blk * 512, 512),
                        self.ps[k][:], AF.Identity,
                        bias=BQKV[:, C + eo: C + eo + 1]), waits=(ev_mm,), inc="S")
                    self.free_bank(bank, ev_k)
        ev_k_all = P.now("S")

        for h in range(2):
            ev_xn = self.xn_half(oXN3, h)
            ev_w = self.load_wslot(oW, "wq")
            self.proj_half(oW, oXN3, oQ, h, 0, ev_w, ev_xn)
            ev_w = self.load_wslot(oW, "wv")
            self.proj_half(oW, oXN3, oV, h, 2, ev_w, ev_xn, to_token_major=True)
            ev_w = self.load_wslot(oW, "wo")
            ev_qv = P.now("S", "V")
            for nl in range(4):
                ng = h * 4 + nl
                q0 = nl * 256
                for mg in range(2):
                    for qt in range(2):
                        for mi in range(4):
                            m = mg * 4 + mi
                            bank = qt * 2 + mi // 2
                            boff = (mi % 2) * 256
                            w = self.use_bank(bank, ev_k_all + ev_qv) if mi % 2 == 0 else ()
                            ev_mm = None
                            for c in range(C):
                                ev_mm = P.emit("tensor", lambda e, c=c, qt=qt, m=m, k=bank, boff=boff, q0=q0: e.matmul(
                                    self.ps[k][:, boff:boff + 256],
                                    self.bf(oQ + c * 1024 + q0 + qt * 128, 128),
                                    self.bf(oK + c * 2048 + m * 256, 256),
                                    start=(c == 0), stop=(c == C - 1)),
                                    waits=(w if c == 0 else ()), inc="T" if c == C - 1 else None)
                    ev_sc = ("T", P.cnt["T"])
                    for qt in range(2):
                        pv0, pv1 = self.ps[qt * 2], self.ps[qt * 2 + 1]
                        P.emit("vector", lambda e, pv0=pv0: e.tensor_reduce(
                            RV1[:, 0:2], pv0[:].rearrange("p (g k) -> p g k", k=256),
                            axis=AX.X, op=ALU.max), waits=(ev_sc,))
                        P.emit("vector", lambda e, pv1=pv1: e.tensor_reduce(
                            RV1[:, 2:4], pv1[:].rearrange("p (g k) -> p g k", k=256),
                            axis=AX.X, op=ALU.max))
                        P.emit("vector", lambda e: e.drain())
                        def mk_sub(pv, rcols, dcol):
                            def f(e, pv=pv, rcols=rcols, dcol=dcol):
                                return e.tensor_tensor(
                                    S1[:, dcol:dcol + 512].rearrange("p (g k) -> p g k", k=256),
                                    pv[:].rearrange("p (g k) -> p g k", k=256),
                                    RV1[:, rcols:rcols + 2].rearrange("p (g o) -> p g o", o=1)
                                        .broadcast_to([128, 2, 256]), ALU.subtract)
                            return f
                        P.emit("vector", mk_sub(pv0, 0, 0))
                        ev_d = P.emit("vector", mk_sub(pv1, 2, 512), inc="V")
                        ev_e = P.emit("scalar", lambda e: e.activation(
                            S1[:, 0:1024], S1[:, 0:1024], AF.Exp, scale=SCALE),
                            waits=(ev_d,), inc="S")
                        P.emit("vector", lambda e: e.tensor_reduce(
                            RV2[:, 0:4], S1[:, 0:1024].rearrange("p (g k) -> p g k", k=256),
                            axis=AX.X, op=ALU.add), waits=(ev_e,))
                        P.emit("vector", lambda e: e.drain())
                        P.emit("vector", lambda e: e.reciprocal(RV3[:, 0:4], RV2[:, 0:4]))
                        P.emit("vector", lambda e: e.drain())
                        P.emit("vector", lambda e: e.tensor_tensor(
                            S1[:, 0:1024].rearrange("p (g k) -> p g k", k=256),
                            S1[:, 0:1024].rearrange("p (g k) -> p g k", k=256),
                            RV3[:, 0:4].rearrange("p (g o) -> p g o", o=1)
                                .broadcast_to([128, 4, 256]), ALU.mult))
                        P.emit("vector", lambda e: e.drain())
                        P.emit("vector", lambda e: e.tensor_tensor(
                            S1[:, 1024:1280], S1[:, 0:256], S1[:, 256:512], ALU.add))
                        P.emit("vector", lambda e: e.tensor_tensor(
                            S1[:, 1280:1536], S1[:, 512:768], S1[:, 768:1024], ALU.add))
                        P.emit("vector", lambda e: e.drain())
                        P.emit("vector", lambda e: e.tensor_tensor(
                            S1[:, 1024:1280], S1[:, 1024:1280], S1[:, 1280:1536], ALU.add))
                        P.emit("vector", lambda e: e.drain())
                        if mg * 4 <= ng < mg * 4 + 4:
                            mi = ng - mg * 4
                            P.emit("vector", lambda e, mi=mi: e.tensor_tensor(
                                S1[:, 1024:1280], S1[:, 1024:1280],
                                S1[:, mi * 256:(mi + 1) * 256], ALU.add))
                            P.emit("vector", lambda e: e.drain())
                        if mg == 0:
                            ev_g = P.emit("vector", lambda e, qt=qt: e.tensor_copy(
                                S2[:, qt * 256:(qt + 1) * 256], S1[:, 1024:1280]), inc="V")
                        else:
                            ev_g = P.emit("vector", lambda e, qt=qt: e.tensor_tensor(
                                S2[:, qt * 256:(qt + 1) * 256],
                                S2[:, qt * 256:(qt + 1) * 256], S1[:, 1024:1280], ALU.add),
                                inc="V")
                        self.free_bank(qt * 2, ev_d)
                        self.free_bank(qt * 2 + 1, ev_d)
                ev_g_all = ("V", P.cnt["V"])
                P.emit("vector", lambda e: e.tensor_copy(S1b[:, 0:512], S2[:, 0:512]),
                       waits=(ev_g_all,))
                ev_gb = P.emit("vector", lambda e: e.drain(), inc="V")
                for qt in range(2):
                    for kb in range(2):
                        tb = 4 + (qt * 2 + kb) % 2
                        tw = self.use_bank(tb, (ev_gb,))
                        ev_tp = P.emit("tensor", lambda e, qt=qt, kb=kb, k=tb: e.transpose(
                            self.ps[k][:, 0:64].bitcast(BF16),
                            S1b[:, qt * 256 + kb * 128: qt * 256 + (kb + 1) * 128], IDB[:]),
                            waits=tw, inc="T")
                        ev_gt = P.emit("vector", lambda e, qt=qt, kb=kb, k=tb: e.tensor_copy(
                            S1b[:, 512 + kb * 256 + qt * 128: 512 + kb * 256 + (qt + 1) * 128],
                            self.ps[k][:, 0:64].bitcast(BF16)), waits=(ev_tp,), inc="V")
                        self.free_bank(tb, ev_gt)
                ev_gt_all = ("V", P.cnt["V"])
                for ec in range(C):
                    bank = 4 + ec // 2
                    boff = (ec % 2) * 256
                    w = self.use_bank(bank, (ev_gt_all,)) if ec % 2 == 0 else ()
                    ev_mm = None
                    for kt in range(2):
                        lt = (ng - h * 4) * 2 + kt
                        ev_mm = P.emit("tensor", lambda e, lt=lt, ec=ec, kt=kt, k=bank, boff=boff: e.matmul(
                            self.ps[k][:, boff:boff + 256],
                            self.bf(oV + lt * 1024 + ec * 128, 128),
                            S1b[:, 512 + kt * 256: 512 + (kt + 1) * 256],
                            start=(kt == 0), stop=(kt == 1)),
                            waits=(w if kt == 0 else ()), inc="T" if kt == 1 else None)
                    if ec % 2 == 1:
                        ev_av = P.emit("vector", lambda e, ec=ec, k=bank: e.tensor_copy(
                            S1b[:, 1536 + (ec - 1) * 256: 1536 + (ec + 1) * 256],
                            self.ps[k][:]), waits=(ev_mm,), inc="V")
                        self.free_bank(bank, ev_av)
                ev_av_all = ("V", P.cnt["V"])
                gq = h * 1024 + nl * 256
                for eo in range(C):
                    bank = eo // 2
                    boff = (eo % 2) * 256
                    w = self.use_bank(bank, (ev_av_all, ("X6", P.cnt["X6"]))) if eo % 2 == 0 else ()
                    ev_mm = None
                    for ec in range(C):
                        ev_mm = P.emit("tensor", lambda e, ec=ec, eo=eo, k=bank, boff=boff: e.matmul(
                            self.ps[k][:, boff:boff + 256],
                            self.bf(oW + ec * 1024 + eo * 128, 128),
                            S1b[:, 1536 + ec * 256: 1536 + (ec + 1) * 256],
                            start=(ec == 0), stop=(ec == C - 1)),
                            waits=(w if ec == 0 else ()), inc="T" if ec == C - 1 else None)
                    if eo % 2 == 1:
                        for eo2 in (eo - 1, eo):
                            bo2 = (eo2 % 2) * 256
                            ev_o = P.emit("vector", lambda e, eo2=eo2, k=bank, bo2=bo2, gq=gq: e.scalar_tensor_tensor(
                                XT[:, eo2 * N + gq: eo2 * N + gq + 256],
                                self.ps[k][:, bo2:bo2 + 256],
                                BQKV[:, 3 * C + eo2: 3 * C + eo2 + 1],
                                XT[:, eo2 * N + gq: eo2 * N + gq + 256],
                                ALU.add, ALU.add), waits=(ev_mm,), inc="V")
                        self.free_bank(bank, ev_o)

    def block_l1_ffn(self, seq):
        P = self.P
        XT, M, R, S1, S2 = (self.sb[k] for k in ("XT", "M", "R", "S1", "S2"))
        GROUT, G0B, GTR, SELS = self.sb["GROUT"], self.sb["G0B"], self.sb["GTR"], self.sb["SELS"]
        RA, RB, RC = self.sb["RA"], self.sb["RB"], self.sb["RC"]
        RV1, RV2, RV3 = self.sb["RV1"], self.sb["RV2"], self.sb["RV3"]
        WGT, WET, GBB, EBB = self.sb["WGT"], self.sb["WET"], self.sb["GBB"], self.sb["EBB"]
        S1b = S1[:].bitcast(BF16)
        S2b = S2[:].bitcast(BF16)
        oHR, oXN4 = 0, 32768
        NEG = -1e30
        w1t, w2t, w_int, w_outt = (self.dram[k] for k in ("w1_t", "w2_t", "w_in_t", "w_out_t"))

        ev_xn, ev_rt = self.xn_snapshot(oXN4, routers=(
            (WGT[:], self.sb["WGLO"][:], self.sb["WGTH"][:], 2, (0, 1, 2, 3), RB[:, 0:32]),
            (WET[:], self.sb["WELO"][:], self.sb["WETH"][:], 8, (4, 5, 6, 7), GROUT[:])))
        ev_gl = ev_el = ev_rt
        def g2(ap):
            return ap.rearrange("p (t k) -> p t k", k=2)
        def g8(ap):
            return ap.rearrange("p (t k) -> p t k", k=8)
        def b16(ap, k):
            return ap.rearrange("p (t o) -> p t o", o=1).broadcast_to([128, 16, k])
        ev1 = P.emit("vector", lambda e: e.tensor_tensor(
            g2(RA[:, 0:32]), g2(RB[:, 0:32]),
            GBB[:].rearrange("p (o k) -> p o k", o=1).broadcast_to([128, 16, 2]), ALU.add),
            waits=(ev_gl,), inc="V")
        ev2 = P.emit("vector", lambda e: e.tensor_tensor(
            g8(GROUT[:]), g8(GROUT[:]),
            EBB[:].rearrange("p (o k) -> p o k", o=1).broadcast_to([128, 16, 8]), ALU.add),
            waits=(ev_el,), inc="V")
        P.emit("vector", lambda e: e.drain())
        GL = RA[:, 0:32]
        GEQ = RB[:, 0:32]
        P.emit("vector", lambda e: e.drain())
        P.emit("vector", lambda e: e.tensor_reduce(RV1[:], g2(GL), axis=AX.X, op=ALU.max))
        P.emit("vector", lambda e: e.drain())
        P.emit("vector", lambda e: e.tensor_tensor(g2(GEQ), g2(GL), b16(RV1[:], 2), ALU.is_ge))
        P.emit("vector", lambda e: e.tensor_reduce(RV2[:], g2(GL), axis=AX.X, op=ALU.add))
        P.emit("vector", lambda e: e.drain())
        P.emit("vector", lambda e: e.tensor_tensor(RV3[:], RV1[:], RV2[:], ALU.subtract))
        P.emit("vector", lambda e: e.drain())
        ev_sm = P.emit("vector", lambda e: e.tensor_tensor(RV3[:], RV3[:], RV1[:], ALU.add),
                       inc="V")
        ev_gp = P.emit("scalar", lambda e: e.activation(RV3[:], RV3[:], AF.Sigmoid),
                       waits=(ev_sm,), inc="S")
        geq_b = GEQ.rearrange("p (t g o) -> p t g o", g=2, o=1).broadcast_to([128, 16, 2, 4])
        P.emit("vector", lambda e: e.tensor_scalar(
            RC[:].rearrange("p (t g k) -> p t g k", g=2, k=4), geq_b, 1e30, -1e30,
            ALU.mult, ALU.add))
        P.emit("vector", lambda e: e.drain())
        P.emit("vector", lambda e: e.tensor_tensor(RC[:], GROUT[:], RC[:], ALU.add))
        P.emit("vector", lambda e: e.drain())
        P.emit("vector", lambda e: e.tensor_reduce(RV1[:], g8(RC[:]), axis=AX.X, op=ALU.max))
        P.emit("vector", lambda e: e.drain())
        ev_d = P.emit("vector", lambda e: e.tensor_tensor(
            g8(G0B[:]), g8(RC[:]), b16(RV1[:], 8), ALU.subtract), inc="V")
        ev_e = P.emit("scalar", lambda e: e.activation(G0B[:], G0B[:], AF.Exp),
                      waits=(ev_d,), inc="S")
        P.emit("vector", lambda e: e.tensor_reduce(RV1[:], g8(G0B[:]), axis=AX.X, op=ALU.add),
               waits=(ev_e,))
        P.emit("vector", lambda e: e.drain())
        P.emit("vector", lambda e: e.reciprocal(RV2[:], RV1[:]))
        P.emit("vector", lambda e: e.drain())
        P.emit("vector", lambda e: e.tensor_tensor(g8(G0B[:]), g8(G0B[:]), b16(RV2[:], 8), ALU.mult),
               waits=(ev_gp,))
        P.emit("vector", lambda e: e.drain())
        P.emit("vector", lambda e: e.tensor_tensor(g8(G0B[:]), g8(G0B[:]), b16(RV3[:], 8), ALU.mult))
        P.emit("vector", lambda e: e.drain())
        P.emit("vector", lambda e: e.tensor_reduce(RV1[:], g8(G0B[:]), axis=AX.X, op=ALU.max))
        P.emit("vector", lambda e: e.drain())
        P.emit("vector", lambda e: e.tensor_tensor(g8(RB[:]), g8(G0B[:]), b16(RV1[:], 8), ALU.is_ge))
        P.emit("vector", lambda e: e.drain())
        P.emit("vector", lambda e: e.scalar_tensor_tensor(g8(RC[:]), g8(RB[:]), NEG, g8(G0B[:]),
                                                          ALU.mult, ALU.add))
        P.emit("vector", lambda e: e.drain())
        P.emit("vector", lambda e: e.tensor_reduce(RV2[:], g8(RC[:]), axis=AX.X, op=ALU.max))
        P.emit("vector", lambda e: e.drain())
        P.emit("vector", lambda e: e.tensor_tensor(g8(RA[:]), g8(RC[:]), b16(RV2[:], 8), ALU.is_ge))
        P.emit("vector", lambda e: e.drain())
        P.emit("vector", lambda e: e.tensor_tensor(g8(RB[:]), g8(RB[:]), b16(RV1[:], 8), ALU.mult))
        P.emit("vector", lambda e: e.drain())
        P.emit("vector", lambda e: e.tensor_tensor(g8(RA[:]), g8(RA[:]), b16(RV2[:], 8), ALU.mult))
        P.emit("vector", lambda e: e.drain())
        ev_gates = P.emit("vector", lambda e: e.tensor_tensor(G0B[:], RB[:], RA[:], ALU.add),
                          inc="V")
        ev_gtr = self.gates_transpose(ev_gates)

        ev_stripe = [None, None]
        ev_hr = None
        for hc in range(16):
            buf = hc % 2
            wl = (ev_stripe[buf],) if ev_stripe[buf] is not None else (ev_xn, ("V", P.cnt["V"]))
            bs = "X7" if buf == 0 else "X8"
            ev_la = P.emit("gpsimd", lambda e, hc=hc, buf=buf: e.dma_start(
                out=S2b[:, buf * 2048: buf * 2048 + 1024].rearrange("p (c k) -> p c k", k=128),
                in_=w1t[:, hc * 128:(hc + 1) * 128].rearrange("(c p) k -> p c k", p=128)),
                waits=wl, inc=bs)
            ev_lb = P.emit("gpsimd", lambda e, hc=hc, buf=buf: e.dma_start(
                out=S2b[:, buf * 2048 + 1024: buf * 2048 + 2048].rearrange("p (c k) -> p c k", k=128),
                in_=w1t[:, 2048 + hc * 128: 2048 + (hc + 1) * 128].rearrange("(c p) k -> p c k", p=128)),
                inc=bs)
            for nb in range(4):
                n0 = nb * 512
                bka = 4 + (nb % 2) * 2
                bkb = bka + 1
                wa = self.use_bank(bka, (ev_lb,) if nb == 0 else ())
                wb = self.use_bank(bkb, ())
                ev_ma = ev_mb = None
                for c in range(C):
                    ev_ma = P.emit("tensor", lambda e, c=c, buf=buf, n0=n0, bka=bka: e.matmul(
                        self.ps[bka][:], S2b[:, buf * 2048 + c * 128: buf * 2048 + (c + 1) * 128],
                        self.bf(oXN4 + c * N + n0, 512),
                        start=(c == 0), stop=(c == C - 1)),
                        waits=(wa if c == 0 else ()), inc="T" if c == C - 1 else None)
                for c in range(C):
                    ev_mb = P.emit("tensor", lambda e, c=c, buf=buf, n0=n0, bkb=bkb: e.matmul(
                        self.ps[bkb][:], S2b[:, buf * 2048 + 1024 + c * 128: buf * 2048 + 1024 + (c + 1) * 128],
                        self.bf(oXN4 + c * N + n0, 512),
                        start=(c == 0), stop=(c == C - 1)),
                        waits=(wb if c == 0 else ()), inc="T" if c == C - 1 else None)
                sl = nb % 2
                ev_sa = P.emit("scalar", lambda e, sl=sl, bka=bka: e.activation(
                    S1b[:, sl * 512:(sl + 1) * 512], self.ps[bka][:], AF.Silu),
                    waits=(ev_ma,), inc="S")
                self.free_bank(bka, ev_sa)
                ev_hr = P.emit("vector", lambda e, hc=hc, n0=n0, sl=sl, bkb=bkb: e.tensor_tensor(
                    self.bf(oHR + hc * N + n0, 512),
                    S1b[:, sl * 512:(sl + 1) * 512], self.ps[bkb][:], ALU.mult),
                    waits=(ev_sa, ev_mb), inc="V")
                self.free_bank(bkb, ev_hr)
            ev_stripe[buf] = ev_hr and ("T", P.cnt["T"])
        ev_hr_all = ("V", P.cnt["V"])
        ev_gb_free = None
        ev_stripe = [None, None]
        ev_hr_mm_done = P.now("T")
        for ex in range(8):
            wg = (ev_gtr,) + ev_hr_mm_done if ex == 0 else ()
            if ev_gb_free is not None:
                wg = wg + (ev_gb_free,)
            ev_gb = None
            for g in range(4):
                bank = 6 + g % 2
                bwg = self.use_bank(bank, wg if g < 2 else ())
                ev_bc = P.emit("tensor", lambda e, ex=ex, g=g, k=bank: e.matmul(
                    self.ps[k][:], SELS[:, ex * 128:(ex + 1) * 128],
                    GTR[:, g * 512:(g + 1) * 512], start=True, stop=True),
                    waits=bwg, inc="T")
                ev_gb = P.emit("vector", lambda e, g=g, k=bank: e.tensor_copy(
                    S2b[:, 2048 + g * 512: 2048 + (g + 1) * 512], self.ps[k][:]),
                    waits=(ev_bc,), inc="V")
                self.free_bank(bank, ev_gb)
            for nb in range(4):
                n0 = nb * 512
                ev_y = None
                for hc in range(16):
                    buf = hc % 2
                    wl = (ev_stripe[buf],) if ev_stripe[buf] is not None else ev_hr_mm_done
                    ev_l = P.emit("gpsimd", lambda e, ex=ex, hc=hc, buf=buf: e.dma_start(
                        out=S2b[:, buf * 1024:(buf + 1) * 1024],
                        in_=w2t[ex, hc * 128:(hc + 1) * 128, :]), waits=wl,
                        inc="X7" if buf == 0 else "X8")
                    sl = hc % 2
                    whg = (ev_gb, ("V", P.cnt["V"]))
                    if ev_y is not None:
                        whg = whg + (ev_y,)
                    ev_hg = P.emit("gpsimd", lambda e, hc=hc, n0=n0, sl=sl: e.tensor_tensor(
                        S1b[:, sl * 512:(sl + 1) * 512],
                        self.bf(oHR + hc * N + n0, 512),
                        S2b[:, 2048 + n0: 2048 + n0 + 512], ALU.mult),
                        waits=whg, inc="G")
                    for dc in range(C):
                        wy = ()
                        if dc == 0:
                            wy = (ev_l, ev_hg)
                            if hc == 0:
                                wy = self.use_bank(dc, wy) + self.use_bank((dc + 1) % 8, ())
                                for dcb in range(2, 8):
                                    wy = wy + self.use_bank(dcb, ())
                                self.bank_free = [None] * 8
                        ev_y = P.emit("tensor", lambda e, buf=buf, dc=dc, hc=hc: e.matmul(
                            self.ps[dc][:],
                            S2b[:, buf * 1024 + dc * 128: buf * 1024 + (dc + 1) * 128],
                            S1b[:, (hc % 2) * 512: (hc % 2 + 1) * 512],
                            start=(hc == 0), stop=(hc == 15)),
                            waits=wy, inc="T" if (dc == C - 1) else None)
                    ev_stripe[buf] = ev_y
                for dc in range(C):
                    ev_add = P.emit("vector", lambda e, dc=dc, n0=n0: e.tensor_tensor(
                        XT[:, dc * N + n0: dc * N + n0 + 512],
                        XT[:, dc * N + n0: dc * N + n0 + 512], self.ps[dc][:], ALU.add),
                        waits=(ev_y,), inc="V")
                    self.free_bank(dc, ev_add)
            ev_gb_free = ("G", P.cnt["G"])

        oHST, oWAB, oWO = 0, 8192, 12288
        ev_hr_free = ("T", P.cnt["T"])
        ev_ab = [None, None]
        ev_wo = [None, None]
        for nb in range(4):
            n0 = nb * 512
            ev_hst = None
            for hc in range(16):
                buf = hc % 2
                wl = (ev_ab[buf],) if ev_ab[buf] is not None else ()
                if nb == 0 and hc < 2:
                    wl = wl + (ev_hr_free,)
                bs = "X7" if buf == 0 else "X8"
                ev_la = P.emit("gpsimd", lambda e, hc=hc, buf=buf: e.dma_start(
                    out=self.bf(oWAB + buf * 2048, 1024).rearrange("p (c k) -> p c k", k=128),
                    in_=w_int[:, hc * 128:(hc + 1) * 128].rearrange("(c p) k -> p c k", p=128)),
                    waits=wl, inc=bs)
                ev_lb = P.emit("gpsimd", lambda e, hc=hc, buf=buf: e.dma_start(
                    out=self.bf(oWAB + buf * 2048 + 1024, 1024).rearrange("p (c k) -> p c k", k=128),
                    in_=w_int[:, 2048 + hc * 128: 2048 + (hc + 1) * 128].rearrange("(c p) k -> p c k", p=128)),
                    inc=bs)
                wa = self.use_bank(6, (ev_lb,))
                wb = self.use_bank(7, ())
                ev_ma = ev_mb = None
                for c in range(C):
                    ev_ma = P.emit("tensor", lambda e, c=c, buf=buf, n0=n0: e.matmul(
                        self.ps[6][:], self.bf(oWAB + buf * 2048 + c * 128, 128),
                        self.bf(oXN4 + c * N + n0, 512),
                        start=(c == 0), stop=(c == C - 1)),
                        waits=(wa if c == 0 else ()), inc="T" if c == C - 1 else None)
                for c in range(C):
                    ev_mb = P.emit("tensor", lambda e, c=c, buf=buf, n0=n0: e.matmul(
                        self.ps[7][:], self.bf(oWAB + buf * 2048 + 1024 + c * 128, 128),
                        self.bf(oXN4 + c * N + n0, 512),
                        start=(c == 0), stop=(c == C - 1)),
                        waits=(wb if c == 0 else ()), inc="T" if c == C - 1 else None)
                ev_ab[buf] = ev_mb
                sl = hc % 2
                ev_sa = P.emit("scalar", lambda e, sl=sl: e.activation(
                    S1b[:, sl * 512:(sl + 1) * 512], self.ps[6][:], AF.Silu),
                    waits=(ev_ma,), inc="S")
                self.free_bank(6, ev_sa)
                ev_hst = P.emit("vector", lambda e, hc=hc, sl=sl: e.tensor_tensor(
                    self.bf(oHST + hc * 512, 512),
                    S1b[:, sl * 512:(sl + 1) * 512], self.ps[7][:], ALU.mult),
                    waits=(ev_sa, ev_mb), inc="V")
                self.free_bank(7, ev_hst)
            ev_y = None
            for hc in range(16):
                buf = hc % 2
                wl = (ev_wo[buf],) if ev_wo[buf] is not None else ()
                ev_l = P.emit("gpsimd", lambda e, hc=hc, buf=buf: e.dma_start(
                    out=self.bf(oWO + buf * 1024, 1024),
                    in_=w_outt[hc * 128:(hc + 1) * 128, :]), waits=wl,
                    inc="X9" if buf == 0 else "X5")
                for dc in range(C):
                    wy = ()
                    if dc == 0:
                        wy = (ev_l, ev_hst)
                        if hc == 0:
                            for dcb in range(8):
                                wy = wy + self.use_bank(dcb, ())
                            self.bank_free = [None] * 8
                    ev_y = P.emit("tensor", lambda e, buf=buf, dc=dc, hc=hc: e.matmul(
                        self.ps[dc][:],
                        self.bf(oWO + buf * 1024 + dc * 128, 128),
                        self.bf(oHST + hc * 512, 512),
                        start=(hc == 0), stop=(hc == 15)),
                        waits=wy, inc="T" if dc == C - 1 else None)
                ev_wo[buf] = ev_y
            for dc in range(C):
                ev_add = P.emit("vector", lambda e, dc=dc, n0=n0: e.tensor_tensor(
                    XT[:, dc * N + n0: dc * N + n0 + 512],
                    XT[:, dc * N + n0: dc * N + n0 + 512], self.ps[dc][:], ALU.add),
                    waits=(ev_y,), inc="V")
                self.free_bank(dc, ev_add)


def make_core_inputs(x_core, W, cst):
    d = dict(cst)
    d["xin"] = np.ascontiguousarray(np.asarray(x_core, np.float32).reshape(NSEQ * N, D))
    d["bvq"] = feat_major_vec(W["L0_attn_bv_q"])
    d["bvk"] = feat_major_vec(W["L0_attn_bv_k"])
    d["bvv"] = feat_major_vec(W["L0_attn_bv_v"])
    d["rw_t"] = np.ascontiguousarray(np.asarray(W["L0_ffn_router_w"], np.float32).T)
    d["l0w1"] = np.ascontiguousarray(np.asarray(W["L0_ffn_w1"], np.float32))
    d["l0w2"] = np.ascontiguousarray(np.asarray(W["L0_ffn_w2"], np.float32))
    d["wq"] = np.ascontiguousarray(np.asarray(W["L1_attn_Wq"], np.float32))
    d["wk"] = np.ascontiguousarray(np.asarray(W["L1_attn_Wk"], np.float32))
    d["wv"] = np.ascontiguousarray(np.asarray(W["L1_attn_Wv"], np.float32))
    d["wo"] = np.ascontiguousarray(np.asarray(W["L1_attn_Wo"], np.float32))
    d["bq"] = feat_major_vec(W["L1_attn_bq"])
    d["bk"] = feat_major_vec(W["L1_attn_bk"])
    d["bv"] = feat_major_vec(W["L1_attn_bv"])
    d["bo"] = feat_major_vec(W["L1_attn_bo"])
    d["w_in_t"] = np.ascontiguousarray(np.asarray(W["L1_ffn_w_in"], np.float32).T)
    d["w_out_t"] = np.ascontiguousarray(np.asarray(W["L1_ffn_w_out"], np.float32).T)
    d["w1_t"] = np.ascontiguousarray(np.asarray(W["L1_ffn_w1"], np.float32).T)
    d["w2_t"] = np.ascontiguousarray(np.asarray(W["L1_ffn_w2"], np.float32).transpose(0, 2, 1))
    d["wg_t"] = np.ascontiguousarray(np.asarray(W["L1_ffn_wg"], np.float32).T)
    d["we_t"] = np.ascontiguousarray(np.asarray(W["L1_ffn_we"], np.float32).T)
    d["gb_b"] = np.tile(np.asarray(W["L1_ffn_gb"], np.float32)[None, :], (128, 1))
    d["eb_b"] = np.tile(np.asarray(W["L1_ffn_eb"], np.float32)[None, :], (128, 1))
    d["ones_f32"] = cst["ones_f32"]
    return d



def kernel(**inputs):
    x = np.asarray(inputs["L0_attn_x"], np.float32)
    cst = host_constants()
    b = Builder()
    in_maps = [make_core_inputs(x[2 * i:2 * i + 2], inputs, cst) for i in range(8)]
    res = run_bass_kernel_spmd(b.nc, in_maps, core_ids=list(range(8)))
    out = np.concatenate([res.results[i]["yout"].reshape(NSEQ, N, D) for i in range(8)], axis=0)
    return out.astype(np.float32)
